# version 02
# baseline (speedup 1.0000x reference)
import contextlib
import numpy as np

import concourse.bass as bass
import concourse.mybir as mybir
from concourse.bass_utils import run_bass_kernel_spmd

F32 = mybir.dt.float32
BF16 = mybir.dt.bfloat16
AF = mybir.ActivationFunctionType
ALU = mybir.AluOpType
AXX = mybir.AxisListType.X

NCORES = 8
D = 1024
DI = 2048
NST = 16
DT_RANK = 64
HALO = 3
NEXP = 8
FF = 4096
EPS = 1e-5


def fsplit(total, chunk=512):
    n = (total + chunk - 1) // chunk
    base, rem = divmod(total, n)
    out = []
    off = 0
    for i in range(n):
        sz = base + (1 if i < rem else 0)
        out.append((off, sz))
        off += sz
    return out


class Ctr:
    def __init__(self, sem):
        self.sem = sem
        self.n = 0


class B:

    def __init__(self):
        self.nc = bass.Bass("TRN2", target_bir_lowering=False, debug=False)
        self.es = contextlib.ExitStack()
        self._sems = {}

    def sem(self, name) -> Ctr:
        if name not in self._sems:
            self._sems[name] = Ctr(self.es.enter_context(self.nc.semaphore(name)))
        return self._sems[name]

    def sb(self, name, shape, dtype=F32, scope=None):
        return (scope or self.es).enter_context(
            self.nc.sbuf_tensor(name, shape, dtype))

    def ps(self, name, shape, dtype=F32, scope=None):
        return (scope or self.es).enter_context(
            self.nc.psum_tensor(name, shape, dtype))

    def din(self, name, shape, dtype=F32):
        return self.nc.dram_tensor(name, shape, dtype, kind="ExternalInput").ap()

    def dout(self, name, shape, dtype=F32):
        return self.nc.dram_tensor(name, shape, dtype, kind="ExternalOutput").ap()

    def close(self):
        self.es.close()

    def init_consts(self, values):
        self.const_values = [float(v) for v in values]
        n = len(self.const_values)
        ap = self.din("cconsts", [1, n])
        tile = self.sb("cconsts_sb", [1, n])
        for i, v in enumerate(self.const_values):
            self.nc.const_aps.register(F32, v, tile[:, i:i + 1])
        sd = self.sem("d")
        with self.nc.Block() as blk:
            @blk.sync
            def _(s):
                s.dma_start(out=tile[:], in_=ap).then_inc(sd.sem, 16)
                s.wait_ge(sd.sem, sd.n + 16)
        sd.n += 16

    def host_consts(self):
        return np.asarray([self.const_values], dtype=np.float32)



def phase_layernorm(b, x_tiles, T, out_bf=None, out_f32=None, tag="ln",
                    scope=True, dbg=False):
    nc = b.nc
    sc = contextlib.ExitStack() if scope else None
    _sb = lambda n, sh, dt=F32: b.sb(n, sh, dt, scope=sc)
    _ps = lambda n, sh: b.ps(n, sh, scope=sc)
    ones = _sb(f"{tag}_ones", [128, 1])
    onesr = _sb(f"{tag}_onesr", [1, 128])
    sq = [_sb(f"{tag}_sq{g}", [128, 512]) for g in range(8)]
    xf = _sb(f"{tag}_xf", [128, 512])
    mrow = _sb(f"{tag}_mrow", [1, 512])
    msq = _sb(f"{tag}_msq", [1, 512])
    m2 = _sb(f"{tag}_m2", [1, 512])
    vrow = _sb(f"{tag}_vrow", [1, 512])
    rrow = _sb(f"{tag}_rrow", [1, 512])
    mb = _sb(f"{tag}_mb", [128, 512])
    rb = _sb(f"{tag}_rb", [128, 512])
    p1 = _ps(f"{tag}_p1", [1, 512])
    p2 = _ps(f"{tag}_p2", [1, 512])
    pb = _ps(f"{tag}_pb", [128, 512])
    pb2 = _ps(f"{tag}_pb2", [128, 512])

    ss, stt, sv = b.sem("s"), b.sem("t"), b.sem("v")
    chunks = fsplit(T, 512)
    nch = len(chunks)
    s0, t0, v0 = ss.n, stt.n, sv.n

    with nc.Block() as blk:
        @blk.vector
        def _(v):
            v.memset(ones[:], 1.0)
            v.memset(onesr[:], 1.0).then_inc(sv.sem, 1)
            for i, (off, sz) in enumerate(chunks):
                v.wait_ge(ss.sem, s0 + 4 * i + 2)
                v.tensor_sub(vrow[:, 0:sz], msq[:, 0:sz], m2[:, 0:sz]) \
                    .then_inc(sv.sem, 1)
                v.wait_ge(ss.sem, s0 + 4 * i + 4)
                last = None
                for g in range(8):
                    xin = x_tiles[g][:, off:off + sz]
                    v.tensor_sub(xf[:, 0:sz], xin, mb[:, 0:sz])
                    if out_f32 is not None:
                        xo = out_f32[g][:, off:off + sz]
                        last = v.tensor_mul(xo, xf[:, 0:sz], rb[:, 0:sz])
                        if out_bf is not None:
                            last = v.tensor_copy(out_bf[g][:, off:off + sz], xo)
                    else:
                        last = v.tensor_mul(out_bf[g][:, off:off + sz],
                                            xf[:, 0:sz], rb[:, 0:sz])
                last.then_inc(sv.sem, 1)
        sv.n = v0 + 1 + 2 * nch

        @blk.scalar
        def _(s):
            for i, (off, sz) in enumerate(chunks):
                if i > 0:
                    s.wait_ge(stt.sem, t0 + 2 * (i - 1) + 1)
                for g in range(8):
                    inst = s.activation(out=sq[g][:, 0:sz],
                                        in_=x_tiles[g][:, off:off + sz],
                                        func=AF.Square)
                inst.then_inc(ss.sem, 1)
                s.wait_ge(stt.sem, t0 + 2 * i + 1)
                s.activation(out=mrow[:, 0:sz], in_=p1[:, 0:sz], func=AF.Copy,
                             scale=1.0 / D)
                s.activation(out=msq[:, 0:sz], in_=p2[:, 0:sz], func=AF.Copy,
                             scale=1.0 / D)
                s.activation(out=m2[:, 0:sz], in_=mrow[:, 0:sz],
                             func=AF.Square).then_inc(ss.sem, 1)
                s.wait_ge(sv.sem, v0 + 1 + 2 * i + 1)
                s.activation(out=rrow[:, 0:sz], in_=vrow[:, 0:sz], func=AF.Ln,
                             bias=EPS)
                s.activation(out=rrow[:, 0:sz], in_=rrow[:, 0:sz], func=AF.Exp,
                             scale=-0.5).then_inc(ss.sem, 1)
                s.wait_ge(stt.sem, t0 + 2 * i + 2)
                s.activation(out=mb[:, 0:sz], in_=pb[:, 0:sz], func=AF.Copy)
                s.activation(out=rb[:, 0:sz], in_=pb2[:, 0:sz], func=AF.Copy) \
                    .then_inc(ss.sem, 1)
        ss.n = s0 + 4 * nch

        @blk.tensor
        def _(t):
            t.wait_ge(sv.sem, v0 + 1)
            for i, (off, sz) in enumerate(chunks):
                t.wait_ge(ss.sem, s0 + 4 * i + 1)
                if i > 0:
                    t.wait_ge(sv.sem, v0 + 1 + 2 * (i - 1) + 2)
                for g in range(8):
                    t.matmul(p1[:, 0:sz], ones[:], x_tiles[g][:, off:off + sz],
                             start=(g == 0), stop=(g == 7))
                for g in range(8):
                    inst = t.matmul(p2[:, 0:sz], ones[:], sq[g][:, 0:sz],
                                    start=(g == 0), stop=(g == 7))
                inst.then_inc(stt.sem, 1)
                t.wait_ge(ss.sem, s0 + 4 * i + 3)
                t.matmul(pb[:, 0:sz], onesr[:], mrow[:, 0:sz], start=True,
                         stop=True)
                t.matmul(pb2[:, 0:sz], onesr[:], rrow[:, 0:sz], start=True,
                         stop=True).then_inc(stt.sem, 1)
        stt.n = t0 + 2 * nch
    if dbg:
        return dict(mrow=mrow, msq=msq, m2=m2, vrow=vrow, rrow=rrow, mb=mb, rb=rb, pb=pb, pb2=pb2)
    if sc is not None:
        sc.close()



def phase_load_dmaj(b, hbm_ap, tiles, W, tag="ld", cast=False):
    nc = b.nc
    sd = b.sem("d")
    with nc.Block() as blk:
        if cast:
            @blk.gpsimd
            def _(g):
                for gi, tl in enumerate(tiles):
                    g.dma_start(out=tl[:, 0:W],
                                in_=hbm_ap[gi * 128:(gi + 1) * 128, 0:W]) \
                        .then_inc(sd.sem, 16)
                g.wait_ge(sd.sem, sd.n + 16 * len(tiles))
        else:
            @blk.sync
            def _(s):
                for gi, tl in enumerate(tiles):
                    s.dma_start(out=tl[:, 0:W],
                                in_=hbm_ap[gi * 128:(gi + 1) * 128, 0:W]) \
                        .then_inc(sd.sem, 16)
                s.wait_ge(sd.sem, sd.n + 16 * len(tiles))
    sd.n += 16 * len(tiles)


def phase_store_dmaj(b, tiles, hbm_ap, W, tag="st", col0=0):
    nc = b.nc
    sd = b.sem("d")
    with nc.Block() as blk:
        @blk.sync
        def _(s):
            for gi, tl in enumerate(tiles):
                s.dma_start(out=hbm_ap[gi * 128:(gi + 1) * 128, :],
                            in_=tl[:, col0:col0 + hbm_ap.shape[1]]) \
                    .then_inc(sd.sem, 16)
            s.wait_ge(sd.sem, sd.n + 16 * len(tiles))
    sd.n += 16 * len(tiles)



def phase_gating(b, xln_f32, T, gate_w_ap, ident, cwT_hbm, tag="gt",
                 scope=True):
    nc = b.nc
    sc = contextlib.ExitStack() if scope else None
    _sb = lambda n, sh, dt=F32: b.sb(n, sh, dt, scope=sc)
    _ps = lambda n, sh: b.ps(n, sh, scope=sc)
    gw = _sb(f"{tag}_gw", [128, 64])
    cwf = _sb(f"{tag}_cwf", [128, 8])
    cw2 = _sb(f"{tag}_cw2", [128, 8])
    m1 = _sb(f"{tag}_m1", [128, 1])
    m2 = _sb(f"{tag}_m2", [128, 1])
    negm1 = _sb(f"{tag}_negm1", [128, 1])
    dm = _sb(f"{tag}_dm", [128, 1])
    sig = _sb(f"{tag}_sig", [128, 1])
    mv = _sb(f"{tag}_mv", [128, 8])
    mk1 = _sb(f"{tag}_mk1", [128, 8])
    mk2 = _sb(f"{tag}_mk2", [128, 8])
    eall = _sb(f"{tag}_eall", [128, 8])
    emsk = _sb(f"{tag}_emsk", [128, 8])
    cwt_s = _sb(f"{tag}_cwt", [8, 128])
    pcw = _ps(f"{tag}_pcw", [128, 8])
    ptr = _ps(f"{tag}_ptr", [8, 128])

    sd, sv, ss, stt = b.sem("d"), b.sem("v"), b.sem("s"), b.sem("t")
    d0, v0, s0, t0 = sd.n, sv.n, ss.n, stt.n
    nch = (T + 127) // 128

    with nc.Block() as blk:
        @blk.sync
        def _(s):
            for g in range(8):
                s.dma_start(out=gw[:, 8 * g:8 * g + 8],
                            in_=gate_w_ap[128 * g:128 * (g + 1), :]) \
                    .then_inc(sd.sem, 16)
            for i in range(nch):
                tt = min(128, T - i * 128)
                s.wait_ge(sv.sem, v0 + 11 * i + 11)
                s.dma_start(out=cwT_hbm[:, i * 128:i * 128 + tt],
                            in_=cwt_s[:, 0:tt]).then_inc(sd.sem, 16)
            s.wait_ge(sd.sem, d0 + 16 * (8 + nch))
        sd.n = d0 + 16 * (8 + nch)

        @blk.tensor
        def _(t):
            t.wait_ge(sd.sem, d0 + 16 * 8)
            for i in range(nch):
                tt = min(128, T - i * 128)
                if i > 0:
                    t.wait_ge(sv.sem, v0 + 11 * (i - 1) + 1)
                for g in range(8):
                    inst = t.matmul(pcw[0:tt, :],
                                    xln_f32[g][:, i * 128:i * 128 + tt],
                                    gw[:, 8 * g:8 * g + 8], start=(g == 0),
                                    stop=(g == 7))
                inst.then_inc(stt.sem, 1)
                t.wait_ge(sv.sem, v0 + 11 * i + 10)
                if i > 0:
                    t.wait_ge(sv.sem, v0 + 11 * (i - 1) + 11)
                t.matmul(ptr[:, 0:tt], cw2[0:tt, :], ident[0:tt, 0:tt],
                         start=True, stop=True).then_inc(stt.sem, 1)
        stt.n = t0 + 2 * nch

        NV = 11

        @blk.vector
        def _(v):
            k = [v0]

            def step(inst):
                inst.then_inc(sv.sem, 1)
                k[0] += 1
                v.wait_ge(sv.sem, k[0])

            for i in range(nch):
                tt = min(128, T - i * 128)
                v.wait_ge(stt.sem, t0 + 2 * i + 1)
                v.memset(mv[:], 3.0e38)
                step(v.tensor_copy(cwf[0:tt, :], pcw[0:tt, :]))
                step(v.reduce_max(m1[0:tt, :], cwf[0:tt, :], axis=AXX))
                step(v.tensor_copy(mv[0:tt, 0:1], m1[0:tt, :]))
                step(v.match_replace(mk1[0:tt, :], mv[0:tt, :], cwf[0:tt, :],
                                     -1e9))
                step(v.reduce_max(m2[0:tt, :], mk1[0:tt, :], axis=AXX))
                step(v.tensor_copy(mv[0:tt, 0:1], m2[0:tt, :]))
                v.tensor_scalar_mul(negm1[0:tt, :], m1[0:tt, :], -1.0)
                step(v.tensor_sub(dm[0:tt, :], m2[0:tt, :], m1[0:tt, :]))
                step(v.match_replace(mk2[0:tt, :], mv[0:tt, :], mk1[0:tt, :],
                                     -1e9))
                v.wait_ge(ss.sem, s0 + i + 1)
                step(v.tensor_sub(cw2[0:tt, :], eall[0:tt, :], emsk[0:tt, :]))
                step(v.tensor_scalar_mul(cw2[0:tt, :], cw2[0:tt, :],
                                         sig[0:tt, :]))
                v.wait_ge(stt.sem, t0 + 2 * i + 2)
                step(v.tensor_copy(cwt_s[:, 0:tt], ptr[:, 0:tt]))
        sv.n = v0 + NV * nch

        @blk.scalar
        def _(s):
            for i in range(nch):
                tt = min(128, T - i * 128)
                s.wait_ge(sv.sem, v0 + 11 * i + 8)
                s.activation(out=eall[0:tt, :], in_=cwf[0:tt, :], func=AF.Exp,
                             bias=negm1[0:tt, :])
                s.activation(out=emsk[0:tt, :], in_=mk2[0:tt, :], func=AF.Exp,
                             bias=negm1[0:tt, :])
                s.activation(out=sig[0:tt, :], in_=dm[0:tt, :], func=AF.Sigmoid,
                             scale=-1.0).then_inc(ss.sem, 1)
        ss.n = s0 + nch

    if sc is not None:
        sc.close()
        return None
    return dict(cwf=cwf, cw2=cw2, m1=m1, m2=m2, negm1=negm1, dm=dm, sig=sig,
                mv=mv, mk1=mk1, mk2=mk2, eall=eall, emsk=emsk)



def phase_moe(b, xln_bf, resid, T, w1_ap, w2_ap, cwT_hbm, scale_out=0.5,
              tag="moe", serial=False):
    nc = b.nc
    sc = contextlib.ExitStack()
    _sb = lambda n, sh, dt=F32: b.sb(tag + n, sh, dt, scope=sc)
    _ps = lambda n, sh: b.ps(tag + n, sh, scope=sc)

    w1s = [[_sb(f"w1_{bi}_{dc}", [128, 1024], BF16) for dc in range(8)]
           for bi in range(2)]
    w2s = [_sb(f"w2_{wc}", [128, 1024], BF16) for wc in range(8)]
    hq = [_sb(f"hq{wc}", [128, T], BF16) for wc in range(8)]
    htmp = [_sb(f"htmp{k}", [128, 512], BF16) for k in range(2)]
    ce = _sb("ce", [128, T], BF16)
    cwrow = _sb("cwrow", [1, T])
    onesr = _sb("onesr", [1, 128])
    junk = _sb("junk", [1, 128])
    ph = [_ps(f"ph{k}", [128, 512]) for k in range(2)]
    po = [_ps(f"po{k}", [128, 512]) for k in range(2)]
    pce = _ps("pce", [128, 512])

    sd, sv, ss, stt = b.sem("d"), b.sem("v"), b.sem("s"), b.sem("t")
    sgp = [b.sem("g0"), b.sem("g1")]
    sg2 = b.sem("g2")
    d0, v0, s0, t0 = sd.n, sv.n, ss.n, stt.n
    gp0 = [sgp[0].n, sgp[1].n]
    g20 = sg2.n
    TQ = fsplit(T, 512)
    NJ = len(TQ)
    NU = 8 * NJ
    EQ = [(e, q) for e in range(8) for q in range(4)]

    def t_base(idx):
        return t0 + 2 * NU * idx + NJ * (idx // 4 + 1)

    def t_pce_done(e, j):
        return t0 + 2 * NU * (4 * e) + NJ * e + j + 1

    def t_h_done(idx, u):
        return t_base(idx) + u + 1

    def t_o_done(idx, u):
        return t_base(idx) + NU + u + 1

    def v_base(idx):
        return v0 + 1 + 2 * NU * idx + NJ * (idx // 4 + 1)

    def v_ce_done(e, j):
        return v0 + 1 + 2 * NU * (4 * e) + NJ * e + j + 1

    def v_h_done(idx, u):
        return v_base(idx) + u + 1

    def v_o_done(idx, u):
        return v_base(idx) + NU + u + 1

    def s_h_done(idx, u):
        return s0 + NU * idx + u + 1

    def g_batch(idx):
        return gp0[idx % 2] + 128 * (idx // 2 + 1)

    def g2_batch(idx):
        return g20 + 128 * (idx + 1)

    def d_cw(e):
        return d0 + 16 * (e + 1)

    with nc.Block() as blk:
        @blk.gpsimd
        def _(g):
            for idx, (e, q) in enumerate(EQ):
                bi = idx % 2
                if idx >= 2:
                    g.wait_ge(stt.sem, t_h_done(idx - 2, NU - 1))
                for dc in range(8):
                    g.dma_start(out=w1s[bi][dc][:],
                                in_=w1_ap[e, 128 * dc:128 * (dc + 1),
                                          1024 * q:1024 * (q + 1)]) \
                        .then_inc(sgp[bi].sem, 16)
                if idx >= 1:
                    g.wait_ge(stt.sem, t_o_done(idx - 1, NU - 1))
                for wc in range(8):
                    g.dma_start(out=w2s[wc][:],
                                in_=w2_ap[e, 1024 * q + 128 * wc:
                                          1024 * q + 128 * (wc + 1), :]) \
                        .then_inc(sg2.sem, 16)
            g.wait_ge(sgp[0].sem, gp0[0] + 128 * (len(EQ) // 2))
            g.wait_ge(sgp[1].sem, gp0[1] + 128 * (len(EQ) // 2))
            g.wait_ge(sg2.sem, g20 + 128 * len(EQ))
        sgp[0].n = gp0[0] + 128 * (len(EQ) // 2)
        sgp[1].n = gp0[1] + 128 * (len(EQ) // 2)
        sg2.n = g20 + 128 * len(EQ)

        @blk.sync
        def _(s):
            for e in range(8):
                if e > 0:
                    s.wait_ge(stt.sem, t_pce_done(e - 1, NJ - 1))
                s.dma_start(out=cwrow[:], in_=cwT_hbm[e:e + 1, 0:T]) \
                    .then_inc(sd.sem, 16)
            s.wait_ge(sd.sem, d0 + 16 * 8)
        sd.n = d0 + 16 * 8

        @blk.tensor
        def _(t):
            t.wait_ge(sv.sem, v0 + 1)
            for idx, (e, q) in enumerate(EQ):
                bi = idx % 2
                if q == 0:
                    t.wait_ge(sd.sem, d_cw(e))
                    for j, (off, sz) in enumerate(TQ):
                        if e > 0 or j > 0:
                            pj = (e, j - 1) if j > 0 else (e - 1, NJ - 1)
                            t.wait_ge(sv.sem, v_ce_done(*pj))
                        t.matmul(pce[:, 0:sz], onesr[:],
                                 cwrow[0:1, off:off + sz], start=True,
                                 stop=True).then_inc(stt.sem, 1)
                t.wait_ge(sgp[idx % 2].sem, g_batch(idx))
                for u in range(NU):
                    wcc, j = divmod(u, NJ)
                    off, sz = TQ[j]
                    if idx > 0 or u >= 2:
                        pi, pu = (idx, u - 2) if u >= 2 else (idx - 1, NU + u - 2)
                        t.wait_ge(ss.sem, s_h_done(pi, pu))
                    for dc in range(8):
                        inst = t.matmul(ph[u % 2][:, 0:sz],
                                        w1s[bi][dc][:, 128 * wcc:128 * (wcc + 1)],
                                        xln_bf[dc][:, off:off + sz],
                                        start=(dc == 0), stop=(dc == 7))
                    inst.then_inc(stt.sem, 1)
                    if serial:
                        t.wait_ge(sv.sem, v_h_done(idx, u))

                t.wait_ge(sg2.sem, g2_batch(idx))
                for u in range(NU):
                    doc, j = divmod(u, NJ)
                    off, sz = TQ[j]
                    t.wait_ge(sv.sem, v_h_done(idx, (8 - 1) * NJ + j))
                    if idx > 0 or u >= 2:
                        pi, pu = (idx, u - 2) if u >= 2 else (idx - 1, NU + u - 2)
                        t.wait_ge(sv.sem, v_o_done(pi, pu) if pu < NU
                                  else v_o_done(pi, pu - NU))
                    for wc in range(8):
                        inst = t.matmul(po[u % 2][:, 0:sz],
                                        w2s[wc][:, 128 * doc:128 * (doc + 1)],
                                        hq[wc][:, off:off + sz],
                                        start=(wc == 0), stop=(wc == 7))
                    inst.then_inc(stt.sem, 1)
                    if serial:
                        t.wait_ge(sv.sem, v_o_done(idx, u))
        stt.n = t0 + 2 * NU * 32 + NJ * 8

        @blk.scalar
        def _(s):
            for idx, (e, q) in enumerate(EQ):
                for u in range(NU):
                    wcc, j = divmod(u, NJ)
                    off, sz = TQ[j]
                    s.wait_ge(stt.sem, t_h_done(idx, u))
                    if idx > 0 or u >= 2:
                        pi, pu = (idx, u - 2) if u >= 2 else (idx - 1, NU + u - 2)
                        s.wait_ge(sv.sem, v_h_done(pi, pu) if pu < NU
                                  else v_h_done(pi, pu - NU))
                    s.activation(out=htmp[u % 2][:, 0:sz], in_=ph[u % 2][:, 0:sz],
                                 func=AF.Gelu).then_inc(ss.sem, 1)
        ss.n = s0 + NU * 32

        @blk.vector
        def _(v):
            v.memset(onesr[:], 1.0)
            v.tensor_copy(junk[0:1, 0:128], onesr[0:1, 0:128]) \
                .then_inc(sv.sem, 1)
            for idx, (e, q) in enumerate(EQ):
                if q == 0:
                    for j, (off, sz) in enumerate(TQ):
                        v.wait_ge(stt.sem, t_pce_done(e, j))
                        v.tensor_copy(ce[:, off:off + sz], pce[:, 0:sz]) \
                            .then_inc(sv.sem, 1)
                if q == 0:
                    v.wait_ge(sv.sem, v_ce_done(e, NJ - 1))
                for u in range(NU):
                    wcc, j = divmod(u, NJ)
                    off, sz = TQ[j]
                    v.wait_ge(ss.sem, s_h_done(idx, u))
                    v.tensor_mul(hq[wcc][:, off:off + sz], htmp[u % 2][:, 0:sz],
                                 ce[:, off:off + sz]).then_inc(sv.sem, 1)
                for u in range(NU):
                    doc, j = divmod(u, NJ)
                    off, sz = TQ[j]
                    v.wait_ge(stt.sem, t_o_done(idx, u))
                    v.scalar_tensor_tensor(
                        out=resid[doc][:, off:off + sz],
                        in0=po[u % 2][:, 0:sz], scalar=scale_out,
                        in1=resid[doc][:, off:off + sz],
                        op0=ALU.mult, op1=ALU.add).then_inc(sv.sem, 1)
        sv.n = v0 + 1 + 2 * NU * 32 + NJ * 8

    sc.close()



def phase_mamba_front(b, xln_bf, T, Th, win_ap, diag_ap, bconv_dm_ap, wx_ap,
                      wdt_ap, bdt_dm_ap, o_u, o_delta, o_zsilu, o_bc, o_dsum,
                      tag="mf"):
    nc = b.nc
    sc = contextlib.ExitStack()
    _sb = lambda n, sh, dt=F32: b.sb(tag + n, sh, dt, scope=sc)
    _ps = lambda n, sh: b.ps(tag + n, sh, scope=sc)

    winb = [[_sb(f"win{bi}_{dc}", [128, 128], BF16) for dc in range(8)]
            for bi in range(2)]
    diagb = [[_sb(f"diag{bi}_{k}", [128, 128]) for k in range(4)]
             for bi in range(2)]
    wxb = [_sb(f"wx{bi}", [128, 96]) for bi in range(2)]
    wdb = [_sb(f"wd{bi}", [64, 128]) for bi in range(2)]
    bconv_t = _sb("bconv", [128, 16])
    bdt_t = _sb("bdt", [128, 16])
    xz = _sb("xz", [128, 2051])
    xbr = _sb("xbr", [128, 2048])
    ubf = _sb("ubf", [128, 2048], BF16)
    dl = _sb("dl", [128, 2048])
    dta = _sb("dta", [128, 512])
    dtb = _sb("dtb", [128, 512])
    zbf = _sb("zbf", [128, 2048], BF16)
    params_sb = _sb("params", [96, 2048])
    dsum_t = _sb("dsum", [128, 16])
    pxz = _ps("pxz", [128, 512])
    pcv = _ps("pcv", [128, 512])
    ppar = [_ps(f"ppar{j}", [96, 512]) for j in range(4)]
    pdl = _ps("pdl", [128, 512])

    sd, sv, ss, stt = b.sem("d"), b.sem("v"), b.sem("s"), b.sem("t")
    sgp = [b.sem("g0"), b.sem("g1")]
    d0, v0, s0, t0 = sd.n, sv.n, ss.n, stt.n
    gp0 = [sgp[0].n, sgp[1].n]
    TQh = fsplit(Th, 512)
    TQ = fsplit(T, 512)
    NJh, NJ = len(TQh), len(TQ)
    assert T <= 2048 and Th <= 2051 and len(TQ) <= 4

    def g_done(cc):
        par = cc % 2
        tot = 0
        for c in range(cc + 1):
            if c % 2 == par:
                tot += 208 if c < 16 else 128
        return gp0[par] + tot

    def g_dl_done(cc):
        par = cc % 2
        base = g_done(30 if par == 0 else 31)
        return base + 16 * (cc // 2 + 1)

    def t_cc_base(cc):
        tot = 0
        for c in range(cc):
            tot += (NJh + 2 * NJ) if c < 16 else NJ
        return t0 + tot

    t_all = t_cc_base(32)

    def t_dl_base(cc):
        return t_all + NJ * cc

    def s_cc_base(cc):
        tot = 0
        for c in range(cc):
            tot += (NJh + NJ) if c < 16 else NJ
        return s0 + tot

    s_all = s_cc_base(32)

    def s_dl_base(cc):
        return s_all + 1 + NJ * cc

    def v_dl_base(cc):
        return v0 + (NJ + 1) * cc

    def d_u(cc):
        return d0 + 16 * (2 + cc + 1)

    def d_z(zc):
        return d0 + 16 * (18 + zc + 1)

    def d_dl(cc):
        return d0 + 16 * (34 + cc + 1)

    NDMA = 52

    with nc.Block() as blk:
        @blk.gpsimd
        def _(g):
            for cc in range(32):
                par = cc % 2
                if cc >= 2:
                    g.wait_ge(stt.sem, t_cc_base(cc - 2) +
                              (NJh + 2 * NJ if cc - 2 < 16 else NJ))
                for dc in range(8):
                    g.dma_start(out=winb[par][dc][:],
                                in_=win_ap[128 * dc:128 * (dc + 1),
                                           128 * cc:128 * (cc + 1)]) \
                        .then_inc(sgp[par].sem, 16)
                if cc < 16:
                    for k in range(4):
                        g.dma_start(out=diagb[par][k][:], in_=diag_ap[cc, k]) \
                            .then_inc(sgp[par].sem, 16)
                    g.dma_start(out=wxb[par][:],
                                in_=wx_ap[128 * cc:128 * (cc + 1), :]) \
                        .then_inc(sgp[par].sem, 16)
            for cc in range(16):
                par = cc % 2
                if cc >= 2:
                    g.wait_ge(stt.sem, t_dl_base(cc - 2) + NJ)
                g.dma_start(out=wdb[par][:],
                            in_=wdt_ap[:, 128 * cc:128 * (cc + 1)]) \
                    .then_inc(sgp[par].sem, 16)
            g.wait_ge(sgp[0].sem, g_dl_done(14))
            g.wait_ge(sgp[1].sem, g_dl_done(15))
        sgp[0].n = g_dl_done(14)
        sgp[1].n = g_dl_done(15)

        @blk.sync
        def _(s):
            s.dma_start(out=bconv_t[:], in_=bconv_dm_ap).then_inc(sd.sem, 16)
            s.dma_start(out=bdt_t[:], in_=bdt_dm_ap).then_inc(sd.sem, 16)
            for cc in range(16):
                s.wait_ge(ss.sem, s_cc_base(cc) + NJh + NJ)
                s.dma_start(out=o_u[128 * cc:128 * (cc + 1), :],
                            in_=ubf[:, 0:T]).then_inc(sd.sem, 16)
            for zc in range(16):
                s.wait_ge(ss.sem, s_cc_base(16 + zc) + NJ)
                s.dma_start(out=o_zsilu[128 * zc:128 * (zc + 1), :],
                            in_=zbf[:, 0:T]).then_inc(sd.sem, 16)
            for cc in range(16):
                s.wait_ge(sv.sem, v_dl_base(cc) + NJ + 1)
                s.dma_start(out=o_delta[128 * cc:128 * (cc + 1), :],
                            in_=dl[:, 0:T]).then_inc(sd.sem, 16)
            s.wait_ge(ss.sem, s_all + 1)
            s.dma_start(out=o_bc, in_=params_sb[64:96, 0:T]).then_inc(sd.sem, 16)
            s.wait_ge(sv.sem, v_dl_base(15) + NJ + 1)
            s.dma_start(out=o_dsum, in_=dsum_t[:]).then_inc(sd.sem, 16)
            s.wait_ge(sd.sem, d0 + 16 * NDMA)
        sd.n = d0 + 16 * NDMA

        @blk.tensor
        def _(t):
            for cc in range(32):
                par = cc % 2
                tb = t_cc_base(cc)
                t.wait_ge(sgp[par].sem, g_done(cc))
                TQx = TQh if cc < 16 else TQ
                coff = 0 if cc < 16 else HALO
                for j, (off, sz) in enumerate(TQx):
                    if cc >= 1 or j >= 1:
                        pcc, pj = (cc, j - 1) if j >= 1 else (cc - 1, None)
                        if pj is None:
                            pj = (NJh if pcc < 16 else NJ) - 1
                        t.wait_ge(ss.sem, s_cc_base(pcc) + pj + 1)
                    for dc in range(8):
                        inst = t.matmul(pxz[:, 0:sz], winb[par][dc][:],
                                        xln_bf[dc][:, coff + off:
                                                    coff + off + sz],
                                        start=(dc == 0), stop=(dc == 7))
                    inst.then_inc(stt.sem, 1)
                if cc < 16:
                    t.wait_ge(ss.sem, s_cc_base(cc) + NJh)
                    for j, (off, sz) in enumerate(TQ):
                        if cc >= 1 or j >= 1:
                            pcc, pj = (cc, j - 1) if j >= 1 else (cc - 1, NJ - 1)
                            t.wait_ge(ss.sem, s_cc_base(pcc) + NJh + pj + 1)
                        for k in range(4):
                            inst = t.matmul(pcv[:, 0:sz], diagb[par][k][:],
                                            xz[:, off + k:off + k + sz],
                                            start=(k == 0), stop=(k == 3))
                        inst.then_inc(stt.sem, 1)
                    t.wait_ge(ss.sem, s_cc_base(cc) + NJh + NJ)
                    for j, (off, sz) in enumerate(TQ):
                        t.matmul(ppar[j][:, 0:sz], wxb[par][:],
                                 xbr[:, off:off + sz], start=(cc == 0),
                                 stop=(cc == 15)).then_inc(stt.sem, 1)
            t.wait_ge(ss.sem, s_all + 1)
            for cc in range(16):
                par = cc % 2
                t.wait_ge(sgp[par].sem, g_dl_done(cc))
                for j, (off, sz) in enumerate(TQ):
                    if cc >= 1 or j >= 1:
                        pcc, pj = (cc, j - 1) if j >= 1 else (cc - 1, NJ - 1)
                        t.wait_ge(ss.sem, s_dl_base(pcc) + pj + 1)
                    t.matmul(pdl[:, 0:sz], wdb[par][:],
                             params_sb[0:64, off:off + sz], start=True,
                             stop=True).then_inc(stt.sem, 1)
        stt.n = t_all + NJ * 16

        @blk.scalar
        def _(s):
            s.wait_ge(sd.sem, d0 + 32)
            for cc in range(32):
                sb_ = s_cc_base(cc)
                TQx = TQh if cc < 16 else TQ
                for j, (off, sz) in enumerate(TQx):
                    s.wait_ge(stt.sem, t_cc_base(cc) + j + 1)
                    if cc < 16:
                        s.activation(out=xz[:, off:off + sz], in_=pxz[:, 0:sz],
                                     func=AF.Copy).then_inc(ss.sem, 1)
                    else:
                        if cc >= 18:
                            pass
                        if j == 0 and cc >= 17:
                            s.wait_ge(sd.sem, d_z(cc - 16 - 1))
                        s.activation(out=zbf[:, off:off + sz], in_=pxz[:, 0:sz],
                                     func=AF.Silu).then_inc(ss.sem, 1)
                if cc < 16:
                    if cc >= 1:
                        s.wait_ge(sd.sem, d_u(cc - 1))
                    for j, (off, sz) in enumerate(TQ):
                        s.wait_ge(stt.sem, t_cc_base(cc) + NJh + j + 1)
                        s.activation(out=xbr[:, off:off + sz], in_=pcv[:, 0:sz],
                                     func=AF.Silu, bias=bconv_t[:, cc:cc + 1])
                        s.activation(out=ubf[:, off:off + sz],
                                     in_=xbr[:, off:off + sz], func=AF.Copy) \
                            .then_inc(ss.sem, 1)
            s.wait_ge(stt.sem, t_all)
            for j, (off, sz) in enumerate(TQ):
                inst = s.activation(out=params_sb[:, off:off + sz],
                                    in_=ppar[j][:, 0:sz], func=AF.Copy)
            inst.then_inc(ss.sem, 1)
            for cc in range(16):
                if cc >= 1:
                    s.wait_ge(sd.sem, d_dl(cc - 1))
                for j, (off, sz) in enumerate(TQ):
                    s.wait_ge(stt.sem, t_dl_base(cc) + j + 1)
                    if cc >= 1 or j >= 1:
                        pv = (v_dl_base(cc) + j) if j >= 1 else                             (v_dl_base(cc - 1) + NJ)
                        s.wait_ge(sv.sem, pv)
                    bia = bdt_t[:, cc:cc + 1]
                    s.activation(out=dta[:, 0:sz], in_=pdl[:, 0:sz],
                                 func=AF.Abs, bias=bia)
                    s.activation(out=dta[:, 0:sz], in_=dta[:, 0:sz],
                                 func=AF.Exp, scale=-1.0)
                    s.activation(out=dta[:, 0:sz], in_=dta[:, 0:sz],
                                 func=AF.Ln, bias=1.0)
                    s.activation(out=dtb[:, 0:sz], in_=pdl[:, 0:sz],
                                 func=AF.Relu, bias=bia).then_inc(ss.sem, 1)
        ss.n = s_all + 1 + NJ * 16

        @blk.vector
        def _(v):
            for cc in range(16):
                if cc >= 1:
                    v.wait_ge(sd.sem, d_dl(cc - 1))
                for j, (off, sz) in enumerate(TQ):
                    v.wait_ge(ss.sem, s_dl_base(cc) + j + 1)
                    v.tensor_add(dl[:, off:off + sz], dta[:, 0:sz],
                                 dtb[:, 0:sz]).then_inc(sv.sem, 1)
                v.reduce_sum(dsum_t[:, cc:cc + 1], dl[:, 0:T], axis=AXX) \
                    .then_inc(sv.sem, 1)
        sv.n = v0 + (NJ + 1) * 16

    sc.close()



def phase_scan(b, T, o_delta, o_u, o_bc, a_dm_ap, dp_dm_ap, init_ap, o_y,
               o_state, tag="sc"):
    nc = b.nc
    with_y = o_y is not None
    sc = contextlib.ExitStack()
    _sb = lambda n, sh, dt=F32: b.sb(tag + n, sh, dt, scope=sc)
    _ps = lambda n, sh: b.ps(tag + n, sh, scope=sc)

    CW = 256
    TS = fsplit(T, CW)
    NQ = len(TS)
    delq = _sb("delq", [128, 16 * CW])
    uq = _sb("uq", [128, 16 * CW])
    duq = _sb("duq", [128, 16 * CW])
    da = _sb("da", [128, 16 * CW])
    dbu = _sb("dbu", [128, 16 * CW])
    sn = _sb("sn", [128, 16 * CW])
    yq = _sb("yq", [128, 16 * CW]) if with_y else None
    ym = _sb("ym", [128, 16 * CW]) if with_y else None
    states = _sb("states", [128, 256])
    araw = _sb("araw", [128, 256])
    a_t = _sb("a_t", [128, 256])
    dp_t = _sb("dp_t", [128, 16]) if with_y else None
    bcrow = _sb("bcrow", [1, 32 * CW])
    bn = _sb("bn", [128, CW])
    cn = _sb("cn", [128, CW]) if with_y else None
    onesr = _sb("onesr", [1, 128])
    junk = _sb("junk", [1, 128])
    pb = _ps("pb", [128, CW])
    pc = _ps("pc", [128, CW]) if with_y else None

    sd, sv, ss, stt = b.sem("d"), b.sem("v"), b.sem("s"), b.sem("t")
    sgu = b.sem("g0")
    d0, v0, s0, t0 = sd.n, sv.n, ss.n, stt.n
    g0 = sgu.n

    NVn = 5 if with_y else 3
    NV = 1 + 16 * NVn + (1 if with_y else 0)
    NSn = 3 if with_y else 2
    NS = 16 * NSn
    NTn = 2 if with_y else 1
    NT = 16 * NTn
    NR = 32 if with_y else 16

    def v_tq(i):
        return v0 + 2 + NV * i

    def v_n(i, n, k):
        return v_tq(i) + 1 + NVn * n + k

    def s_tq(i):
        return s0 + 1 + NS * i

    def s_n(i, n, k):
        return s_tq(i) + NSn * n + k

    def t_n(i, n, k):
        return t0 + NT * i + NTn * n + k

    nini = 1 + (1 if with_y else 0) + (1 if init_ap is not None else 0)
    d_yst = {}
    d_del = {}
    cnt = nini
    for i in range(NQ):
        if with_y and i > 0:
            cnt += 16
            d_yst[i - 1] = cnt
        cnt += 16 + NR
        d_del[i] = cnt
    if with_y:
        cnt += 16
        d_yst[NQ - 1] = cnt
    if o_state is not None:
        cnt += 1
    NDMA = cnt

    with nc.Block() as blk:
        @blk.sync
        def _(s):
            s.dma_start(out=araw[:], in_=a_dm_ap).then_inc(sd.sem, 16)
            if with_y:
                s.dma_start(out=dp_t[:], in_=dp_dm_ap).then_inc(sd.sem, 16)
            if init_ap is not None:
                s.dma_start(out=states[:], in_=init_ap).then_inc(sd.sem, 16)
            for i, (off, sz) in enumerate(TS):
                if i > 0:
                    s.wait_ge(sv.sem, v_tq(i - 1) + NV)
                    if with_y:
                        poff, psz = TS[i - 1]
                        for cc in range(16):
                            s.dma_start(out=o_y[128 * cc:128 * (cc + 1),
                                                poff:poff + psz],
                                        in_=yq[:, CW * cc:CW * cc + psz]) \
                                .then_inc(sd.sem, 16)
                for cc in range(16):
                    s.dma_start(out=delq[:, CW * cc:CW * cc + sz],
                                in_=o_delta[128 * cc:128 * (cc + 1),
                                            off:off + sz]).then_inc(sd.sem, 16)
                for r in range(NR):
                    s.dma_start(out=bcrow[0:1, CW * r:CW * r + sz],
                                in_=o_bc[r:r + 1, off:off + sz]) \
                        .then_inc(sd.sem, 16)
            s.wait_ge(sv.sem, v_tq(NQ - 1) + NV)
            if with_y:
                poff, psz = TS[NQ - 1]
                for cc in range(16):
                    s.dma_start(out=o_y[128 * cc:128 * (cc + 1),
                                        poff:poff + psz],
                                in_=yq[:, CW * cc:CW * cc + psz]) \
                        .then_inc(sd.sem, 16)
            if o_state is not None:
                s.dma_start(out=o_state, in_=states[:]).then_inc(sd.sem, 16)
            s.wait_ge(sd.sem, d0 + 16 * NDMA)
        sd.n = d0 + 16 * NDMA

        @blk.gpsimd
        def _(g):
            for i, (off, sz) in enumerate(TS):
                if i > 0:
                    g.wait_ge(sv.sem, v_tq(i - 1) + NV)
                for cc in range(16):
                    g.dma_start(out=uq[:, CW * cc:CW * cc + sz],
                                in_=o_u[128 * cc:128 * (cc + 1),
                                        off:off + sz]).then_inc(sgu.sem, 16)
            g.wait_ge(sgu.sem, g0 + 256 * NQ)
        sgu.n = g0 + 256 * NQ

        @blk.vector
        def _(v):
            vk = [v0]

            def step(inst, k=1):
                inst.then_inc(sv.sem, k)
                vk[0] += k
                v.wait_ge(sv.sem, vk[0])

            v.wait_ge(ss.sem, s0 + 1)
            step(v.tensor_scalar_mul(a_t[:], a_t[:], -1.0))
            if init_ap is None:
                v.memset(states[:], 0.0)
            step(v.memset(onesr[:], 1.0))
            for i, (off, sz) in enumerate(TS):
                v.wait_ge(sd.sem, d0 + 16 * d_del[i])
                v.wait_ge(sgu.sem, g0 + 256 * (i + 1))
                step(v.tensor_mul(duq[:], delq[:], uq[:]))
                if with_y and i > 0:
                    v.wait_ge(sd.sem, d0 + 16 * d_yst[i - 1])
                for n in range(16):
                    v.wait_ge(ss.sem, s_n(i, n, 2))
                    step(v.tensor_tensor(
                        out=dbu[:].rearrange("p (c w) -> p c w", c=16),
                        in0=duq[:].rearrange("p (c w) -> p c w", c=16),
                        in1=bn[:].unsqueeze(1).broadcast_to([128, 16, CW]),
                        op=ALU.mult))
                    v.wait_ge(ss.sem, s_n(i, n, 1))
                    for dg in range(16):
                        inst = v.tensor_tensor_scan(
                            out=sn[:, CW * dg:CW * dg + sz],
                            data0=da[:, CW * dg:CW * dg + sz],
                            data1=dbu[:, CW * dg:CW * dg + sz],
                            initial=states[:, 16 * dg + n:16 * dg + n + 1],
                            op0=ALU.mult, op1=ALU.add)
                    step(inst)
                    step(v.tensor_copy(
                        states[:].rearrange("p (c n) -> p c n", c=16)
                        [:, :, n:n + 1],
                        sn[:].rearrange("p (c w) -> p c w", c=16)
                        [:, :, sz - 1:sz]))
                    if with_y:
                        v.wait_ge(ss.sem, s_n(i, n, 3))
                        dst = yq if n == 0 else ym
                        step(v.tensor_tensor(
                            out=dst[:].rearrange("p (c w) -> p c w", c=16),
                            in0=sn[:].rearrange("p (c w) -> p c w", c=16),
                            in1=cn[:].unsqueeze(1).broadcast_to([128, 16, CW]),
                            op=ALU.mult))
                        if n == 0:
                            step(v.tensor_copy(junk[0:1, 0:128],
                                               onesr[0:1, 0:128]))
                        else:
                            step(v.tensor_add(yq[:], yq[:], ym[:]))
                if with_y:
                    for cc in range(16):
                        inst = v.scalar_tensor_tensor(
                            out=yq[:, CW * cc:CW * cc + sz],
                            in0=uq[:, CW * cc:CW * cc + sz],
                            scalar=dp_t[:, cc:cc + 1],
                            in1=yq[:, CW * cc:CW * cc + sz],
                            op0=ALU.mult, op1=ALU.add)
                    step(inst)
        sv.n = v0 + 2 + NV * NQ

        @blk.scalar
        def _(s):
            s.wait_ge(sd.sem, d0 + 16)
            s.activation(out=a_t[:], in_=araw[:], func=AF.Exp) \
                .then_inc(ss.sem, 1)
            for i, (off, sz) in enumerate(TS):
                for n in range(16):
                    if n == 0:
                        s.wait_ge(sd.sem, d0 + 16 * d_del[i])
                        s.wait_ge(sv.sem, v0 + 2)
                    pv = v_n(i, n - 1, 2) if n > 0 else \
                        (v_tq(i - 1) + NV if i > 0 else None)
                    if pv is not None:
                        s.wait_ge(sv.sem, pv)
                    for dg in range(16):
                        inst = s.activation(
                            out=da[:, CW * dg:CW * dg + sz],
                            in_=delq[:, CW * dg:CW * dg + sz], func=AF.Exp,
                            scale=a_t[:, 16 * dg + n:16 * dg + n + 1])
                    inst.then_inc(ss.sem, 1)
                    s.wait_ge(stt.sem, t_n(i, n, 1))
                    if n > 0:
                        s.wait_ge(sv.sem, v_n(i, n - 1, 1))
                    s.activation(out=bn[:, 0:sz], in_=pb[:, 0:sz],
                                 func=AF.Copy).then_inc(ss.sem, 1)
                    if with_y:
                        s.wait_ge(stt.sem, t_n(i, n, 2))
                        if n > 0:
                            s.wait_ge(sv.sem, v_n(i, n - 1, 4))
                        s.activation(out=cn[:, 0:sz], in_=pc[:, 0:sz],
                                     func=AF.Copy).then_inc(ss.sem, 1)
        ss.n = s0 + 1 + NS * NQ

        @blk.tensor
        def _(t):
            t.wait_ge(sv.sem, v0 + 2)
            for i, (off, sz) in enumerate(TS):
                t.wait_ge(sd.sem, d0 + 16 * d_del[i])
                for n in range(16):
                    if n > 0 or i > 0:
                        ps_ = s_n(i, n - 1, 2) if n > 0 else s_tq(i - 1) + NS
                        t.wait_ge(ss.sem, ps_)
                    t.matmul(pb[:, 0:sz], onesr[:],
                             bcrow[0:1, CW * n:CW * n + sz], start=True,
                             stop=True).then_inc(stt.sem, 1)
                    if with_y:
                        if n > 0 or i > 0:
                            ps_ = s_n(i, n - 1, 3) if n > 0 else \
                                s_tq(i - 1) + NS
                            t.wait_ge(ss.sem, ps_)
                        t.matmul(pc[:, 0:sz], onesr[:],
                                 bcrow[0:1, CW * (16 + n):CW * (16 + n) + sz],
                                 start=True, stop=True).then_inc(stt.sem, 1)
        stt.n = t0 + NT * NQ

    sc.close()



def build_launch1(T):
    Th = T + HALO
    b = B()
    x_ap = b.din("x_dmaj", [1024, Th])
    gw_ap = b.din("gate_w", [1024, 8])
    id_ap = b.din("ident", [128, 128])
    w1_ap = b.din("w1", [8, 1024, 4096])
    w2_ap = b.din("w2", [8, 4096, 1024])
    win_ap = b.din("win", [1024, 4096])
    diag_ap = b.din("conv_diag", [16, 4, 128, 128])
    bconv_ap = b.din("bconv_dm", [128, 16])
    wx_ap = b.din("wx", [2048, 96])
    wdt_ap = b.din("wdt", [64, 2048])
    bdt_ap = b.din("bdt_dm", [128, 16])
    a_ap = b.din("a_dm", [128, 256])
    o_resid1 = b.dout("o_resid1", [1024, T])
    o_u = b.dout("o_u", [2048, T], BF16)
    o_delta = b.dout("o_delta", [2048, T])
    o_zsilu = b.dout("o_zsilu", [2048, T], BF16)
    o_bc = b.dout("o_bc", [32, T])
    o_dsum = b.dout("o_dsum", [128, 16])
    o_state = b.dout("o_state", [128, 256])
    cwT = b.nc.dram_tensor("cwT", [8, Th], F32, kind="Internal").ap()

    with b.es:
        b.init_consts([EPS, 1.0 / D, -0.5, -1.0])
        ident = b.sb("ident_sb", [128, 128])
        phase_load_dmaj(b, id_ap, [ident], 128, tag="ldi")
        with contextlib.ExitStack() as sc1:
            resid = [b.sb(f"res{g}", [128, Th], scope=sc1) for g in range(8)]
            xlb = [b.sb(f"xlb{g}", [128, Th], BF16, scope=sc1)
                   for g in range(8)]
            phase_load_dmaj(b, x_ap, resid, Th, tag="ldx")
            with contextlib.ExitStack() as sc2:
                xlf = [b.sb(f"xlf{g}", [128, Th], scope=sc2) for g in range(8)]
                phase_layernorm(b, resid, Th, out_bf=xlb, out_f32=xlf,
                                tag="ln1")
                phase_gating(b, xlf, Th, gw_ap, ident, cwT, tag="gt")
            phase_moe(b, xlb, resid, Th, w1_ap, w2_ap, cwT, scale_out=0.5,
                      tag="moe")
            phase_layernorm(b, resid, Th, out_bf=xlb, tag="ln2")
            phase_store_dmaj(b, resid, o_resid1, T, tag="str1", col0=HALO)
            phase_mamba_front(b, xlb, T, Th, win_ap, diag_ap, bconv_ap,
                              wx_ap, wdt_ap, bdt_ap, o_u, o_delta, o_zsilu,
                              o_bc, o_dsum, tag="mf")
        phase_scan(b, T, o_delta, o_u, o_bc, a_ap, None, None, None,
                   o_state, tag="sc")
    return b


def host_prep_l0(inp, T):
    w = {}
    w["gate_w"] = np.ascontiguousarray(inp["L0_ffn_gate_w"])
    w["w1"] = np.ascontiguousarray(inp["L0_ffn_w1"])
    w["w2"] = np.ascontiguousarray(inp["L0_ffn_w2"])
    w["ident"] = np.eye(128, dtype=np.float32)
    w["win"] = np.ascontiguousarray(inp["L0_attn_Win"])
    wc = np.asarray(inp["L0_attn_Wconv"])
    diag = np.zeros((16, 4, 128, 128), dtype=np.float32)
    for cc in range(16):
        for k in range(4):
            np.fill_diagonal(diag[cc, k], wc[128 * cc:128 * (cc + 1), 0, k])
    w["conv_diag"] = diag
    w["bconv_dm"] = np.ascontiguousarray(
        np.asarray(inp["L0_attn_bconv"]).reshape(16, 128).T)
    w["wx"] = np.ascontiguousarray(inp["L0_attn_Wx"])
    w["wdt"] = np.ascontiguousarray(inp["L0_attn_Wdt"])
    w["bdt_dm"] = np.ascontiguousarray(
        np.asarray(inp["L0_attn_bdt"]).reshape(16, 128).T)
    w["a_dm"] = np.ascontiguousarray(
        np.asarray(inp["L0_attn_A_log"]).reshape(16, 128, 16)
        .transpose(1, 0, 2).reshape(128, 256))
    w["dp_dm"] = np.ascontiguousarray(
        np.asarray(inp["L0_attn_Dp"]).reshape(16, 128).T)
    w["wout"] = np.ascontiguousarray(inp["L0_attn_Wout"])
    return w



def phase_chain(b, ch_dsum_ap, ch_state_ap, msel_ap, a_dm_ap, sin_hbm,
                tag="chn"):
    nc = b.nc
    sc = contextlib.ExitStack()
    _sb = lambda n, sh, dt=F32: b.sb(tag + n, sh, dt, scope=sc)
    dsums = _sb("dsums", [128, 4 * 16])
    stats = _sb("stats", [128, 4 * 256])
    msel = _sb("msel", [128, 4])
    araw = _sb("araw", [128, 256])
    a_t = _sb("a_t", [128, 256])
    pk = _sb("pk", [128, 256])
    srun = _sb("srun", [128, 256])
    tmp = _sb("tmp", [128, 256])
    sin = _sb("sin", [128, 256])
    sd, sv, ss = b.sem("d"), b.sem("v"), b.sem("s")
    d0, v0, s0 = sd.n, sv.n, ss.n

    with nc.Block() as blk:
        @blk.sync
        def _(s):
            for k in range(4):
                s.dma_start(out=dsums[:, 16 * k:16 * (k + 1)],
                            in_=ch_dsum_ap[k]).then_inc(sd.sem, 16)
                s.dma_start(out=stats[:, 256 * k:256 * (k + 1)],
                            in_=ch_state_ap[k]).then_inc(sd.sem, 16)
            s.dma_start(out=msel[:], in_=msel_ap).then_inc(sd.sem, 16)
            s.dma_start(out=araw[:], in_=a_dm_ap).then_inc(sd.sem, 16)
            s.wait_ge(sd.sem, d0 + 16 * 10)
        sd.n = d0 + 16 * 10

        @blk.scalar
        def _(s):
            s.wait_ge(sd.sem, d0 + 16 * 10)
            s.activation(out=a_t[:], in_=araw[:], func=AF.Exp) \
                .then_inc(ss.sem, 1)
            for k in range(4):
                s.wait_ge(sv.sem, v0 + 2 if k == 0 else v0 + 5 * k)
                s.activation(out=pk[:], in_=pk[:], func=AF.Exp) \
                    .then_inc(ss.sem, 1)
        ss.n = s0 + 5

        @blk.vector
        def _(v):
            vk = [v0]

            def step(inst):
                inst.then_inc(sv.sem, 1)
                vk[0] += 1
                v.wait_ge(sv.sem, vk[0])

            v.wait_ge(ss.sem, s0 + 1)
            step(v.tensor_scalar_mul(a_t[:], a_t[:], -1.0))
            for k in range(4):
                step(v.tensor_tensor(
                    out=pk[:].rearrange("p (c n) -> p c n", c=16),
                    in0=a_t[:].rearrange("p (c n) -> p c n", c=16),
                    in1=dsums[:, 16 * k:16 * (k + 1)].unsqueeze(2)
                    .broadcast_to([128, 16, 16]), op=ALU.mult))
                v.wait_ge(ss.sem, s0 + 2 + k)
                if k == 0:
                    step(v.tensor_copy(srun[:], stats[:, 0:256]))
                else:
                    step(v.tensor_mul(tmp[:], pk[:], srun[:]))
                    step(v.tensor_add(srun[:], tmp[:],
                                      stats[:, 256 * k:256 * (k + 1)]))
                if k == 0:
                    step(v.tensor_scalar_mul(sin[:], srun[:],
                                             msel[:, 0:1]))
                else:
                    step(v.tensor_scalar_mul(tmp[:], srun[:],
                                             msel[:, k:k + 1]))
                    step(v.tensor_add(sin[:], sin[:], tmp[:]))
        sv.n = v0 + 1 + 3 + 5 * 3

        @blk.gpsimd
        def _(g):
            g.wait_ge(sv.sem, v0 + 1 + 3 + 5 * 3)
            g.dma_start(out=sin_hbm, in_=sin[:]).then_inc(sd.sem, 16)
            g.wait_ge(sd.sem, d0 + 16 * 11)
        sd.n = d0 + 16 * 11

    sc.close()



def phase_attn_tail(b, resid, T, y_hbm, zs_ap, wout_ap, tag="at"):
    nc = b.nc
    sc = contextlib.ExitStack()
    _sb = lambda n, sh, dt=F32: b.sb(tag + n, sh, dt, scope=sc)
    _ps = lambda n, sh: b.ps(tag + n, sh, scope=sc)
    CWt = 256
    woutb = [_sb(f"wo{cc}", [128, 1024], BF16) for cc in range(16)]
    ytq = _sb("ytq", [128, 16 * CWt])
    ztq = _sb("ztq", [128, 16 * CWt])
    yz = _sb("yz", [128, 16 * CWt], BF16)
    po = [_ps(f"po{k}", [128, CWt]) for k in range(2)]
    sd, sv, stt = b.sem("d"), b.sem("v"), b.sem("t")
    sgu = b.sem("g0")
    d0, v0, t0 = sd.n, sv.n, stt.n
    g0 = sgu.n
    TQ = fsplit(T, CWt)
    NJ = len(TQ)

    def v_tq(i):
        return v0 + 9 * i

    def t_tq(i):
        return t0 + 8 * i

    with nc.Block() as blk:
        @blk.sync
        def _(s):
            for i, (off, sz) in enumerate(TQ):
                if i > 0:
                    s.wait_ge(sv.sem, v_tq(i - 1) + 1)
                for cc in range(16):
                    s.dma_start(out=ytq[:, CWt * cc:CWt * cc + sz],
                                in_=y_hbm[128 * cc:128 * (cc + 1),
                                          off:off + sz]).then_inc(sd.sem, 16)
            s.wait_ge(sd.sem, d0 + 16 * 16 * NJ)
        sd.n = d0 + 16 * 16 * NJ

        @blk.gpsimd
        def _(g):
            for cc in range(16):
                g.dma_start(out=woutb[cc][:],
                            in_=wout_ap[128 * cc:128 * (cc + 1), :]) \
                    .then_inc(sgu.sem, 16)
            for i, (off, sz) in enumerate(TQ):
                if i > 0:
                    g.wait_ge(sv.sem, v_tq(i - 1) + 1)
                for cc in range(16):
                    g.dma_start(out=ztq[:, CWt * cc:CWt * cc + sz],
                                in_=zs_ap[128 * cc:128 * (cc + 1),
                                          off:off + sz]).then_inc(sgu.sem, 16)
            g.wait_ge(sgu.sem, g0 + 16 * 16 * (NJ + 1))
        sgu.n = g0 + 16 * 16 * (NJ + 1)

        @blk.vector
        def _(v):
            vk = [v0]

            def step(inst):
                inst.then_inc(sv.sem, 1)
                vk[0] += 1
                v.wait_ge(sv.sem, vk[0])

            for i, (off, sz) in enumerate(TQ):
                v.wait_ge(sd.sem, d0 + 16 * 16 * (i + 1))
                v.wait_ge(sgu.sem, g0 + 16 * 16 * (i + 2))
                if i > 0:
                    v.wait_ge(stt.sem, t_tq(i - 1) + 8)
                step(v.tensor_mul(yz[:], ytq[:], ztq[:]))
                for doc in range(8):
                    v.wait_ge(stt.sem, t_tq(i) + doc + 1)
                    step(v.tensor_add(resid[doc][:, off:off + sz],
                                      resid[doc][:, off:off + sz],
                                      po[doc % 2][:, 0:sz]))
        sv.n = v0 + 9 * NJ

        @blk.tensor
        def _(t):
            t.wait_ge(sgu.sem, g0 + 16 * 16)
            for i, (off, sz) in enumerate(TQ):
                t.wait_ge(sv.sem, v_tq(i) + 1)
                for doc in range(8):
                    if i > 0 or doc >= 2:
                        pi, pd = (i, doc - 2) if doc >= 2 else (i - 1, 6 + doc)
                        t.wait_ge(sv.sem, v_tq(pi) + 1 + pd + 1)
                    for cc in range(16):
                        inst = t.matmul(po[doc % 2][:, 0:sz],
                                        woutb[cc][:, 128 * doc:128 * (doc + 1)],
                                        yz[:, CWt * cc:CWt * cc + sz],
                                        start=(cc == 0), stop=(cc == 15))
                    inst.then_inc(stt.sem, 1)
        stt.n = t0 + 8 * NJ

    sc.close()



def build_launch2(T):
    b = B()
    r1_ap = b.din("resid1", [1024, T])
    u_ap = b.din("d_u", [2048, T], BF16)
    dl_ap = b.din("d_delta", [2048, T])
    zs_ap = b.din("d_zsilu", [2048, T], BF16)
    bc_ap = b.din("d_bc", [32, T])
    a_ap = b.din("a_dm", [128, 256])
    dp_ap = b.din("dp_dm", [128, 16])
    chd_ap = b.din("ch_dsum", [4, 128, 16])
    chs_ap = b.din("ch_state", [4, 128, 256])
    msel_ap = b.din("msel", [128, 4])
    wout_ap = b.din("wout", [2048, 1024])
    gw_ap = b.din("gate_w", [1024, 8])
    id_ap = b.din("ident", [128, 128])
    w1_ap = b.din("w1", [8, 1024, 4096])
    w2_ap = b.din("w2", [8, 4096, 1024])
    o_resid3 = b.dout("o_resid3", [1024, T])
    o_part = b.dout("o_part", [128, 8])
    sin_hbm = b.nc.dram_tensor("sin_hbm", [128, 256], F32, kind="Internal").ap()
    y_hbm = b.nc.dram_tensor("y_hbm", [2048, T], F32, kind="Internal").ap()
    cwT = b.nc.dram_tensor("cwT2", [8, T], F32, kind="Internal").ap()

    with b.es:
        b.init_consts([EPS, 1.0 / D, -0.5, -1.0])
        ident = b.sb("ident_sb", [128, 128])
        phase_load_dmaj(b, id_ap, [ident], 128, tag="ldi")
        phase_chain(b, chd_ap, chs_ap, msel_ap, a_ap, sin_hbm, tag="chn")
        phase_scan(b, T, dl_ap, u_ap, bc_ap, a_ap, dp_ap, sin_hbm, y_hbm,
                   None, tag="sc")
        with contextlib.ExitStack() as sc1:
            resid = [b.sb(f"res{g}", [128, T], scope=sc1) for g in range(8)]
            phase_load_dmaj(b, r1_ap, resid, T, tag="ldr")
            phase_attn_tail(b, resid, T, y_hbm, zs_ap, wout_ap, tag="at")
            xlb = [b.sb(f"xlb{g}", [128, T], BF16, scope=sc1)
                   for g in range(8)]
            with contextlib.ExitStack() as sc2:
                xlf = [b.sb(f"xlf{g}", [128, T], scope=sc2) for g in range(8)]
                phase_layernorm(b, resid, T, out_bf=xlb, out_f32=xlf,
                                tag="ln3")
                phase_gating(b, xlf, T, gw_ap, ident, cwT, tag="gt2")
            phase_moe(b, xlb, resid, T, w1_ap, w2_ap, cwT, scale_out=0.5,
                      tag="mo2")
            with contextlib.ExitStack() as sc3:
                xlf2 = [b.sb(f"xl4{g}", [128, T], scope=sc3) for g in range(8)]
                phase_layernorm(b, resid, T, out_f32=xlf2, tag="ln4")
                phase_partials(b, xlf2, T, o_part, tag="pp")
            phase_store_dmaj(b, resid, o_resid3, T, tag="str3")
    return b


def phase_partials(b, xlf, T, o_part, tag="pp"):
    nc = b.nc
    sc = contextlib.ExitStack()
    part = b.sb(tag + "part", [128, 8], scope=sc)
    sv, sd = b.sem("v"), b.sem("d")
    v0, d0 = sv.n, sd.n
    with nc.Block() as blk:
        @blk.vector
        def _(v):
            for g in range(8):
                inst = v.reduce_sum(part[:, g:g + 1], xlf[g][:, 0:T], axis=AXX)
            inst.then_inc(sv.sem, 1)
        sv.n = v0 + 1

        @blk.sync
        def _(s):
            s.wait_ge(sv.sem, v0 + 1)
            s.dma_start(out=o_part, in_=part[:]).then_inc(sd.sem, 16)
            s.wait_ge(sd.sem, d0 + 16)
        sd.n = d0 + 16
    sc.close()



def phase_gf(b, parts_ap, bsel_ap, gcw, ar_ap, c0_t, c1_t, nbtok, tag="gf"):
    nc = b.nc
    sc = contextlib.ExitStack()
    _sb = lambda n, sh, dt=F32: b.sb(tag + n, sh, dt, scope=sc)
    _ps = lambda n, sh: b.ps(tag + n, sh, scope=sc)
    gc_w1_ap, gc_b1_ap, gc_w2_ap, gc_b2_ap, gc_w3_ap, gc_b3_ap = gcw
    parts = _sb("parts", [128, 64])
    bsel = _sb("bsel", [128, 8])
    mean = _sb("mean", [128, 8])
    tmp = _sb("tmp", [128, 8])
    gw1 = _sb("gw1", [128, 8 * 512])
    gw2 = _sb("gw2", [128, 4 * 256])
    gw3 = _sb("gw3", [128, 2])
    gb1 = _sb("gb1", [128, 4])
    gb2 = _sb("gb2", [128, 2])
    b3ar = _sb("b3ar", [1, 8])
    h1 = _sb("h1", [128, 4])
    h2 = _sb("h2", [128, 2])
    sc1 = _sb("sc1", [1, 8])
    ph = _ps("ph", [128, 8])
    p3 = _ps("p3", [1, 8])
    sd, sv, ss, stt = b.sem("d"), b.sem("v"), b.sem("s"), b.sem("t")
    d0, v0, s0, t0 = sd.n, sv.n, ss.n, stt.n

    with nc.Block() as blk:
        @blk.sync
        def _(s):
            for j in range(8):
                s.dma_start(out=parts[:, 8 * j:8 * (j + 1)], in_=parts_ap[j]) \
                    .then_inc(sd.sem, 16)
            s.dma_start(out=bsel[:], in_=bsel_ap).then_inc(sd.sem, 16)
            s.dma_start(out=gb1[:], in_=gc_b1_ap).then_inc(sd.sem, 16)
            s.dma_start(out=gb2[:], in_=gc_b2_ap).then_inc(sd.sem, 16)
            s.dma_start(out=b3ar[0:1, 0:1], in_=gc_b3_ap).then_inc(sd.sem, 16)
            s.dma_start(out=b3ar[0:1, 1:2], in_=ar_ap).then_inc(sd.sem, 16)
            for dc in range(8):
                s.dma_start(out=gw1[:, 512 * dc:512 * (dc + 1)],
                            in_=gc_w1_ap[128 * dc:128 * (dc + 1), :]) \
                    .then_inc(sd.sem, 16)
            for dc in range(4):
                s.dma_start(out=gw2[:, 256 * dc:256 * (dc + 1)],
                            in_=gc_w2_ap[128 * dc:128 * (dc + 1), :]) \
                    .then_inc(sd.sem, 16)
            for dc in range(2):
                s.dma_start(out=gw3[:, dc:dc + 1],
                            in_=gc_w3_ap[128 * dc:128 * (dc + 1), :]) \
                    .then_inc(sd.sem, 16)
            s.wait_ge(sd.sem, d0 + 16 * 27)
        sd.n = d0 + 16 * 27

        @blk.vector
        def _(v):
            vk = [v0]

            def step(inst):
                inst.then_inc(sv.sem, 1)
                vk[0] += 1
                v.wait_ge(sv.sem, vk[0])

            v.wait_ge(sd.sem, d0 + 16 * 27)
            step(v.tensor_scalar_mul(mean[:], parts[:, 0:8], bsel[:, 0:1]))
            for j in range(1, 8):
                step(v.tensor_scalar_mul(tmp[:], parts[:, 8 * j:8 * (j + 1)],
                                         bsel[:, j:j + 1]))
                step(v.tensor_add(mean[:], mean[:], tmp[:]))
            step(v.tensor_scalar_mul(mean[:], mean[:], 1.0 / nbtok))
            step(v.tensor_scalar_mul(c1_t[0:1, 0:1], b3ar[0:1, 1:2], 0.6))
            step(v.tensor_scalar_mul(sc1[0:1, 0:1], b3ar[0:1, 1:2], 2.0))
            v.wait_ge(ss.sem, s0 + 4)
            step(v.tensor_scalar(out=tmp[0:1, 0:1], in0=b3ar[0:1, 2:3],
                                 scalar1=0.7, scalar2=-0.5, op0=ALU.mult,
                                 op1=ALU.add))
            step(v.tensor_mul(tmp[0:1, 1:2], tmp[0:1, 0:1], sc1[0:1, 0:1]))
            step(v.tensor_scalar_add(c0_t[0:1, 0:1], tmp[0:1, 1:2], 0.1))
            step(v.tensor_copy(c0_t[0:1, 1:2], b3ar[0:1, 2:3]))
            step(v.tensor_copy(c0_t[0:1, 2:3], mean[0:1, 0:1]))
            step(v.tensor_copy(c0_t[0:1, 3:4], h1[0:1, 0:1]))
        NV = 24
        sv.n = v0 + NV

        @blk.tensor
        def _(t):
            t.wait_ge(sv.sem, v0 + 16)
            for hc in range(4):
                for dc in range(8):
                    inst = t.matmul(ph[:, 2 * hc:2 * hc + 1],
                                    gw1[:, 512 * dc + 128 * hc:
                                        512 * dc + 128 * (hc + 1)],
                                    mean[:, dc:dc + 1], start=(dc == 0),
                                    stop=(dc == 7))
            inst.then_inc(stt.sem, 1)
            t.wait_ge(ss.sem, s0 + 1)
            for hc in range(2):
                for dc in range(4):
                    inst = t.matmul(ph[:, 2 * hc:2 * hc + 1],
                                    gw2[:, 256 * dc + 128 * hc:
                                        256 * dc + 128 * (hc + 1)],
                                    h1[:, dc:dc + 1], start=(dc == 0),
                                    stop=(dc == 3))
            inst.then_inc(stt.sem, 1)
            t.wait_ge(ss.sem, s0 + 2)
            for dc in range(2):
                inst = t.matmul(p3[0:1, 0:1], gw3[:, dc:dc + 1],
                                h2[:, dc:dc + 1], start=(dc == 0),
                                stop=(dc == 1))
            inst.then_inc(stt.sem, 1)
        stt.n = t0 + 3

        @blk.scalar
        def _(s):
            sk = [s0]

            def sstep(inst):
                inst.then_inc(ss.sem, 1)
                sk[0] += 1
                s.wait_ge(ss.sem, sk[0])

            s.wait_ge(stt.sem, t0 + 1)
            for hc in range(4):
                inst = s.activation(out=h1[:, hc:hc + 1],
                                    in_=ph[:, 2 * hc:2 * hc + 1],
                                    func=AF.Gelu, bias=gb1[:, hc:hc + 1])
            sstep(inst)
            s.wait_ge(stt.sem, t0 + 2)
            for hc in range(2):
                inst = s.activation(out=h2[:, hc:hc + 1],
                                    in_=ph[:, 2 * hc:2 * hc + 1],
                                    func=AF.Gelu, bias=gb2[:, hc:hc + 1])
            sstep(inst)
            s.wait_ge(stt.sem, t0 + 3)
            sstep(s.activation(out=b3ar[0:1, 3:4], in_=p3[0:1, 0:1],
                               func=AF.Copy))
            sstep(s.activation(out=b3ar[0:1, 2:3], in_=b3ar[0:1, 3:4],
                               func=AF.Sigmoid, bias=b3ar[0:1, 0:1]))
        ss.n = s0 + 4

    sc.close()



TW_SRC = [4, 5, 6, 7, 0, 1, 2, 3]
TW_SGN = [1.0, 1.0, -1.0, -1.0, -1.0, -1.0, 1.0, 1.0]


def phase_coupling(b, outA, outbf, resid, Tc, cnw, c0_t, c1_t,
                   tag="cp", CW=512):
    nc = b.nc
    sc = contextlib.ExitStack()
    _sb = lambda n, sh, dt=F32: b.sb(tag + n, sh, dt, scope=sc)
    _ps = lambda n, sh: b.ps(tag + n, sh, scope=sc)
    (w1_ap, b1_ap, w2_ap, b2_ap, w3_ap, b3_ap, w4_ap, b4_ap) = cnw
    cw1 = [_sb(f"w1_{dc}", [128, 1024], BF16) for dc in range(8)]
    cw2 = [_sb(f"w2_{dc}", [128, 512], BF16) for dc in range(8)]
    cw3 = [_sb(f"w3_{dc}", [128, 256], BF16) for dc in range(4)]
    cw4 = [_sb(f"w4_{dc}", [128, 1], BF16) for dc in range(2)]
    cb1 = _sb("cb1", [128, 8])
    cb2 = _sb("cb2", [128, 4])
    cb3 = _sb("cb3", [128, 2])
    cb4 = _sb("cb4", [1, 1])
    h1b = [_sb(f"h1b{k}", [128, CW], BF16) for k in range(8)]
    h2b = [_sb(f"h2b{k}", [128, CW], BF16) for k in range(4)]
    h3b = [_sb(f"h3b{k}", [128, CW], BF16) for k in range(2)]
    tfrow = _sb("tfrow", [1, CW])
    crow = _sb("crow", [1, CW])
    cb_t = _sb("cb_t", [128, CW])
    tmp = _sb("tmp", [128, CW])
    tmp2 = _sb("tmp2", [128, CW])
    onesr = _sb("onesr", [1, 128])
    ph = [_ps(f"ph{k}", [128, CW]) for k in range(2)]
    p4 = _ps("p4", [1, CW])
    pcb = _ps("pcb", [128, CW])

    sd, sv, ss, stt = b.sem("d"), b.sem("v"), b.sem("s"), b.sem("t")
    sg_ = b.sem("g0")
    d0, v0, s0, t0, g0 = sd.n, sv.n, ss.n, stt.n, sg_.n
    TQ = fsplit(Tc, CW)
    NJ = len(TQ)

    NTq, NSq, NVq = 16, 16, 25

    def t_q(c, j):
        return t0 + NTq * (NJ * c + j)

    def s_q(c, j):
        return s0 + NSq * (NJ * c + j)

    def v_q(c, j):
        return v0 + 9 + NVq * (NJ * c + j)

    with nc.Block() as blk:
        @blk.gpsimd
        def _(g):
            for dc in range(8):
                g.dma_start(out=cw1[dc][:],
                            in_=w1_ap[128 * dc:128 * (dc + 1), :]) \
                    .then_inc(sg_.sem, 16)
            for dc in range(8):
                g.dma_start(out=cw2[dc][:],
                            in_=w2_ap[128 * dc:128 * (dc + 1), :]) \
                    .then_inc(sg_.sem, 16)
            for dc in range(4):
                g.dma_start(out=cw3[dc][:],
                            in_=w3_ap[128 * dc:128 * (dc + 1), :]) \
                    .then_inc(sg_.sem, 16)
            for dc in range(2):
                g.dma_start(out=cw4[dc][:],
                            in_=w4_ap[128 * dc:128 * (dc + 1), :]) \
                    .then_inc(sg_.sem, 16)
            g.wait_ge(sg_.sem, g0 + 16 * 22)
        sg_.n = g0 + 16 * 22

        @blk.sync
        def _(s):
            s.dma_start(out=cb1[:], in_=b1_ap).then_inc(sd.sem, 16)
            s.dma_start(out=cb2[:], in_=b2_ap).then_inc(sd.sem, 16)
            s.dma_start(out=cb3[:], in_=b3_ap).then_inc(sd.sem, 16)
            s.dma_start(out=cb4[:], in_=b4_ap).then_inc(sd.sem, 16)
            s.wait_ge(sd.sem, d0 + 16 * 4)
        sd.n = d0 + 16 * 4

        @blk.vector
        def _(v):
            vk = [v0]

            def step(inst):
                inst.then_inc(sv.sem, 1)
                vk[0] += 1
                v.wait_ge(sv.sem, vk[0])

            v.memset(onesr[:], 1.0).then_inc(sv.sem, 1)
            vk[0] += 1
            for g_ in range(8):
                inst = v.tensor_copy(outbf[g_][:, 0:Tc], outA[g_][:, 0:Tc])
                inst.then_inc(sv.sem, 1)
                vk[0] += 1
            v.wait_ge(sv.sem, vk[0])
            for c in range(3):
                for j, (off, sz) in enumerate(TQ):
                    v.wait_ge(ss.sem, s_q(c, j) + 15)
                    step(v.tensor_scalar(out=crow[0:1, 0:sz],
                                         in0=tfrow[0:1, 0:sz],
                                         scalar1=c1_t[0:1, 0:1],
                                         scalar2=c0_t[0:1, 0:1],
                                         op0=ALU.mult, op1=ALU.add))
                    v.wait_ge(ss.sem, s_q(c, j) + 16)
                    for pg in range(4):
                        step(v.tensor_mul(tmp[:, 0:sz], cb_t[:, 0:sz],
                                          outA[pg + 4][:, off:off + sz]))
                        step(v.tensor_mul(tmp2[:, 0:sz], cb_t[:, 0:sz],
                                          outA[pg][:, off:off + sz]))
                        step(v.scalar_tensor_tensor(
                            out=outA[pg][:, off:off + sz], in0=tmp[:, 0:sz],
                            scalar=TW_SGN[pg], in1=outA[pg][:, off:off + sz],
                            op0=ALU.mult, op1=ALU.add))
                        step(v.tensor_copy(outbf[pg][:, off:off + sz],
                                           outA[pg][:, off:off + sz]))
                        step(v.scalar_tensor_tensor(
                            out=outA[pg + 4][:, off:off + sz],
                            in0=tmp2[:, 0:sz], scalar=TW_SGN[pg + 4],
                            in1=outA[pg + 4][:, off:off + sz],
                            op0=ALU.mult, op1=ALU.add))
                        step(v.tensor_copy(outbf[pg + 4][:, off:off + sz],
                                           outA[pg + 4][:, off:off + sz]))
            for g_ in range(8):
                step(v.scalar_tensor_tensor(
                    out=resid[g_][:, 0:Tc], in0=outA[g_][:, 0:Tc], scalar=0.5,
                    in1=resid[g_][:, 0:Tc], op0=ALU.mult, op1=ALU.add))
        sv.n = v0 + 9 + NVq * 3 * NJ + 8

        @blk.tensor
        def _(t):
            t.wait_ge(sg_.sem, g0 + 16 * 22)
            t.wait_ge(sv.sem, v0 + 9)
            for c in range(3):
                for j, (off, sz) in enumerate(TQ):
                    tb = t_q(c, j)
                    if c > 0 or j > 0:
                        pass
                    for hc in range(8):
                        if c > 0 or j > 0:
                            pc_, pj = (c, j - 1) if j > 0 else (c - 1, NJ - 1)
                            t.wait_ge(ss.sem, s_q(pc_, pj) + hc % 2 + 15)
                        if c > 0:
                            t.wait_ge(sv.sem, v_q(c - 1, j) + NVq)
                        for dc in range(8):
                            inst = t.matmul(ph[hc % 2][:, 0:sz],
                                            cw1[dc][:, 128 * hc:128 * (hc + 1)],
                                            outbf[dc][:, off:off + sz],
                                            start=(dc == 0), stop=(dc == 7))
                        inst.then_inc(stt.sem, 1)
                        t.wait_ge(ss.sem, s_q(c, j) + hc + 1)
                    for hc in range(4):
                        for dc in range(8):
                            inst = t.matmul(ph[hc % 2][:, 0:sz],
                                            cw2[dc][:, 128 * hc:128 * (hc + 1)],
                                            h1b[dc][:, 0:sz],
                                            start=(dc == 0), stop=(dc == 7))
                        inst.then_inc(stt.sem, 1)
                        t.wait_ge(ss.sem, s_q(c, j) + 8 + hc + 1)
                    for hc in range(2):
                        for dc in range(4):
                            inst = t.matmul(ph[hc % 2][:, 0:sz],
                                            cw3[dc][:, 128 * hc:128 * (hc + 1)],
                                            h2b[dc][:, 0:sz],
                                            start=(dc == 0), stop=(dc == 3))
                        inst.then_inc(stt.sem, 1)
                        t.wait_ge(ss.sem, s_q(c, j) + 12 + hc + 1)
                    for dc in range(2):
                        inst = t.matmul(p4[0:1, 0:sz], cw4[dc][:],
                                        h3b[dc][:, 0:sz], start=(dc == 0),
                                        stop=(dc == 1))
                    inst.then_inc(stt.sem, 1)
                    t.wait_ge(ss.sem, s_q(c, j) + 15)
                    t.wait_ge(sv.sem, v_q(c, j) + 1)
                    t.matmul(pcb[:, 0:sz], onesr[:], crow[0:1, 0:sz],
                             start=True, stop=True).then_inc(stt.sem, 1)
                    t.wait_ge(ss.sem, s_q(c, j) + 16)
        stt.n = t0 + NTq * 3 * NJ

        @blk.scalar
        def _(s):
            s.wait_ge(sd.sem, d0 + 16 * 4)
            for c in range(3):
                for j, (off, sz) in enumerate(TQ):
                    for hc in range(8):
                        s.wait_ge(stt.sem, t_q(c, j) + hc + 1)
                        s.activation(out=h1b[hc][:, 0:sz],
                                     in_=ph[hc % 2][:, 0:sz], func=AF.Gelu,
                                     bias=cb1[:, hc:hc + 1]).then_inc(ss.sem, 1)
                    for hc in range(4):
                        s.wait_ge(stt.sem, t_q(c, j) + 8 + hc + 1)
                        s.activation(out=h2b[hc][:, 0:sz],
                                     in_=ph[hc % 2][:, 0:sz], func=AF.Gelu,
                                     bias=cb2[:, hc:hc + 1]).then_inc(ss.sem, 1)
                    for hc in range(2):
                        s.wait_ge(stt.sem, t_q(c, j) + 12 + hc + 1)
                        s.activation(out=h3b[hc][:, 0:sz],
                                     in_=ph[hc % 2][:, 0:sz], func=AF.Gelu,
                                     bias=cb3[:, hc:hc + 1]).then_inc(ss.sem, 1)
                    s.wait_ge(stt.sem, t_q(c, j) + 15)
                    s.activation(out=tfrow[0:1, 0:sz], in_=p4[0:1, 0:sz],
                                 func=AF.Sigmoid, bias=cb4[0:1, 0:1]) \
                        .then_inc(ss.sem, 1)
                    s.wait_ge(stt.sem, t_q(c, j) + 16)
                    s.activation(out=cb_t[:, 0:sz], in_=pcb[:, 0:sz],
                                 func=AF.Copy).then_inc(ss.sem, 1)
        ss.n = s0 + NSq * 3 * NJ

    sc.close()



def build_launch4(T, nbtok=8192.0):
    b = B()
    r5_ap = b.din("resid5", [1024, T])
    parts_ap = b.din("parts", [8, 128, 8])
    bsel_ap = b.din("bsel", [128, 8])
    gw1_ap = b.din("gc_w1", [1024, 512])
    gb1_ap = b.din("gc_b1_dm", [128, 4])
    gw2_ap = b.din("gc_w2", [512, 256])
    gb2_ap = b.din("gc_b2_dm", [128, 2])
    gw3_ap = b.din("gc_w3", [256, 1])
    gb3_ap = b.din("gc_b3", [1, 1])
    ar_ap = b.din("ar", [1, 1])
    cw1_ap = b.din("cn_w1", [1024, 1024])
    cb1_ap = b.din("cn_b1_dm", [128, 8])
    cw2_ap = b.din("cn_w2", [1024, 512])
    cb2_ap = b.din("cn_b2_dm", [128, 4])
    cw3_ap = b.din("cn_w3", [512, 256])
    cb3_ap = b.din("cn_b3_dm", [128, 2])
    cw4_ap = b.din("cn_w4", [256, 1])
    cb4_ap = b.din("cn_b4", [1, 1])
    o_final = b.dout("o_final", [1024, T])
    o_dbg4 = b.dout("o_dbg4", [1, 8])

    with b.es:
        b.init_consts([EPS, 1.0 / D, -0.5, -1.0])
        c0_t = b.sb("c0t", [1, 4])
        c1_t = b.sb("c1t", [1, 4])
        phase_gf(b, parts_ap, bsel_ap,
                 (gw1_ap, gb1_ap, gw2_ap, gb2_ap, gw3_ap, gb3_ap), ar_ap,
                 c0_t, c1_t, nbtok, tag="gf")
        resid = [b.sb(f"res{g}", [128, T]) for g in range(8)]
        outA = [b.sb(f"oA{g}", [128, T]) for g in range(8)]
        outbf = [b.sb(f"obf{g}", [128, T], BF16) for g in range(8)]
        phase_load_dmaj(b, r5_ap, resid, T, tag="ldr")
        phase_layernorm(b, resid, T, out_f32=outA, tag="ln6")
        phase_coupling(b, outA, outbf, resid, T,
                       (cw1_ap, cb1_ap, cw2_ap, cb2_ap, cw3_ap, cb3_ap,
                        cw4_ap, cb4_ap), c0_t, c1_t, tag="cp", CW=256)
        phase_store_dmaj(b, resid, o_final, T, tag="stf")
        sdbg = b.sem("d")
        with b.nc.Block() as blk:
            @blk.vector
            def _(v):
                v.tensor_copy(c0_t[0:1, 0:4], c0_t[0:1, 0:4]) if False else None
            @blk.sync
            def _(s):
                s.dma_start(out=o_dbg4[0:1, 0:4], in_=c0_t[0:1, 0:4]) \
                    .then_inc(sdbg.sem, 16)
                s.dma_start(out=o_dbg4[0:1, 4:8], in_=c1_t[0:1, 0:4]) \
                    .then_inc(sdbg.sem, 16)
                s.wait_ge(sdbg.sem, sdbg.n + 32)
        sdbg.n += 32
    return b


def host_prep_l1(inp):
    w = {}
    f32 = np.float32
    w["gc_w1"] = np.ascontiguousarray(inp["L1_ffn_gc_w1"], dtype=f32)
    w["gc_b1_dm"] = np.ascontiguousarray(
        np.asarray(inp["L1_ffn_gc_b1"], dtype=f32).reshape(4, 128).T)
    w["gc_w2"] = np.ascontiguousarray(inp["L1_ffn_gc_w2"], dtype=f32)
    w["gc_b2_dm"] = np.ascontiguousarray(
        np.asarray(inp["L1_ffn_gc_b2"], dtype=f32).reshape(2, 128).T)
    w["gc_w3"] = np.ascontiguousarray(inp["L1_ffn_gc_w3"], dtype=f32)
    w["gc_b3"] = np.asarray(inp["L1_ffn_gc_b3"], dtype=f32).reshape(1, 1)
    w["ar"] = np.asarray(inp["L1_ffn_adaptive_range"], dtype=f32).reshape(1, 1)
    w["cn_w1"] = np.ascontiguousarray(inp["L1_ffn_cn_w1"], dtype=f32)
    w["cn_b1_dm"] = np.ascontiguousarray(
        np.asarray(inp["L1_ffn_cn_b1"], dtype=f32).reshape(8, 128).T)
    w["cn_w2"] = np.ascontiguousarray(inp["L1_ffn_cn_w2"], dtype=f32)
    w["cn_b2_dm"] = np.ascontiguousarray(
        np.asarray(inp["L1_ffn_cn_b2"], dtype=f32).reshape(4, 128).T)
    w["cn_w3"] = np.ascontiguousarray(inp["L1_ffn_cn_w3"], dtype=f32)
    w["cn_b3_dm"] = np.ascontiguousarray(
        np.asarray(inp["L1_ffn_cn_b3"], dtype=f32).reshape(2, 128).T)
    w["cn_w4"] = np.ascontiguousarray(inp["L1_ffn_cn_w4"], dtype=f32)
    w["cn_b4"] = np.asarray(inp["L1_ffn_cn_b4"], dtype=f32).reshape(1, 1)
    w["xp_mu_dm"] = np.ascontiguousarray(
        np.asarray(inp["L1_attn_xp_mu"], dtype=f32).reshape(8, 128).T)
    w["xp_w1T"] = np.ascontiguousarray(np.asarray(inp["L1_attn_xp_W1"]).T,
                                       dtype=f32)
    w["xp_w2r"] = np.ascontiguousarray(
        np.asarray(inp["L1_attn_xp_W2"], dtype=f32).reshape(1024, 5, 32)
        .transpose(1, 2, 0))
    w["x_bias_dm"] = np.ascontiguousarray(
        np.asarray(inp["L1_attn_x_bias"], dtype=f32).reshape(5, 8, 128)
        .transpose(0, 2, 1))
    w["rwT"] = np.ascontiguousarray(np.asarray(inp["L1_attn_r_W"]).T, dtype=f32)
    w["kwT"] = np.ascontiguousarray(np.asarray(inp["L1_attn_k_W"]).T, dtype=f32)
    w["vwT"] = np.ascontiguousarray(np.asarray(inp["L1_attn_v_W"]).T, dtype=f32)
    w["waT"] = np.ascontiguousarray(np.asarray(inp["L1_attn_w_A"]).T, dtype=f32)
    w["wbT"] = np.ascontiguousarray(np.asarray(inp["L1_attn_w_B"]).T, dtype=f32)
    w["w_b_dm"] = np.ascontiguousarray(
        np.asarray(inp["L1_attn_w_b"], dtype=f32).reshape(4, 128).T)
    w["gaT"] = np.ascontiguousarray(np.asarray(inp["L1_attn_g_A"]).T, dtype=f32)
    w["gbT"] = np.ascontiguousarray(np.asarray(inp["L1_attn_g_B"]).T, dtype=f32)
    w["g_b_row"] = np.asarray(inp["L1_attn_g_b"], dtype=f32).reshape(1, 1024)
    w["bonus_dm"] = np.ascontiguousarray(np.asarray(inp["L1_attn_bonus"]).T,
                                         dtype=f32)
    w["gn_w_row"] = np.asarray(inp["L1_attn_gn_w"], dtype=f32).reshape(1, 1024)
    w["gn_b_row"] = np.asarray(inp["L1_attn_gn_b"], dtype=f32).reshape(1, 1024)
    w["owT"] = np.ascontiguousarray(np.asarray(inp["L1_attn_o_W"]).T, dtype=f32)
    m = np.zeros((128, 128), dtype=f32)
    for sc_ in range(2):
        for t in range(64):
            for tau in range(64):
                if tau < t:
                    m[64 * sc_ + t, 64 * sc_ + tau] = 1.0
    w["m_low"] = np.ascontiguousarray(m.T)
    w["m_low_t"] = np.ascontiguousarray(m)
    return w



class Prog:
    def __init__(self, b, tag):
        self.b = b
        self.items = []
        self.tick = b.sem(tag + "_tick")

    def op(self, eng, fn, inc=1):
        self.items.append((eng, fn, inc))

    def v(self, fn):
        self.op("vector", fn)

    def s(self, fn):
        self.op("scalar", fn)

    def t(self, fn):
        self.op("tensor", fn)

    def d(self, fn):
        self.op("sync", fn, inc=16)

    def g(self, fn):
        self.op("gpsimd", fn, inc=16)

    def emit(self):
        nc = self.b.nc
        base = self.tick.n
        targets = []
        c = base
        for (_, _, inc) in self.items:
            targets.append(c)
            c += inc
        total = c
        items = self.items
        tick = self.tick

        with nc.Block() as blk:
            def body(eng_name):
                def run(eng):
                    for k, (en, fn, inc) in enumerate(items):
                        if en != eng_name:
                            continue
                        if targets[k] > base:
                            eng.wait_ge(tick.sem, targets[k])
                        inst = fn(eng)
                        inst.then_inc(tick.sem, inc)
                    if eng_name == "sync":
                        eng.wait_ge(tick.sem, total)
                return run

            blk.vector(body("vector"))
            blk.scalar(body("scalar"))
            blk.tensor(body("tensor"))
            blk.sync(body("sync"))
            blk.gpsimd(body("gpsimd"))
        self.tick.n = total



def phase_rwkv_front(b, x4, T, Tc, w_aps, hm_ap, out_hbm, tag="rk"):
    nc = b.nc
    sc = contextlib.ExitStack()
    _sb = lambda n, sh, dt=F32: b.sb(tag + n, sh, dt, scope=sc)
    _ps = lambda n, sh: b.ps(tag + n, sh, scope=sc)
    P = Prog(b, tag)
    TQ = fsplit(T, 512)
    NTJ = T // 128
    (r_hbm, k_hbm, w_hbm, v_hbm, g_hbm) = out_hbm

    mixbf = [_sb(f"mx{g}", [128, T], BF16) for g in range(8)]
    mu_t = _sb("mu", [128, 8])
    xb_t = _sb("xb", [128, 40])
    hm = _sb("hm", [128, 1])
    xh = _sb("xh", [128, 1])
    xw1b = [_sb(f"xw1{dc}", [128, 160], BF16) for dc in range(8)]
    xw2b = _sb("xw2", [64, 1024], BF16)
    xlA = _sb("xlA", [128, T], BF16)
    xlB = _sb("xlB", [32, T], BF16)
    xlC = _sb("xlC", [64, T], BF16)
    x5t = _sb("x5t", [128, 512])
    tmp5a = _sb("tm5a", [128, 512])
    tmp5 = _sb("tmp5", [128, 512])
    pwb = [_sb(f"pw{dc}", [128, 1024], BF16) for dc in range(8)]
    wbb = _sb("wbb", [64, 512], BF16)
    w_bt = _sb("w_bt", [128, 4])
    rkstg = [_sb(f"rk{i}", [128, T], BF16) for i in range(2)]
    wstg = [_sb(f"ws{i}", [128, T]) for i in range(2)]
    vstg = [_sb(f"vs{i}", [128, 1024], BF16) for i in range(2)]
    wgstg = _sb("wg", [64, T], BF16)
    pp1 = _ps("pp1", [128, 512])
    pp2 = _ps("pp2", [64, 512])

    (xw1_ap, xw2_ap, xbias_ap, mu_ap, rw_ap, kw_ap, vw_ap, wa_ap, wb_ap,
     wbias_ap, ga_ap, gb_ap, gbrow_ap, bonus_ap, gnw_ap, gnb_ap, ow_ap,
     mlow_ap, ident) = w_aps

    for dc in range(8):
        P.g(lambda g, dc=dc: g.dma_start(out=xw1b[dc][:],
                                         in_=xw1_ap[128 * dc:128 * (dc + 1), :]))
    for i in range(5):
        P.d(lambda s, i=i: s.dma_start(out=xb_t[:, 8 * i:8 * (i + 1)],
                                       in_=xbias_ap[i]))
    P.d(lambda s: s.dma_start(out=mu_t[:], in_=mu_ap))
    P.d(lambda s: s.dma_start(out=w_bt[:], in_=wbias_ap))
    P.d(lambda s: s.dma_start(out=hm[:], in_=hm_ap))

    def delta_chunk(g_, off, sz):
        P.v(lambda v, g_=g_, off=off, sz=sz: v.tensor_sub(
            tmp5a[:, 0:sz], x4[g_][:, off:off + sz],
            x4[g_][:, 1 + off:1 + off + sz]))
        if off == 0:
            P.v(lambda v, g_=g_: v.tensor_scalar(
                out=xh[:], in0=x4[g_][:, 0:1], scalar1=hm[:, 0:1],
                scalar2=None, op0=ALU.mult))
            P.v(lambda v, g_=g_: v.tensor_sub(tmp5a[:, 0:1], xh[:],
                                              x4[g_][:, 1:2]))

    for g_ in range(8):
        for (off, sz) in TQ:
            delta_chunk(g_, off, sz)
            P.v(lambda v, g_=g_, off=off, sz=sz: v.scalar_tensor_tensor(
                out=mixbf[g_][:, off:off + sz], in0=tmp5a[:, 0:sz],
                scalar=mu_t[:, g_:g_ + 1], in1=x4[g_][:, 1 + off:1 + off + sz],
                op0=ALU.mult, op1=ALU.add))

    for (off, sz) in TQ:
        def mmA(t, off=off, sz=sz):
            for dc in range(8):
                inst = t.matmul(pp1[:, 0:sz], xw1b[dc][:, 0:128],
                                mixbf[dc][:, off:off + sz], start=(dc == 0),
                                stop=(dc == 7))
            return inst
        P.t(mmA)
        P.s(lambda s, off=off, sz=sz: s.activation(
            out=xlA[:, off:off + sz], in_=pp1[:, 0:sz], func=AF.Tanh))
        def mmB(t, off=off, sz=sz):
            for dc in range(8):
                inst = t.matmul(pp2[0:32, 0:sz], xw1b[dc][:, 128:160],
                                mixbf[dc][:, off:off + sz], start=(dc == 0),
                                stop=(dc == 7))
            return inst
        P.t(mmB)
        P.s(lambda s, off=off, sz=sz: s.activation(
            out=xlB[:, off:off + sz], in_=pp2[0:32, 0:sz], func=AF.Tanh))
    P.v(lambda v: v.tensor_copy(xlC[:, 0:T], xlA[64:128, 0:T]))

    def xl_slice(i):
        return [xlA[0:32, :], xlA[32:64, :], xlC[0:32, :], xlC[32:64, :],
                xlB[0:32, :]][i]

    def build_mix(i):
        P.g(lambda g, i=i: g.dma_start(out=xw2b[0:32, :], in_=xw2_ap[i]))
        P.g(lambda g, i=i: g.dma_start(out=xw2b[32:64, :], in_=xw2_ap[i]))
        wrow = slice(32, 64) if i in (1, 3) else slice(0, 32)
        for hc in range(8):
            for (off, sz) in TQ:
                P.t(lambda t, i=i, hc=hc, off=off, sz=sz, wrow=wrow: t.matmul(
                    pp1[:, 0:sz], xw2b[wrow, 128 * hc:128 * (hc + 1)],
                    xl_slice(i)[:, off:off + sz], start=True, stop=True))
                P.v(lambda v, i=i, hc=hc, off=off, sz=sz: v.tensor_scalar(
                    out=x5t[:, 0:sz], in0=pp1[:, 0:sz],
                    scalar1=xb_t[:, 8 * i + hc:8 * i + hc + 1],
                    scalar2=None, op0=ALU.add))
                delta_chunk(hc, off, sz)
                P.v(lambda v, off=off, sz=sz: v.tensor_mul(
                    tmp5[:, 0:sz], tmp5a[:, 0:sz], x5t[:, 0:sz]))
                P.v(lambda v, hc=hc, off=off, sz=sz: v.tensor_add(
                    mixbf[hc][:, off:off + sz], tmp5[:, 0:sz],
                    x4[hc][:, 1 + off:1 + off + sz]))

    def load_proj(w_ap, width):
        for dc in range(8):
            P.g(lambda g, dc=dc, w_ap=w_ap, width=width: g.dma_start(
                out=pwb[dc][:, 0:width],
                in_=w_ap[128 * dc:128 * (dc + 1), 0:width]))

    def proj_kmaj(dst_hbm, width):
        for kc in range(4):
            stg = rkstg[kc % 2]
            for (off, sz) in TQ:
                def mm(t, kc=kc, off=off, sz=sz):
                    for dc in range(8):
                        inst = t.matmul(pp1[:, 0:sz],
                                        pwb[dc][:, 128 * kc:128 * (kc + 1)],
                                        mixbf[dc][:, off:off + sz],
                                        start=(dc == 0), stop=(dc == 7))
                    return inst
                P.t(mm)
                P.s(lambda s, stg=stg, off=off, sz=sz: s.activation(
                    out=stg[:, off:off + sz], in_=pp1[:, 0:sz], func=AF.Copy))
            P.d(lambda s, kc=kc, stg=stg, dst_hbm=dst_hbm: s.dma_start(
                out=dst_hbm[128 * kc:128 * (kc + 1), :], in_=stg[:]))

    build_mix(0)
    load_proj(rw_ap, 512)
    proj_kmaj(r_hbm, 512)
    build_mix(1)
    load_proj(wa_ap, 64)
    for (off, sz) in TQ:
        def mmw(t, off=off, sz=sz):
            for dc in range(8):
                inst = t.matmul(pp2[:, 0:sz], pwb[dc][:, 0:64],
                                mixbf[dc][:, off:off + sz], start=(dc == 0),
                                stop=(dc == 7))
            return inst
        P.t(mmw)
        P.s(lambda s, off=off, sz=sz: s.activation(
            out=wgstg[:, off:off + sz], in_=pp2[:, 0:sz], func=AF.Tanh))
    P.g(lambda g: g.dma_start(out=wbb[:], in_=wb_ap))
    for kc in range(4):
        stg = wstg[kc % 2]
        for (off, sz) in TQ:
            P.t(lambda t, kc=kc, off=off, sz=sz: t.matmul(
                pp1[:, 0:sz], wbb[:, 128 * kc:128 * (kc + 1)],
                wgstg[:, off:off + sz], start=True, stop=True))
            P.v(lambda v, kc=kc, stg=stg, off=off, sz=sz: v.tensor_scalar(
                out=stg[:, off:off + sz], in0=pp1[:, 0:sz],
                scalar1=w_bt[:, kc:kc + 1], scalar2=None, op0=ALU.add))
        P.d(lambda s, kc=kc, stg=stg: s.dma_start(
            out=w_hbm[128 * kc:128 * (kc + 1), :], in_=stg[:]))
    build_mix(2)
    load_proj(kw_ap, 512)
    proj_kmaj(k_hbm, 512)
    build_mix(3)
    load_proj(vw_ap, 1024)
    for j in range(NTJ):
        stg = vstg[j % 2]
        for hf in range(2):
            def mmv(t, j=j, hf=hf):
                for dc in range(8):
                    inst = t.matmul(pp1[:, 0:512],
                                    mixbf[dc][:, 128 * j:128 * (j + 1)],
                                    pwb[dc][:, 512 * hf:512 * (hf + 1)],
                                    start=(dc == 0), stop=(dc == 7))
                return inst
            P.t(mmv)
            P.s(lambda s, stg=stg, hf=hf: s.activation(
                out=stg[:, 512 * hf:512 * (hf + 1)], in_=pp1[:, 0:512],
                func=AF.Copy))
        P.d(lambda s, j=j, stg=stg: s.dma_start(
            out=v_hbm[128 * j:128 * (j + 1), :], in_=stg[:]))
    build_mix(4)
    load_proj(ga_ap, 64)
    for (off, sz) in TQ:
        def mmg(t, off=off, sz=sz):
            for dc in range(8):
                inst = t.matmul(pp2[:, 0:sz], pwb[dc][:, 0:64],
                                mixbf[dc][:, off:off + sz], start=(dc == 0),
                                stop=(dc == 7))
            return inst
        P.t(mmg)
        P.s(lambda s, off=off, sz=sz: s.activation(
            out=wgstg[:, off:off + sz], in_=pp2[:, 0:sz], func=AF.Tanh))
    P.d(lambda s: s.dma_start(out=g_hbm, in_=wgstg[:]))
    P.emit()
    sc.close()


def phase_rwkv_core(b, resid, T, hbm, w_aps, ident, tag="rc", dbg_ap=None):
    nc = b.nc
    sc = contextlib.ExitStack()
    _sb = lambda n, sh, dt=F32: b.sb(tag + n, sh, dt, scope=sc)
    _ps = lambda n, sh: b.ps(tag + n, sh, scope=sc)
    P = Prog(b, tag)
    NTJ = T // 128
    (r_hbm, k_hbm, w_hbm, v_hbm, g_hbm) = hbm
    (xw1_ap, xw2_ap, xbias_ap, mu_ap, rw_ap, kw_ap, vw_ap, wa_ap, wb_ap,
     wbias_ap, ga_ap, gb_ap, gbrow_ap, bonus_ap, gnw_ap, gnb_ap, ow_ap,
     mlow_ap, _id) = w_aps
    rj = [_sb(f"rj{h}", [128, 128], BF16) for h in range(4)]
    kj = [_sb(f"kj{h}", [128, 128], BF16) for h in range(4)]
    wj = [_sb(f"wj{h}", [128, 128]) for h in range(4)]
    vj = _sb("vj", [128, 1024], BF16)
    gj = _sb("gj", [64, 128], BF16)

    gbb = _sb("gbb", [64, 1024], BF16)
    owb = [_sb(f"ow{vc}", [128, 1024], BF16) for vc in range(8)]
    mlow = _sb("mlow", [128, 128])
    idbf = _sb("idbf", [128, 128], BF16)
    bon_t = _sb("bon", [128, 4])
    gbrow = _sb("gbrow", [1, 1024])
    gnwrow = _sb("gnwrow", [1, 1024])
    gnbrow = _sb("gnbrow", [1, 1024])
    gwt = _sb("gwt", [128, 1024])
    gbt = _sb("gbt", [128, 1024])
    gvt = _sb("gvt", [128, 1024])
    onesr = _sb("onesr", [1, 128])
    S_t = [_sb(f"S{h}", [128, 256]) for h in range(4)]
    pex = _sb("pex", [128, 128])
    cumA = _sb("cumA", [128, 128])
    cumB = _sb("cumB", [128, 128])
    wex = _sb("wex", [128, 128])
    lqb = _sb("lqb", [128, 128], BF16)
    qt = _sb("qt", [128, 128], BF16)
    negw = _sb("negw", [128, 128])
    kpb = _sb("kpb", [128, 128], BF16)
    ktl = _sb("ktl", [128, 128], BF16)
    argd = _sb("argd", [128, 128])
    ldb = _sb("ldb", [128, 128], BF16)
    kdec = _sb("kdec", [128, 128], BF16)
    lLc = _sb("lLc", [128, 2])
    epst = _sb("epst", [128, 1])
    mhalf = _sb("mhalf", [128, 1])
    mone = _sb("mone", [128, 1])
    rub = _sb("rub", [128, 128], BF16)
    pmb = _sb("pmb", [128, 128], BF16)
    tmpb = _sb("tmpb", [128, 128], BF16)
    ptb = _sb("ptb", [128, 128], BF16)
    kdTf = _sb("kdTf", [128, 128], BF16)
    smid = _sb("smid", [128, 256])
    sbf = _sb("sbf", [128, 256], BF16)
    smbf = _sb("smbf", [128, 256], BF16)
    gsil = _sb("gsil", [128, 1024])
    og = _sb("og", [128, 1024])
    sqt = _sb("sqt", [128, 1024])
    gnr = _sb("gnr", [128, 8])
    gatb = _sb("gatb", [128, 1024], BF16)
    gatT = _sb("gatT", [128, 8 * 128], BF16)
    pP = _ps("pP", [128, 128])
    pT = _ps("pT", [128, 128])
    po = _ps("po", [128, 1024])
    pA = _ps("pA", [128, 512])
    pG = _ps("pG", [128, 512])

    P.g(lambda g: g.dma_start(out=gbb[:], in_=gb_ap))
    for vc in range(8):
        P.g(lambda g, vc=vc: g.dma_start(
            out=owb[vc][:], in_=ow_ap[128 * vc:128 * (vc + 1), :]))
    P.d(lambda s: s.dma_start(out=mlow[:], in_=mlow_ap))
    P.d(lambda s: s.dma_start(out=bon_t[:], in_=bonus_ap))
    P.d(lambda s: s.dma_start(out=gbrow[:], in_=gbrow_ap))
    P.d(lambda s: s.dma_start(out=gnwrow[:], in_=gnw_ap))
    P.d(lambda s: s.dma_start(out=gnbrow[:], in_=gnb_ap))
    P.v(lambda v: v.tensor_copy(idbf[:], ident[:]))
    P.v(lambda v: v.memset(onesr[:], 1.0))
    P.v(lambda v: v.memset(epst[:], EPS))
    P.v(lambda v: v.memset(mhalf[:], -0.5))
    P.v(lambda v: v.memset(mone[:], -1.0))
    for h in range(4):
        P.v(lambda v, h=h: v.memset(S_t[h][:], 0.0))
    for (row, dstt) in ((gbrow, gbt), (gnwrow, gwt), (gnbrow, gvt)):
        for c4 in range(2):
            P.t(lambda t, row=row, c4=c4: t.matmul(
                pG[:, 0:512], onesr[:], row[0:1, 512 * c4:512 * (c4 + 1)],
                start=True, stop=True))
            P.s(lambda s, dstt=dstt, c4=c4: s.activation(
                out=dstt[:, 512 * c4:512 * (c4 + 1)], in_=pG[:, 0:512],
                func=AF.Copy))

    for j in range(NTJ):
        jc = slice(128 * j, 128 * (j + 1))
        for h in range(4):
            P.d(lambda s, h=h, jc=jc: s.dma_start(
                out=rj[h][:], in_=r_hbm[128 * h:128 * (h + 1), jc]))
            P.d(lambda s, h=h, jc=jc: s.dma_start(
                out=kj[h][:], in_=k_hbm[128 * h:128 * (h + 1), jc]))
            P.d(lambda s, h=h, jc=jc: s.dma_start(
                out=wj[h][:], in_=w_hbm[128 * h:128 * (h + 1), jc]))
        P.d(lambda s, j=j: s.dma_start(
            out=vj[:], in_=v_hbm[128 * j:128 * (j + 1), :]))
        P.d(lambda s, jc=jc: s.dma_start(out=gj[:], in_=g_hbm[:, jc]))
        for hf in range(2):
            P.t(lambda t, hf=hf: t.matmul(
                pG[:, 0:512], gj[:],
                gbb[:, 512 * hf:512 * (hf + 1)], start=True, stop=True))
            P.v(lambda v, hf=hf: v.tensor_add(
                gsil[:, 512 * hf:512 * (hf + 1)], pG[:, 0:512],
                gbt[:, 512 * hf:512 * (hf + 1)]))
            P.s(lambda s, hf=hf: s.activation(
                out=gsil[:, 512 * hf:512 * (hf + 1)],
                in_=gsil[:, 512 * hf:512 * (hf + 1)], func=AF.Silu))
        for h in range(4):
            P.s(lambda s, h=h: s.activation(
                out=pex[:], in_=wj[h][:], func=AF.Exp))
            P.v(lambda v: v.tensor_copy(cumA[:], pex[:]))
            for st_, (bsrc, bdst) in zip(
                    [1, 2, 4, 8, 16, 32],
                    [(cumA, cumB), (cumB, cumA), (cumA, cumB), (cumB, cumA),
                     (cumA, cumB), (cumB, cumA)]):
                P.v(lambda v, s_=st_, bs=bsrc, bd=bdst: v.tensor_copy(
                    bd[:].rearrange("p (c w) -> p c w", c=2)[:, :, 0:s_],
                    bs[:].rearrange("p (c w) -> p c w", c=2)[:, :, 0:s_]))
                P.v(lambda v, s_=st_, bs=bsrc, bd=bdst: v.tensor_add(
                    bd[:].rearrange("p (c w) -> p c w", c=2)[:, :, s_:],
                    bs[:].rearrange("p (c w) -> p c w", c=2)[:, :, s_:],
                    bs[:].rearrange("p (c w) -> p c w", c=2)[:, :, 0:64 - s_]))
            cum = cumA
            P.v(lambda v, cum=cum: v.tensor_sub(wex[:], pex[:], cum[:]))
            P.s(lambda s: s.activation(out=wex[:], in_=wex[:], func=AF.Exp))
            P.v(lambda v: v.tensor_copy(lqb[:], wex[:]))
            P.v(lambda v, h=h: v.tensor_mul(qt[:], rj[h][:], lqb[:]))
            P.v(lambda v, cum=cum: v.tensor_scalar(
                out=negw[:], in0=cum[:], scalar1=80.0, scalar2=None,
                op0=ALU.min))
            P.s(lambda s: s.activation(out=negw[:], in_=negw[:], func=AF.Exp))
            P.v(lambda v: v.tensor_copy(kpb[:], negw[:]))
            P.v(lambda v, h=h: v.tensor_mul(ktl[:], kj[h][:], kpb[:]))
            for s_ in range(2):
                P.v(lambda v, s_=s_, cum=cum: v.tensor_scalar(
                    out=argd[:, 64 * s_:64 * (s_ + 1)],
                    in0=cum[:, 64 * s_:64 * (s_ + 1)],
                    scalar1=cum[:, 64 * s_ + 63:64 * (s_ + 1)],
                    scalar2=None, op0=ALU.subtract))
                P.s(lambda s, s_=s_, cum=cum: s.activation(
                    out=lLc[:, s_:s_ + 1],
                    in_=cum[:, 64 * s_ + 63:64 * (s_ + 1)], func=AF.Exp,
                    scale=mone[:, 0:1]))
            P.s(lambda s: s.activation(out=argd[:], in_=argd[:], func=AF.Exp))
            P.v(lambda v: v.tensor_copy(ldb[:], argd[:]))
            P.v(lambda v, h=h: v.tensor_mul(kdec[:], kj[h][:], ldb[:]))
            P.t(lambda t: t.matmul(pP[:], qt[:], ktl[:], start=True,
                                   stop=True))
            P.v(lambda v: v.tensor_mul(pmb[:], pP[:], mlow[:]))
            P.v(lambda v, h=h: v.tensor_scalar_mul(
                rub[:], rj[h][:], bon_t[:, h:h + 1]))
            P.t(lambda t, h=h: t.matmul(pP[:], rub[:], kj[h][:],
                                        start=True, stop=True))
            P.v(lambda v: v.tensor_mul(tmpb[:], pP[:], ident[:]))
            P.v(lambda v: v.tensor_add(pmb[:], pmb[:], tmpb[:]))
            P.t(lambda t: t.matmul(pT[:], pmb[:], idbf[:], start=True,
                                   stop=True))
            P.v(lambda v: v.tensor_copy(ptb[:], pT[:]))
            for s_ in range(2):
                rows = slice(64 * s_, 64 * (s_ + 1))
                P.t(lambda t, rows=rows: t.matmul(
                    pT[rows, :], kdec[:, rows], idbf[:], start=True,
                    stop=True))
                P.v(lambda v, rows=rows: v.tensor_copy(kdTf[rows, :],
                                                       pT[rows, :]))
                P.t(lambda t, s_=s_, h=h, rows=rows: t.matmul(
                    pA[:, 256 * s_:256 * (s_ + 1)], kdTf[rows, :],
                    vj[rows, 256 * h:256 * (h + 1)], start=True,
                    stop=True))
            P.v(lambda v, h=h: v.tensor_scalar_mul(smid[:], S_t[h][:],
                                                   lLc[:, 0:1]))
            P.v(lambda v: v.tensor_add(smid[:], smid[:], pA[:, 0:256]))
            P.v(lambda v, h=h: v.tensor_copy(sbf[:], S_t[h][:]))
            P.v(lambda v: v.tensor_copy(smbf[:], smid[:]))
            P.t(lambda t, h=h: t.matmul(
                po[:, 256 * h:256 * (h + 1)], ptb[0:64, :],
                vj[0:64, 256 * h:256 * (h + 1)], start=True, stop=False))
            P.t(lambda t, h=h: t.matmul(
                po[0:64, 256 * h:256 * (h + 1)], qt[:, 0:64], sbf[:],
                start=False, stop=False))
            P.t(lambda t, h=h: t.matmul(
                po[64:128, 256 * h:256 * (h + 1)], qt[:, 64:128], smbf[:],
                start=False, stop=False))
            P.t(lambda t, h=h: t.matmul(
                po[:, 256 * h:256 * (h + 1)], ptb[64:128, :],
                vj[64:128, 256 * h:256 * (h + 1)], start=False,
                stop=True))
            P.v(lambda v, h=h: v.tensor_scalar_mul(S_t[h][:], smid[:],
                                                   lLc[:, 1:2]))
            P.v(lambda v, h=h: v.tensor_add(S_t[h][:], S_t[h][:],
                                            pA[:, 256:512]))
        P.s(lambda s: s.activation(out=og[:], in_=po[:], func=AF.Copy))
        if dbg_ap is not None:
            P.d(lambda s, j=j: s.dma_start(
                out=dbg_ap[0][:, 1024 * j:1024 * (j + 1)], in_=og[:]))
        P.v(lambda v: v.reduce_sum(gnr[:, 0:1], og[:], axis=AXX))
        P.v(lambda v: v.tensor_scalar_mul(gnr[:, 1:2], gnr[:, 0:1],
                                          1.0 / 1024.0))
        P.s(lambda s: s.activation(out=sqt[:], in_=og[:], func=AF.Square))
        P.v(lambda v: v.reduce_sum(gnr[:, 2:3], sqt[:], axis=AXX))
        P.v(lambda v: v.tensor_scalar_mul(gnr[:, 3:4], gnr[:, 2:3],
                                          1.0 / 1024.0))
        P.s(lambda s: s.activation(out=gnr[:, 4:5], in_=gnr[:, 1:2],
                                   func=AF.Square))
        P.v(lambda v: v.tensor_sub(gnr[:, 5:6], gnr[:, 3:4], gnr[:, 4:5]))
        P.s(lambda s: s.activation(out=gnr[:, 6:7], in_=gnr[:, 5:6],
                                   func=AF.Ln, bias=epst[:, 0:1]))
        P.s(lambda s: s.activation(out=gnr[:, 6:7], in_=gnr[:, 6:7],
                                   func=AF.Exp, scale=mhalf[:, 0:1]))
        P.v(lambda v: v.tensor_scalar(out=og[:], in0=og[:],
                                      scalar1=gnr[:, 1:2],
                                      scalar2=gnr[:, 6:7], op0=ALU.subtract,
                                      op1=ALU.mult))
        P.v(lambda v: v.tensor_mul(og[:], og[:], gwt[:]))
        P.v(lambda v: v.tensor_add(og[:], og[:], gvt[:]))
        P.v(lambda v: v.tensor_mul(gatb[:], og[:], gsil[:]))
        if dbg_ap is not None:
            P.d(lambda s, j=j: s.dma_start(
                out=dbg_ap[1][:, 1024 * j:1024 * (j + 1)], in_=gatb[:]))
            P.d(lambda s, j=j: s.dma_start(
                out=dbg_ap[2][:, 1024 * j:1024 * (j + 1)], in_=gsil[:]))
        for gc in range(8):
            P.t(lambda t, gc=gc: t.matmul(
                pT[:], gatb[:, 128 * gc:128 * (gc + 1)], idbf[:], start=True,
                stop=True))
            P.v(lambda v, gc=gc: v.tensor_copy(
                gatT[:, 128 * gc:128 * (gc + 1)], pT[:]))
        for doc in range(8):
            def mmo(t, j=j, doc=doc):
                for vc in range(8):
                    inst = t.matmul(pT[:], owb[vc][:, 128 * doc:
                                                   128 * (doc + 1)],
                                    gatT[:, 128 * vc:128 * (vc + 1)],
                                    start=(vc == 0), stop=(vc == 7))
                return inst
            P.t(mmo)
            P.v(lambda v, j=j, doc=doc: v.tensor_add(
                resid[doc][:, 1 + 128 * j:1 + 128 * (j + 1)],
                resid[doc][:, 1 + 128 * j:1 + 128 * (j + 1)], pT[:]))
    P.emit()
    sc.close()



def build_launch3(T, nbtok=8192.0, dbg=False):
    b = B()
    Tc = T + 1
    r3_ap = b.din("resid3h", [1024, Tc])
    parts_ap = b.din("parts", [8, 128, 8])
    bsel_ap = b.din("bsel", [128, 8])
    gw1_ap = b.din("gc_w1", [1024, 512])
    gb1_ap = b.din("gc_b1_dm", [128, 4])
    gw2_ap = b.din("gc_w2", [512, 256])
    gb2_ap = b.din("gc_b2_dm", [128, 2])
    gw3_ap = b.din("gc_w3", [256, 1])
    gb3_ap = b.din("gc_b3", [1, 1])
    ar_ap = b.din("ar", [1, 1])
    cw1_ap = b.din("cn_w1", [1024, 1024])
    cb1_ap = b.din("cn_b1_dm", [128, 8])
    cw2_ap = b.din("cn_w2", [1024, 512])
    cb2_ap = b.din("cn_b2_dm", [128, 4])
    cw3_ap = b.din("cn_w3", [512, 256])
    cb3_ap = b.din("cn_b3_dm", [128, 2])
    cw4_ap = b.din("cn_w4", [256, 1])
    cb4_ap = b.din("cn_b4", [1, 1])
    hm_ap = b.din("hm", [128, 1])
    xw1_ap = b.din("xp_w1T", [1024, 160])
    xw2_ap = b.din("xp_w2r", [5, 32, 1024])
    xbias_ap = b.din("x_bias_dm", [5, 128, 8])
    mu_ap = b.din("xp_mu_dm", [128, 8])
    rw_ap = b.din("rwT", [1024, 512])
    kw_ap = b.din("kwT", [1024, 512])
    vw_ap = b.din("vwT", [1024, 1024])
    wa_ap = b.din("waT", [1024, 64])
    wb_ap = b.din("wbT", [64, 512])
    wbias_ap = b.din("w_b_dm", [128, 4])
    ga_ap = b.din("gaT", [1024, 64])
    gb_ap = b.din("gbT", [64, 1024])
    gbrow_ap = b.din("g_b_row", [1, 1024])
    bonus_ap = b.din("bonus_dm", [128, 4])
    gnw_ap = b.din("gn_w_row", [1, 1024])
    gnb_ap = b.din("gn_b_row", [1, 1024])
    ow_ap = b.din("owT", [1024, 1024])
    mlow_ap = b.din("m_low_t", [128, 128])
    id_ap = b.din("ident", [128, 128])
    o_resid5 = b.dout("o_resid5", [1024, T])
    o_part = b.dout("o_part", [128, 8])
    r4_hbm = b.nc.dram_tensor("r4_hbm", [1024, Tc], F32, kind="Internal").ap()
    x4_hbm = b.nc.dram_tensor("x4_hbm", [1024, Tc], F32, kind="Internal").ap()
    kind5 = "ExternalOutput" if dbg else "Internal"
    rH = b.nc.dram_tensor("rH", [512, T], BF16, kind=kind5).ap()
    kH = b.nc.dram_tensor("kH", [512, T], BF16, kind=kind5).ap()
    wH = b.nc.dram_tensor("wH", [512, T], F32, kind=kind5).ap()
    vH = b.nc.dram_tensor("vH", [T, 1024], BF16, kind=kind5).ap()
    gH = b.nc.dram_tensor("gH", [64, T], BF16, kind=kind5).ap()
    x4H = b.nc.dram_tensor("x4H", [1024, T + 1], F32,
                           kind=kind5) if dbg else None
    hbm5 = (rH, kH, wH, vH, gH)

    with b.es:
        b.init_consts([EPS, 1.0 / D, -0.5, -1.0])
        ident = b.sb("ident_sb", [128, 128])
        phase_load_dmaj(b, id_ap, [ident], 128, tag="ldi")
        c0_t = b.sb("c0t", [1, 4])
        c1_t = b.sb("c1t", [1, 4])
        phase_gf(b, parts_ap, bsel_ap,
                 (gw1_ap, gb1_ap, gw2_ap, gb2_ap, gw3_ap, gb3_ap), ar_ap,
                 c0_t, c1_t, nbtok, tag="gf")
        w_aps = (xw1_ap, xw2_ap, xbias_ap, mu_ap, rw_ap, kw_ap, vw_ap,
                 wa_ap, wb_ap, wbias_ap, ga_ap, gb_ap, gbrow_ap, bonus_ap,
                 gnw_ap, gnb_ap, ow_ap, mlow_ap, ident)
        with contextlib.ExitStack() as scA:
            resid = [b.sb(f"res{g}", [128, Tc], scope=scA) for g in range(8)]
            phase_load_dmaj(b, r3_ap, resid, Tc, tag="ldr")
            with contextlib.ExitStack() as scB:
                outA = [b.sb(f"oA{g}", [128, Tc], scope=scB)
                        for g in range(8)]
                outbf = [b.sb(f"obf{g}", [128, Tc], BF16, scope=scB)
                         for g in range(8)]
                phase_layernorm(b, resid, Tc, out_f32=outA, tag="ln5")
                phase_coupling(b, outA, outbf, resid, Tc,
                               (cw1_ap, cb1_ap, cw2_ap, cb2_ap, cw3_ap,
                                cb3_ap, cw4_ap, cb4_ap), c0_t, c1_t,
                               tag="cp", CW=256)
            with contextlib.ExitStack() as scC:
                x4t = [b.sb(f"x4{g}", [128, Tc], scope=scC) for g in range(8)]
                phase_layernorm(b, resid, Tc, out_f32=x4t, tag="lnx4")
                phase_store_dmaj(b, x4t, x4_hbm, Tc, tag="stx4")
                if dbg:
                    phase_store_dmaj(b, x4t, x4H.ap(), Tc, tag="stx4d")
            phase_store_dmaj(b, resid, r4_hbm, Tc, tag="str4")
        with contextlib.ExitStack() as scF:
            x4 = [b.sb(f"xf{g}", [128, Tc], scope=scF) for g in range(8)]
            phase_load_dmaj(b, x4_hbm, x4, Tc, tag="ldx4")
            phase_rwkv_front(b, x4, T, Tc, w_aps, hm_ap, hbm5, tag="rk")
        with contextlib.ExitStack() as scD:
            resid2 = [b.sb(f"rs{g}", [128, Tc], scope=scD) for g in range(8)]
            phase_load_dmaj(b, r4_hbm, resid2, Tc, tag="ldr4")
            dbgc = None
            if dbg:
                dbgc = (b.dout("o_dbgc", [128, 1024 * (T // 128)]),
                        b.dout("o_dbg2", [128, 1024 * (T // 128)], BF16),
                        b.dout("o_dbg3", [128, 1024 * (T // 128)]))
            phase_rwkv_core(b, resid2, T, hbm5, w_aps, ident, tag="rc",
                            dbg_ap=dbgc)
            with contextlib.ExitStack() as scE:
                xlf = [b.sb(f"xl{g}", [128, T], scope=scE) for g in range(8)]
                phase_layernorm(b, [resid2[g][0:128, 1:Tc] for g in range(8)],
                                T, out_f32=xlf, tag="ln6")
                phase_partials(b, xlf, T, o_part, tag="pp")
            phase_store_dmaj(b, resid2, o_resid5, T, tag="str5", col0=1)
    return b



L0_KEYS = ("gate_w", "w1", "w2", "ident", "win", "conv_diag", "bconv_dm",
           "wx", "wdt", "bdt_dm", "a_dm")
L1_GF_KEYS = ("gc_w1", "gc_b1_dm", "gc_w2", "gc_b2_dm", "gc_w3", "gc_b3",
              "ar")
L1_CN_KEYS = ("cn_w1", "cn_b1_dm", "cn_w2", "cn_b2_dm", "cn_w3", "cn_b3_dm",
              "cn_w4", "cn_b4")
L1_RK_KEYS = ("xp_w1T", "xp_w2r", "x_bias_dm", "xp_mu_dm", "rwT", "kwT",
              "vwT", "waT", "wbT", "w_b_dm", "gaT", "gbT", "g_b_row",
              "bonus_dm", "gn_w_row", "gn_b_row", "owT", "m_low_t")


def kernel(**inputs):
    from concourse.bass_utils import run_bass_kernel_spmd
    T = 2048
    cores = list(range(8))
    x = np.asarray(inputs["L0_attn_x"], np.float32)
    w0 = host_prep_l0(inputs, T)
    w1w = host_prep_l1(inputs)

    b1 = build_launch1(T)
    feeds1 = []
    for c in cores:
        bi, qi = c // 4, c % 4
        xT = x[bi].T
        xh = np.zeros((1024, T + HALO), np.float32)
        xh[:, HALO:] = xT[:, T * qi:T * (qi + 1)]
        if qi > 0:
            xh[:, 0:HALO] = xT[:, T * qi - HALO:T * qi]
        feeds1.append(dict(x_dmaj=np.ascontiguousarray(xh),
                           cconsts=b1.host_consts(),
                           **{k: w0[k] for k in L0_KEYS}))
    r1 = run_bass_kernel_spmd(b1.nc, feeds1, core_ids=cores).results

    b2 = build_launch2(T)
    feeds2 = []
    for c in cores:
        bi, qi = c // 4, c % 4
        ch_dsum = np.stack([np.asarray(r1[4 * bi + k]["o_dsum"], np.float32)
                            for k in range(4)])
        ch_state = np.stack([np.asarray(r1[4 * bi + k]["o_state"], np.float32)
                             for k in range(4)])
        msel = np.zeros((128, 4), np.float32)
        if qi > 0:
            msel[:, qi - 1] = 1.0
        feeds2.append(dict(
            resid1=r1[c]["o_resid1"], d_u=r1[c]["o_u"],
            d_delta=r1[c]["o_delta"], d_zsilu=r1[c]["o_zsilu"],
            d_bc=r1[c]["o_bc"], a_dm=w0["a_dm"], dp_dm=w0["dp_dm"],
            ch_dsum=ch_dsum, ch_state=ch_state, msel=msel, wout=w0["wout"],
            gate_w=w0["gate_w"], ident=w0["ident"], w1=w0["w1"],
            w2=w0["w2"], cconsts=b2.host_consts()))
    r2 = run_bass_kernel_spmd(b2.nc, feeds2, core_ids=cores).results

    b3 = build_launch3(T)
    parts3 = np.stack([np.asarray(r2[j]["o_part"], np.float32)
                       for j in range(8)])
    feeds3 = []
    for c in cores:
        bi, qi = c // 4, c % 4
        bsel = np.zeros((128, 8), np.float32)
        bsel[:, 4 * bi:4 * bi + 4] = 1.0
        r3h = np.zeros((1024, T + 1), np.float32)
        r3h[:, 1:] = np.asarray(r2[c]["o_resid3"], np.float32)
        hmv = 0.0
        if qi > 0:
            r3h[:, 0] = np.asarray(r2[c - 1]["o_resid3"],
                                   np.float32)[:, T - 1]
            hmv = 1.0
        feeds3.append(dict(
            resid3h=np.ascontiguousarray(r3h), parts=parts3, bsel=bsel,
            hm=np.full((128, 1), hmv, np.float32), ident=w0["ident"],
            cconsts=b3.host_consts(),
            **{k: w1w[k] for k in L1_GF_KEYS + L1_CN_KEYS + L1_RK_KEYS}))
    r3 = run_bass_kernel_spmd(b3.nc, feeds3, core_ids=cores).results

    b4 = build_launch4(T)
    parts4 = np.stack([np.asarray(r3[j]["o_part"], np.float32)
                       for j in range(8)])
    feeds4 = []
    for c in cores:
        bi = c // 4
        bsel = np.zeros((128, 8), np.float32)
        bsel[:, 4 * bi:4 * bi + 4] = 1.0
        feeds4.append(dict(
            resid5=r3[c]["o_resid5"], parts=parts4, bsel=bsel,
            cconsts=b4.host_consts(),
            **{k: w1w[k] for k in L1_GF_KEYS + L1_CN_KEYS}))
    r4 = run_bass_kernel_spmd(b4.nc, feeds4, core_ids=cores).results

    out = np.empty((2, 8192, 1024), np.float32)
    for c in cores:
        bi, qi = c // 4, c % 4
        out[bi, T * qi:T * (qi + 1), :] = \
            np.asarray(r4[c]["o_final"], np.float32).T
    return out


# version 05
# speedup vs baseline: 1.0458x; 1.0458x over previous
import contextlib
import numpy as np

import concourse.bass as bass
import concourse.mybir as mybir
from concourse.bass_utils import run_bass_kernel_spmd

F32 = mybir.dt.float32
BF16 = mybir.dt.bfloat16
AF = mybir.ActivationFunctionType
ALU = mybir.AluOpType
AXX = mybir.AxisListType.X

NCORES = 8
D = 1024
DI = 2048
NST = 16
DT_RANK = 64
HALO = 3
NEXP = 8
FF = 4096
EPS = 1e-5


def fsplit(total, chunk=512):
    n = (total + chunk - 1) // chunk
    base, rem = divmod(total, n)
    out = []
    off = 0
    for i in range(n):
        sz = base + (1 if i < rem else 0)
        out.append((off, sz))
        off += sz
    return out


class Ctr:
    def __init__(self, sem):
        self.sem = sem
        self.n = 0


class B:

    def __init__(self):
        self.nc = bass.Bass("TRN2", target_bir_lowering=False, debug=False)
        self.es = contextlib.ExitStack()
        self._sems = {}

    def sem(self, name) -> Ctr:
        if name not in self._sems:
            self._sems[name] = Ctr(self.es.enter_context(self.nc.semaphore(name)))
        return self._sems[name]

    def sb(self, name, shape, dtype=F32, scope=None):
        return (scope or self.es).enter_context(
            self.nc.sbuf_tensor(name, shape, dtype))

    def ps(self, name, shape, dtype=F32, scope=None):
        return (scope or self.es).enter_context(
            self.nc.psum_tensor(name, shape, dtype))

    def din(self, name, shape, dtype=F32):
        return self.nc.dram_tensor(name, shape, dtype, kind="ExternalInput").ap()

    def dout(self, name, shape, dtype=F32):
        return self.nc.dram_tensor(name, shape, dtype, kind="ExternalOutput").ap()

    def close(self):
        self.es.close()

    def init_consts(self, values):
        self.const_values = [float(v) for v in values]
        n = len(self.const_values)
        ap = self.din("cconsts", [1, n])
        tile = self.sb("cconsts_sb", [1, n])
        for i, v in enumerate(self.const_values):
            self.nc.const_aps.register(F32, v, tile[:, i:i + 1])
        sd = self.sem("d")
        with self.nc.Block() as blk:
            @blk.sync
            def _(s):
                s.dma_start(out=tile[:], in_=ap).then_inc(sd.sem, 16)
                s.wait_ge(sd.sem, sd.n + 16)
        sd.n += 16

    def host_consts(self):
        return np.asarray([self.const_values], dtype=np.float32)



def phase_layernorm(b, x_tiles, T, out_bf=None, out_f32=None, tag="ln",
                    scope=True, dbg=False):
    nc = b.nc
    sc = contextlib.ExitStack() if scope else None
    _sb = lambda n, sh, dt=F32: b.sb(n, sh, dt, scope=sc)
    _ps = lambda n, sh: b.ps(n, sh, scope=sc)
    ones = _sb(f"{tag}_ones", [128, 1])
    onesr = _sb(f"{tag}_onesr", [1, 128])
    sq = [_sb(f"{tag}_sq{g}", [128, 512]) for g in range(8)]
    xf = _sb(f"{tag}_xf", [128, 512])
    mrow = _sb(f"{tag}_mrow", [1, 512])
    msq = _sb(f"{tag}_msq", [1, 512])
    m2 = _sb(f"{tag}_m2", [1, 512])
    vrow = _sb(f"{tag}_vrow", [1, 512])
    rrow = _sb(f"{tag}_rrow", [1, 512])
    mb = _sb(f"{tag}_mb", [128, 512])
    rb = _sb(f"{tag}_rb", [128, 512])
    p1 = _ps(f"{tag}_p1", [1, 512])
    p2 = _ps(f"{tag}_p2", [1, 512])
    pb = _ps(f"{tag}_pb", [128, 512])
    pb2 = _ps(f"{tag}_pb2", [128, 512])

    ss, stt, sv = b.sem("s"), b.sem("t"), b.sem("v")
    chunks = fsplit(T, 512)
    nch = len(chunks)
    s0, t0, v0 = ss.n, stt.n, sv.n

    with nc.Block() as blk:
        @blk.vector
        def _(v):
            v.memset(ones[:], 1.0)
            v.memset(onesr[:], 1.0).then_inc(sv.sem, 1)
            for i, (off, sz) in enumerate(chunks):
                v.wait_ge(ss.sem, s0 + 4 * i + 2)
                v.tensor_sub(vrow[:, 0:sz], msq[:, 0:sz], m2[:, 0:sz]) \
                    .then_inc(sv.sem, 1)
                v.wait_ge(ss.sem, s0 + 4 * i + 4)
                last = None
                for g in range(8):
                    xin = x_tiles[g][:, off:off + sz]
                    v.tensor_sub(xf[:, 0:sz], xin, mb[:, 0:sz])
                    if out_f32 is not None:
                        xo = out_f32[g][:, off:off + sz]
                        last = v.tensor_mul(xo, xf[:, 0:sz], rb[:, 0:sz])
                        if out_bf is not None:
                            last = v.tensor_copy(out_bf[g][:, off:off + sz], xo)
                    else:
                        last = v.tensor_mul(out_bf[g][:, off:off + sz],
                                            xf[:, 0:sz], rb[:, 0:sz])
                last.then_inc(sv.sem, 1)
        sv.n = v0 + 1 + 2 * nch

        @blk.scalar
        def _(s):
            for i, (off, sz) in enumerate(chunks):
                if i > 0:
                    s.wait_ge(stt.sem, t0 + 2 * (i - 1) + 1)
                for g in range(8):
                    inst = s.activation(out=sq[g][:, 0:sz],
                                        in_=x_tiles[g][:, off:off + sz],
                                        func=AF.Square)
                inst.then_inc(ss.sem, 1)
                s.wait_ge(stt.sem, t0 + 2 * i + 1)
                s.activation(out=mrow[:, 0:sz], in_=p1[:, 0:sz], func=AF.Copy,
                             scale=1.0 / D)
                s.activation(out=msq[:, 0:sz], in_=p2[:, 0:sz], func=AF.Copy,
                             scale=1.0 / D)
                s.activation(out=m2[:, 0:sz], in_=mrow[:, 0:sz],
                             func=AF.Square).then_inc(ss.sem, 1)
                s.wait_ge(sv.sem, v0 + 1 + 2 * i + 1)
                s.activation(out=rrow[:, 0:sz], in_=vrow[:, 0:sz], func=AF.Ln,
                             bias=EPS)
                s.activation(out=rrow[:, 0:sz], in_=rrow[:, 0:sz], func=AF.Exp,
                             scale=-0.5).then_inc(ss.sem, 1)
                s.wait_ge(stt.sem, t0 + 2 * i + 2)
                s.activation(out=mb[:, 0:sz], in_=pb[:, 0:sz], func=AF.Copy)
                s.activation(out=rb[:, 0:sz], in_=pb2[:, 0:sz], func=AF.Copy) \
                    .then_inc(ss.sem, 1)
        ss.n = s0 + 4 * nch

        @blk.tensor
        def _(t):
            t.wait_ge(sv.sem, v0 + 1)
            for i, (off, sz) in enumerate(chunks):
                t.wait_ge(ss.sem, s0 + 4 * i + 1)
                if i > 0:
                    t.wait_ge(sv.sem, v0 + 1 + 2 * (i - 1) + 2)
                for g in range(8):
                    t.matmul(p1[:, 0:sz], ones[:], x_tiles[g][:, off:off + sz],
                             start=(g == 0), stop=(g == 7))
                for g in range(8):
                    inst = t.matmul(p2[:, 0:sz], ones[:], sq[g][:, 0:sz],
                                    start=(g == 0), stop=(g == 7))
                inst.then_inc(stt.sem, 1)
                t.wait_ge(ss.sem, s0 + 4 * i + 3)
                t.matmul(pb[:, 0:sz], onesr[:], mrow[:, 0:sz], start=True,
                         stop=True)
                t.matmul(pb2[:, 0:sz], onesr[:], rrow[:, 0:sz], start=True,
                         stop=True).then_inc(stt.sem, 1)
        stt.n = t0 + 2 * nch
    if dbg:
        return dict(mrow=mrow, msq=msq, m2=m2, vrow=vrow, rrow=rrow, mb=mb, rb=rb, pb=pb, pb2=pb2)
    if sc is not None:
        sc.close()



def phase_load_dmaj(b, hbm_ap, tiles, W, tag="ld", cast=False):
    nc = b.nc
    sd = b.sem("d")
    with nc.Block() as blk:
        if cast:
            @blk.gpsimd
            def _(g):
                for gi, tl in enumerate(tiles):
                    g.dma_start(out=tl[:, 0:W],
                                in_=hbm_ap[gi * 128:(gi + 1) * 128, 0:W]) \
                        .then_inc(sd.sem, 16)
                g.wait_ge(sd.sem, sd.n + 16 * len(tiles))
        else:
            @blk.sync
            def _(s):
                for gi, tl in enumerate(tiles):
                    s.dma_start(out=tl[:, 0:W],
                                in_=hbm_ap[gi * 128:(gi + 1) * 128, 0:W]) \
                        .then_inc(sd.sem, 16)
                s.wait_ge(sd.sem, sd.n + 16 * len(tiles))
    sd.n += 16 * len(tiles)


def phase_store_dmaj(b, tiles, hbm_ap, W, tag="st", col0=0):
    nc = b.nc
    sd = b.sem("d")
    with nc.Block() as blk:
        @blk.sync
        def _(s):
            for gi, tl in enumerate(tiles):
                s.dma_start(out=hbm_ap[gi * 128:(gi + 1) * 128, :],
                            in_=tl[:, col0:col0 + hbm_ap.shape[1]]) \
                    .then_inc(sd.sem, 16)
            s.wait_ge(sd.sem, sd.n + 16 * len(tiles))
    sd.n += 16 * len(tiles)



def phase_gating(b, xln_f32, T, gate_w_ap, ident, cwT_hbm, tag="gt",
                 scope=True):
    nc = b.nc
    sc = contextlib.ExitStack() if scope else None
    _sb = lambda n, sh, dt=F32: b.sb(n, sh, dt, scope=sc)
    _ps = lambda n, sh: b.ps(n, sh, scope=sc)
    gw = _sb(f"{tag}_gw", [128, 64])
    cwf = _sb(f"{tag}_cwf", [128, 8])
    cw2 = _sb(f"{tag}_cw2", [128, 8])
    m1 = _sb(f"{tag}_m1", [128, 1])
    m2 = _sb(f"{tag}_m2", [128, 1])
    negm1 = _sb(f"{tag}_negm1", [128, 1])
    dm = _sb(f"{tag}_dm", [128, 1])
    sig = _sb(f"{tag}_sig", [128, 1])
    mv = _sb(f"{tag}_mv", [128, 8])
    mk1 = _sb(f"{tag}_mk1", [128, 8])
    mk2 = _sb(f"{tag}_mk2", [128, 8])
    eall = _sb(f"{tag}_eall", [128, 8])
    emsk = _sb(f"{tag}_emsk", [128, 8])
    cwt_s = _sb(f"{tag}_cwt", [8, 128])
    pcw = _ps(f"{tag}_pcw", [128, 8])
    ptr = _ps(f"{tag}_ptr", [8, 128])

    sd, sv, ss, stt = b.sem("d"), b.sem("v"), b.sem("s"), b.sem("t")
    d0, v0, s0, t0 = sd.n, sv.n, ss.n, stt.n
    nch = (T + 127) // 128

    with nc.Block() as blk:
        @blk.sync
        def _(s):
            for g in range(8):
                s.dma_start(out=gw[:, 8 * g:8 * g + 8],
                            in_=gate_w_ap[128 * g:128 * (g + 1), :]) \
                    .then_inc(sd.sem, 16)
            for i in range(nch):
                tt = min(128, T - i * 128)
                s.wait_ge(sv.sem, v0 + 11 * i + 11)
                s.dma_start(out=cwT_hbm[:, i * 128:i * 128 + tt],
                            in_=cwt_s[:, 0:tt]).then_inc(sd.sem, 16)
            s.wait_ge(sd.sem, d0 + 16 * (8 + nch))
        sd.n = d0 + 16 * (8 + nch)

        @blk.tensor
        def _(t):
            t.wait_ge(sd.sem, d0 + 16 * 8)
            for i in range(nch):
                tt = min(128, T - i * 128)
                if i > 0:
                    t.wait_ge(sv.sem, v0 + 11 * (i - 1) + 1)
                for g in range(8):
                    inst = t.matmul(pcw[0:tt, :],
                                    xln_f32[g][:, i * 128:i * 128 + tt],
                                    gw[:, 8 * g:8 * g + 8], start=(g == 0),
                                    stop=(g == 7))
                inst.then_inc(stt.sem, 1)
                t.wait_ge(sv.sem, v0 + 11 * i + 10)
                if i > 0:
                    t.wait_ge(sv.sem, v0 + 11 * (i - 1) + 11)
                t.matmul(ptr[:, 0:tt], cw2[0:tt, :], ident[0:tt, 0:tt],
                         start=True, stop=True).then_inc(stt.sem, 1)
        stt.n = t0 + 2 * nch

        NV = 11

        @blk.vector
        def _(v):
            k = [v0]

            def step(inst):
                inst.then_inc(sv.sem, 1)
                k[0] += 1
                v.wait_ge(sv.sem, k[0])

            for i in range(nch):
                tt = min(128, T - i * 128)
                v.wait_ge(stt.sem, t0 + 2 * i + 1)
                v.memset(mv[:], 3.0e38)
                step(v.tensor_copy(cwf[0:tt, :], pcw[0:tt, :]))
                step(v.reduce_max(m1[0:tt, :], cwf[0:tt, :], axis=AXX))
                step(v.tensor_copy(mv[0:tt, 0:1], m1[0:tt, :]))
                step(v.match_replace(mk1[0:tt, :], mv[0:tt, :], cwf[0:tt, :],
                                     -1e9))
                step(v.reduce_max(m2[0:tt, :], mk1[0:tt, :], axis=AXX))
                step(v.tensor_copy(mv[0:tt, 0:1], m2[0:tt, :]))
                v.tensor_scalar_mul(negm1[0:tt, :], m1[0:tt, :], -1.0)
                step(v.tensor_sub(dm[0:tt, :], m2[0:tt, :], m1[0:tt, :]))
                step(v.match_replace(mk2[0:tt, :], mv[0:tt, :], mk1[0:tt, :],
                                     -1e9))
                v.wait_ge(ss.sem, s0 + i + 1)
                step(v.tensor_sub(cw2[0:tt, :], eall[0:tt, :], emsk[0:tt, :]))
                step(v.tensor_scalar_mul(cw2[0:tt, :], cw2[0:tt, :],
                                         sig[0:tt, :]))
                v.wait_ge(stt.sem, t0 + 2 * i + 2)
                step(v.tensor_copy(cwt_s[:, 0:tt], ptr[:, 0:tt]))
        sv.n = v0 + NV * nch

        @blk.scalar
        def _(s):
            for i in range(nch):
                tt = min(128, T - i * 128)
                s.wait_ge(sv.sem, v0 + 11 * i + 8)
                s.activation(out=eall[0:tt, :], in_=cwf[0:tt, :], func=AF.Exp,
                             bias=negm1[0:tt, :])
                s.activation(out=emsk[0:tt, :], in_=mk2[0:tt, :], func=AF.Exp,
                             bias=negm1[0:tt, :])
                s.activation(out=sig[0:tt, :], in_=dm[0:tt, :], func=AF.Sigmoid,
                             scale=-1.0).then_inc(ss.sem, 1)
        ss.n = s0 + nch

    if sc is not None:
        sc.close()
        return None
    return dict(cwf=cwf, cw2=cw2, m1=m1, m2=m2, negm1=negm1, dm=dm, sig=sig,
                mv=mv, mk1=mk1, mk2=mk2, eall=eall, emsk=emsk)



def phase_moe(b, xln_bf, resid, T, w1_ap, w2_ap, cwT_hbm, scale_out=0.5,
              tag="moe", serial=False):
    nc = b.nc
    sc = contextlib.ExitStack()
    _sb = lambda n, sh, dt=F32: b.sb(tag + n, sh, dt, scope=sc)
    _ps = lambda n, sh: b.ps(tag + n, sh, scope=sc)

    w1s = [[_sb(f"w1_{bi}_{dc}", [128, 1024], BF16) for dc in range(8)]
           for bi in range(2)]
    w2s = [_sb(f"w2_{wc}", [128, 1024], BF16) for wc in range(8)]
    hq = [_sb(f"hq{wc}", [128, T], BF16) for wc in range(8)]
    htmp = [_sb(f"htmp{k}", [128, 512], BF16) for k in range(2)]
    ce = _sb("ce", [128, T], BF16)
    cwrow = _sb("cwrow", [1, T])
    onesr = _sb("onesr", [1, 128])
    junk = _sb("junk", [1, 128])
    ph = [_ps(f"ph{k}", [128, 512]) for k in range(2)]
    po = [_ps(f"po{k}", [128, 512]) for k in range(2)]
    pce = _ps("pce", [128, 512])

    sd, sv, ss, stt = b.sem("d"), b.sem("v"), b.sem("s"), b.sem("t")
    sgp = [b.sem("g0"), b.sem("g1")]
    sg2 = b.sem("g2")
    d0, v0, s0, t0 = sd.n, sv.n, ss.n, stt.n
    gp0 = [sgp[0].n, sgp[1].n]
    g20 = sg2.n
    TQ = fsplit(T, 512)
    NJ = len(TQ)
    NU = 8 * NJ
    EQ = [(e, q) for e in range(8) for q in range(4)]

    def t_base(idx):
        return t0 + 2 * NU * idx + NJ * (idx // 4 + 1)

    def t_pce_done(e, j):
        return t0 + 2 * NU * (4 * e) + NJ * e + j + 1

    def t_h_done(idx, u):
        return t_base(idx) + u + 1

    def t_o_done(idx, u):
        return t_base(idx) + NU + u + 1

    def v_base(idx):
        return v0 + 1 + 2 * NU * idx + NJ * (idx // 4 + 1)

    def v_ce_done(e, j):
        return v0 + 1 + 2 * NU * (4 * e) + NJ * e + j + 1

    def v_h_done(idx, u):
        return v_base(idx) + u + 1

    def v_o_done(idx, u):
        return v_base(idx) + NU + u + 1

    def s_h_done(idx, u):
        return s0 + NU * idx + u + 1

    def g_batch(idx):
        return gp0[idx % 2] + 128 * (idx // 2 + 1)

    def g2_batch(idx):
        return g20 + 128 * (idx + 1)

    def d_cw(e):
        return d0 + 16 * (e + 1)

    with nc.Block() as blk:
        @blk.gpsimd
        def _(g):
            for idx, (e, q) in enumerate(EQ):
                bi = idx % 2
                if idx >= 2:
                    g.wait_ge(stt.sem, t_h_done(idx - 2, NU - 1))
                for dc in range(8):
                    g.dma_start(out=w1s[bi][dc][:],
                                in_=w1_ap[e, 128 * dc:128 * (dc + 1),
                                          1024 * q:1024 * (q + 1)]) \
                        .then_inc(sgp[bi].sem, 16)
                if idx >= 1:
                    g.wait_ge(stt.sem, t_o_done(idx - 1, NU - 1))
                for wc in range(8):
                    g.dma_start(out=w2s[wc][:],
                                in_=w2_ap[e, 1024 * q + 128 * wc:
                                          1024 * q + 128 * (wc + 1), :]) \
                        .then_inc(sg2.sem, 16)
            g.wait_ge(sgp[0].sem, gp0[0] + 128 * (len(EQ) // 2))
            g.wait_ge(sgp[1].sem, gp0[1] + 128 * (len(EQ) // 2))
            g.wait_ge(sg2.sem, g20 + 128 * len(EQ))
        sgp[0].n = gp0[0] + 128 * (len(EQ) // 2)
        sgp[1].n = gp0[1] + 128 * (len(EQ) // 2)
        sg2.n = g20 + 128 * len(EQ)

        @blk.sync
        def _(s):
            for e in range(8):
                if e > 0:
                    s.wait_ge(stt.sem, t_pce_done(e - 1, NJ - 1))
                s.dma_start(out=cwrow[:], in_=cwT_hbm[e:e + 1, 0:T]) \
                    .then_inc(sd.sem, 16)
            s.wait_ge(sd.sem, d0 + 16 * 8)
        sd.n = d0 + 16 * 8

        @blk.tensor
        def _(t):
            t.wait_ge(sv.sem, v0 + 1)
            for idx, (e, q) in enumerate(EQ):
                bi = idx % 2
                if q == 0:
                    t.wait_ge(sd.sem, d_cw(e))
                    for j, (off, sz) in enumerate(TQ):
                        if e > 0 or j > 0:
                            pj = (e, j - 1) if j > 0 else (e - 1, NJ - 1)
                            t.wait_ge(sv.sem, v_ce_done(*pj))
                        t.matmul(pce[:, 0:sz], onesr[:],
                                 cwrow[0:1, off:off + sz], start=True,
                                 stop=True).then_inc(stt.sem, 1)
                t.wait_ge(sgp[idx % 2].sem, g_batch(idx))
                for u in range(NU):
                    wcc, j = divmod(u, NJ)
                    off, sz = TQ[j]
                    if idx > 0 or u >= 2:
                        pi, pu = (idx, u - 2) if u >= 2 else (idx - 1, NU + u - 2)
                        t.wait_ge(ss.sem, s_h_done(pi, pu))
                    for dc in range(8):
                        inst = t.matmul(ph[u % 2][:, 0:sz],
                                        w1s[bi][dc][:, 128 * wcc:128 * (wcc + 1)],
                                        xln_bf[dc][:, off:off + sz],
                                        start=(dc == 0), stop=(dc == 7))
                    inst.then_inc(stt.sem, 1)
                    if serial:
                        t.wait_ge(sv.sem, v_h_done(idx, u))

                t.wait_ge(sg2.sem, g2_batch(idx))
                for u in range(NU):
                    doc, j = divmod(u, NJ)
                    off, sz = TQ[j]
                    t.wait_ge(sv.sem, v_h_done(idx, (8 - 1) * NJ + j))
                    if idx > 0 or u >= 2:
                        pi, pu = (idx, u - 2) if u >= 2 else (idx - 1, NU + u - 2)
                        t.wait_ge(sv.sem, v_o_done(pi, pu) if pu < NU
                                  else v_o_done(pi, pu - NU))
                    for wc in range(8):
                        inst = t.matmul(po[u % 2][:, 0:sz],
                                        w2s[wc][:, 128 * doc:128 * (doc + 1)],
                                        hq[wc][:, off:off + sz],
                                        start=(wc == 0), stop=(wc == 7))
                    inst.then_inc(stt.sem, 1)
                    if serial:
                        t.wait_ge(sv.sem, v_o_done(idx, u))
        stt.n = t0 + 2 * NU * 32 + NJ * 8

        @blk.scalar
        def _(s):
            for idx, (e, q) in enumerate(EQ):
                for u in range(NU):
                    wcc, j = divmod(u, NJ)
                    off, sz = TQ[j]
                    s.wait_ge(stt.sem, t_h_done(idx, u))
                    if idx > 0 or u >= 2:
                        pi, pu = (idx, u - 2) if u >= 2 else (idx - 1, NU + u - 2)
                        s.wait_ge(sv.sem, v_h_done(pi, pu) if pu < NU
                                  else v_h_done(pi, pu - NU))
                    s.activation(out=htmp[u % 2][:, 0:sz], in_=ph[u % 2][:, 0:sz],
                                 func=AF.Gelu).then_inc(ss.sem, 1)
        ss.n = s0 + NU * 32

        @blk.vector
        def _(v):
            v.memset(onesr[:], 1.0)
            v.tensor_copy(junk[0:1, 0:128], onesr[0:1, 0:128]) \
                .then_inc(sv.sem, 1)
            for idx, (e, q) in enumerate(EQ):
                if q == 0:
                    for j, (off, sz) in enumerate(TQ):
                        v.wait_ge(stt.sem, t_pce_done(e, j))
                        v.tensor_copy(ce[:, off:off + sz], pce[:, 0:sz]) \
                            .then_inc(sv.sem, 1)
                if q == 0:
                    v.wait_ge(sv.sem, v_ce_done(e, NJ - 1))
                for u in range(NU):
                    wcc, j = divmod(u, NJ)
                    off, sz = TQ[j]
                    v.wait_ge(ss.sem, s_h_done(idx, u))
                    v.tensor_mul(hq[wcc][:, off:off + sz], htmp[u % 2][:, 0:sz],
                                 ce[:, off:off + sz]).then_inc(sv.sem, 1)
                for u in range(NU):
                    doc, j = divmod(u, NJ)
                    off, sz = TQ[j]
                    v.wait_ge(stt.sem, t_o_done(idx, u))
                    v.scalar_tensor_tensor(
                        out=resid[doc][:, off:off + sz],
                        in0=po[u % 2][:, 0:sz], scalar=scale_out,
                        in1=resid[doc][:, off:off + sz],
                        op0=ALU.mult, op1=ALU.add).then_inc(sv.sem, 1)
        sv.n = v0 + 1 + 2 * NU * 32 + NJ * 8

    sc.close()



def phase_mamba_front(b, xln_bf, T, Th, win_ap, diag_ap, bconv_dm_ap, wx_ap,
                      wdt_ap, bdt_dm_ap, o_u, o_delta, o_zsilu, o_bc, o_dsum,
                      tag="mf"):
    nc = b.nc
    sc = contextlib.ExitStack()
    _sb = lambda n, sh, dt=F32: b.sb(tag + n, sh, dt, scope=sc)
    _ps = lambda n, sh: b.ps(tag + n, sh, scope=sc)

    winb = [[_sb(f"win{bi}_{dc}", [128, 128], BF16) for dc in range(8)]
            for bi in range(2)]
    diagb = [[_sb(f"diag{bi}_{k}", [128, 128]) for k in range(4)]
             for bi in range(2)]
    wxb = [_sb(f"wx{bi}", [128, 96]) for bi in range(2)]
    wdb = [_sb(f"wd{bi}", [64, 128]) for bi in range(2)]
    bconv_t = _sb("bconv", [128, 16])
    bdt_t = _sb("bdt", [128, 16])
    xz = _sb("xz", [128, 2051])
    xbr = _sb("xbr", [128, 2048])
    ubf = _sb("ubf", [128, 2048], BF16)
    dl = _sb("dl", [128, 2048])
    dta = _sb("dta", [128, 512])
    dtb = _sb("dtb", [128, 512])
    zbf = _sb("zbf", [128, 2048], BF16)
    params_sb = _sb("params", [96, 2048])
    dsum_t = _sb("dsum", [128, 16])
    pxz = _ps("pxz", [128, 512])
    pcv = _ps("pcv", [128, 512])
    ppar = [_ps(f"ppar{j}", [96, 512]) for j in range(4)]
    pdl = _ps("pdl", [128, 512])

    sd, sv, ss, stt = b.sem("d"), b.sem("v"), b.sem("s"), b.sem("t")
    sgp = [b.sem("g0"), b.sem("g1")]
    d0, v0, s0, t0 = sd.n, sv.n, ss.n, stt.n
    gp0 = [sgp[0].n, sgp[1].n]
    TQh = fsplit(Th, 512)
    TQ = fsplit(T, 512)
    NJh, NJ = len(TQh), len(TQ)
    assert T <= 2048 and Th <= 2051 and len(TQ) <= 4

    def g_done(cc):
        par = cc % 2
        tot = 0
        for c in range(cc + 1):
            if c % 2 == par:
                tot += 208 if c < 16 else 128
        return gp0[par] + tot

    def g_dl_done(cc):
        par = cc % 2
        base = g_done(30 if par == 0 else 31)
        return base + 16 * (cc // 2 + 1)

    def t_cc_base(cc):
        tot = 0
        for c in range(cc):
            tot += (NJh + 2 * NJ) if c < 16 else NJ
        return t0 + tot

    t_all = t_cc_base(32)

    def t_dl_base(cc):
        return t_all + NJ * cc

    def s_cc_base(cc):
        tot = 0
        for c in range(cc):
            tot += (NJh + NJ) if c < 16 else NJ
        return s0 + tot

    s_all = s_cc_base(32)

    def s_dl_base(cc):
        return s_all + 1 + NJ * cc

    def v_dl_base(cc):
        return v0 + (NJ + 1) * cc

    def d_u(cc):
        return d0 + 16 * (2 + cc + 1)

    def d_z(zc):
        return d0 + 16 * (18 + zc + 1)

    def d_dl(cc):
        return d0 + 16 * (34 + cc + 1)

    NDMA = 52

    with nc.Block() as blk:
        @blk.gpsimd
        def _(g):
            for cc in range(32):
                par = cc % 2
                if cc >= 2:
                    g.wait_ge(stt.sem, t_cc_base(cc - 2) +
                              (NJh + 2 * NJ if cc - 2 < 16 else NJ))
                for dc in range(8):
                    g.dma_start(out=winb[par][dc][:],
                                in_=win_ap[128 * dc:128 * (dc + 1),
                                           128 * cc:128 * (cc + 1)]) \
                        .then_inc(sgp[par].sem, 16)
                if cc < 16:
                    for k in range(4):
                        g.dma_start(out=diagb[par][k][:], in_=diag_ap[cc, k]) \
                            .then_inc(sgp[par].sem, 16)
                    g.dma_start(out=wxb[par][:],
                                in_=wx_ap[128 * cc:128 * (cc + 1), :]) \
                        .then_inc(sgp[par].sem, 16)
            for cc in range(16):
                par = cc % 2
                if cc >= 2:
                    g.wait_ge(stt.sem, t_dl_base(cc - 2) + NJ)
                g.dma_start(out=wdb[par][:],
                            in_=wdt_ap[:, 128 * cc:128 * (cc + 1)]) \
                    .then_inc(sgp[par].sem, 16)
            g.wait_ge(sgp[0].sem, g_dl_done(14))
            g.wait_ge(sgp[1].sem, g_dl_done(15))
        sgp[0].n = g_dl_done(14)
        sgp[1].n = g_dl_done(15)

        @blk.sync
        def _(s):
            s.dma_start(out=bconv_t[:], in_=bconv_dm_ap).then_inc(sd.sem, 16)
            s.dma_start(out=bdt_t[:], in_=bdt_dm_ap).then_inc(sd.sem, 16)
            for cc in range(16):
                s.wait_ge(ss.sem, s_cc_base(cc) + NJh + NJ)
                s.dma_start(out=o_u[128 * cc:128 * (cc + 1), :],
                            in_=ubf[:, 0:T]).then_inc(sd.sem, 16)
            for zc in range(16):
                s.wait_ge(ss.sem, s_cc_base(16 + zc) + NJ)
                s.dma_start(out=o_zsilu[128 * zc:128 * (zc + 1), :],
                            in_=zbf[:, 0:T]).then_inc(sd.sem, 16)
            for cc in range(16):
                s.wait_ge(sv.sem, v_dl_base(cc) + NJ + 1)
                s.dma_start(out=o_delta[128 * cc:128 * (cc + 1), :],
                            in_=dl[:, 0:T]).then_inc(sd.sem, 16)
            s.wait_ge(ss.sem, s_all + 1)
            s.dma_start(out=o_bc, in_=params_sb[64:96, 0:T]).then_inc(sd.sem, 16)
            s.wait_ge(sv.sem, v_dl_base(15) + NJ + 1)
            s.dma_start(out=o_dsum, in_=dsum_t[:]).then_inc(sd.sem, 16)
            s.wait_ge(sd.sem, d0 + 16 * NDMA)
        sd.n = d0 + 16 * NDMA

        @blk.tensor
        def _(t):
            for cc in range(32):
                par = cc % 2
                tb = t_cc_base(cc)
                t.wait_ge(sgp[par].sem, g_done(cc))
                TQx = TQh if cc < 16 else TQ
                coff = 0 if cc < 16 else HALO
                for j, (off, sz) in enumerate(TQx):
                    if cc >= 1 or j >= 1:
                        pcc, pj = (cc, j - 1) if j >= 1 else (cc - 1, None)
                        if pj is None:
                            pj = (NJh if pcc < 16 else NJ) - 1
                        t.wait_ge(ss.sem, s_cc_base(pcc) + pj + 1)
                    for dc in range(8):
                        inst = t.matmul(pxz[:, 0:sz], winb[par][dc][:],
                                        xln_bf[dc][:, coff + off:
                                                    coff + off + sz],
                                        start=(dc == 0), stop=(dc == 7))
                    inst.then_inc(stt.sem, 1)
                if cc < 16:
                    t.wait_ge(ss.sem, s_cc_base(cc) + NJh)
                    for j, (off, sz) in enumerate(TQ):
                        if cc >= 1 or j >= 1:
                            pcc, pj = (cc, j - 1) if j >= 1 else (cc - 1, NJ - 1)
                            t.wait_ge(ss.sem, s_cc_base(pcc) + NJh + pj + 1)
                        for k in range(4):
                            inst = t.matmul(pcv[:, 0:sz], diagb[par][k][:],
                                            xz[:, off + k:off + k + sz],
                                            start=(k == 0), stop=(k == 3))
                        inst.then_inc(stt.sem, 1)
                    t.wait_ge(ss.sem, s_cc_base(cc) + NJh + NJ)
                    for j, (off, sz) in enumerate(TQ):
                        t.matmul(ppar[j][:, 0:sz], wxb[par][:],
                                 xbr[:, off:off + sz], start=(cc == 0),
                                 stop=(cc == 15)).then_inc(stt.sem, 1)
            t.wait_ge(ss.sem, s_all + 1)
            for cc in range(16):
                par = cc % 2
                t.wait_ge(sgp[par].sem, g_dl_done(cc))
                for j, (off, sz) in enumerate(TQ):
                    if cc >= 1 or j >= 1:
                        pcc, pj = (cc, j - 1) if j >= 1 else (cc - 1, NJ - 1)
                        t.wait_ge(ss.sem, s_dl_base(pcc) + pj + 1)
                    t.matmul(pdl[:, 0:sz], wdb[par][:],
                             params_sb[0:64, off:off + sz], start=True,
                             stop=True).then_inc(stt.sem, 1)
        stt.n = t_all + NJ * 16

        @blk.scalar
        def _(s):
            s.wait_ge(sd.sem, d0 + 32)
            for cc in range(32):
                sb_ = s_cc_base(cc)
                TQx = TQh if cc < 16 else TQ
                for j, (off, sz) in enumerate(TQx):
                    s.wait_ge(stt.sem, t_cc_base(cc) + j + 1)
                    if cc < 16:
                        s.activation(out=xz[:, off:off + sz], in_=pxz[:, 0:sz],
                                     func=AF.Copy).then_inc(ss.sem, 1)
                    else:
                        if cc >= 18:
                            pass
                        if j == 0 and cc >= 17:
                            s.wait_ge(sd.sem, d_z(cc - 16 - 1))
                        s.activation(out=zbf[:, off:off + sz], in_=pxz[:, 0:sz],
                                     func=AF.Silu).then_inc(ss.sem, 1)
                if cc < 16:
                    if cc >= 1:
                        s.wait_ge(sd.sem, d_u(cc - 1))
                    for j, (off, sz) in enumerate(TQ):
                        s.wait_ge(stt.sem, t_cc_base(cc) + NJh + j + 1)
                        s.activation(out=xbr[:, off:off + sz], in_=pcv[:, 0:sz],
                                     func=AF.Silu, bias=bconv_t[:, cc:cc + 1])
                        s.activation(out=ubf[:, off:off + sz],
                                     in_=xbr[:, off:off + sz], func=AF.Copy) \
                            .then_inc(ss.sem, 1)
            s.wait_ge(stt.sem, t_all)
            for j, (off, sz) in enumerate(TQ):
                inst = s.activation(out=params_sb[:, off:off + sz],
                                    in_=ppar[j][:, 0:sz], func=AF.Copy)
            inst.then_inc(ss.sem, 1)
            for cc in range(16):
                if cc >= 1:
                    s.wait_ge(sd.sem, d_dl(cc - 1))
                for j, (off, sz) in enumerate(TQ):
                    s.wait_ge(stt.sem, t_dl_base(cc) + j + 1)
                    if cc >= 1 or j >= 1:
                        pv = (v_dl_base(cc) + j) if j >= 1 else                             (v_dl_base(cc - 1) + NJ)
                        s.wait_ge(sv.sem, pv)
                    bia = bdt_t[:, cc:cc + 1]
                    s.activation(out=dta[:, 0:sz], in_=pdl[:, 0:sz],
                                 func=AF.Abs, bias=bia)
                    s.activation(out=dta[:, 0:sz], in_=dta[:, 0:sz],
                                 func=AF.Exp, scale=-1.0)
                    s.activation(out=dta[:, 0:sz], in_=dta[:, 0:sz],
                                 func=AF.Ln, bias=1.0)
                    s.activation(out=dtb[:, 0:sz], in_=pdl[:, 0:sz],
                                 func=AF.Relu, bias=bia).then_inc(ss.sem, 1)
        ss.n = s_all + 1 + NJ * 16

        @blk.vector
        def _(v):
            for cc in range(16):
                if cc >= 1:
                    v.wait_ge(sd.sem, d_dl(cc - 1))
                for j, (off, sz) in enumerate(TQ):
                    v.wait_ge(ss.sem, s_dl_base(cc) + j + 1)
                    v.tensor_add(dl[:, off:off + sz], dta[:, 0:sz],
                                 dtb[:, 0:sz]).then_inc(sv.sem, 1)
                v.reduce_sum(dsum_t[:, cc:cc + 1], dl[:, 0:T], axis=AXX) \
                    .then_inc(sv.sem, 1)
        sv.n = v0 + (NJ + 1) * 16

    sc.close()



def phase_scan(b, T, o_delta, o_u, o_bc, a_dm_ap, dp_dm_ap, init_ap, o_y,
               o_state, tag="sc"):
    nc = b.nc
    with_y = o_y is not None
    sc = contextlib.ExitStack()
    _sb = lambda n, sh, dt=F32: b.sb(tag + n, sh, dt, scope=sc)
    _ps = lambda n, sh: b.ps(tag + n, sh, scope=sc)

    CW = 256
    TS = fsplit(T, CW)
    NQ = len(TS)
    delq = _sb("delq", [128, 16 * CW])
    uq = _sb("uq", [128, 16 * CW])
    duq = _sb("duq", [128, 16 * CW])
    da = [_sb("da0", [128, 16 * CW]), _sb("da1", [128, 16 * CW])]
    dbu = [_sb("dbu0", [128, 16 * CW]), _sb("dbu1", [128, 16 * CW])]
    sn = _sb("sn", [128, 16 * CW])
    yq = _sb("yq", [128, 16 * CW]) if with_y else None
    ym = _sb("ym", [128, 16 * CW]) if with_y else None
    states = _sb("states", [128, 256])
    araw = _sb("araw", [128, 256])
    a_t = _sb("a_t", [128, 256])
    dp_t = _sb("dp_t", [128, 16]) if with_y else None
    bcrow = _sb("bcrow", [1, 32 * CW])
    bn = _sb("bn", [128, CW])
    cn = _sb("cn", [128, CW]) if with_y else None
    onesr = _sb("onesr", [1, 128])
    junk = _sb("junk", [1, 128])
    pb = _ps("pb", [128, CW])
    pc = _ps("pc", [128, CW]) if with_y else None

    sd, sv, ss, stt = b.sem("d"), b.sem("v"), b.sem("s"), b.sem("t")
    sgu = b.sem("g0")
    sp = b.sem("p")
    d0, v0, s0, t0 = sd.n, sv.n, ss.n, stt.n
    g0 = sgu.n
    p0 = sp.n
    PD = 0
    VD = 16 - PD

    def p_scan(i, n):
        return p0 + 16 * i + n + 1

    NVn = 5 if with_y else 3
    NV = 1 + 16 * NVn + (1 if with_y else 0)
    NSn = 3 if with_y else 2
    NS = 16 * NSn
    NTn = 2 if with_y else 1
    NT = 16 * NTn
    NR = 32 if with_y else 16

    def v_tq(i):
        return v0 + 2 + NV * i

    def v_n(i, n, k):
        return v_tq(i) + 1 + NVn * n + k

    def s_tq(i):
        return s0 + 1 + NS * i

    def s_n(i, n, k):
        return s_tq(i) + NSn * n + k

    def t_n(i, n, k):
        return t0 + NT * i + NTn * n + k

    nini = 1 + (1 if with_y else 0) + (1 if init_ap is not None else 0)
    d_yst = {}
    d_del = {}
    cnt = nini
    for i in range(NQ):
        if with_y and i > 0:
            cnt += 16
            d_yst[i - 1] = cnt
        cnt += 16 + NR
        d_del[i] = cnt
    if with_y:
        cnt += 16
        d_yst[NQ - 1] = cnt
    if o_state is not None:
        cnt += 1
    NDMA = cnt

    with nc.Block() as blk:
        @blk.sync
        def _(s):
            s.dma_start(out=araw[:], in_=a_dm_ap).then_inc(sd.sem, 16)
            if with_y:
                s.dma_start(out=dp_t[:], in_=dp_dm_ap).then_inc(sd.sem, 16)
            if init_ap is not None:
                s.dma_start(out=states[:], in_=init_ap).then_inc(sd.sem, 16)
            for i, (off, sz) in enumerate(TS):
                if i > 0:
                    s.wait_ge(sv.sem, v_tq(i - 1) + NV)
                    if with_y:
                        poff, psz = TS[i - 1]
                        for cc in range(16):
                            s.dma_start(out=o_y[128 * cc:128 * (cc + 1),
                                                poff:poff + psz],
                                        in_=yq[:, CW * cc:CW * cc + psz]) \
                                .then_inc(sd.sem, 16)
                for cc in range(16):
                    s.dma_start(out=delq[:, CW * cc:CW * cc + sz],
                                in_=o_delta[128 * cc:128 * (cc + 1),
                                            off:off + sz]).then_inc(sd.sem, 16)
                for r in range(NR):
                    s.dma_start(out=bcrow[0:1, CW * r:CW * r + sz],
                                in_=o_bc[r:r + 1, off:off + sz]) \
                        .then_inc(sd.sem, 16)
            s.wait_ge(sv.sem, v_tq(NQ - 1) + NV)
            if with_y:
                poff, psz = TS[NQ - 1]
                for cc in range(16):
                    s.dma_start(out=o_y[128 * cc:128 * (cc + 1),
                                        poff:poff + psz],
                                in_=yq[:, CW * cc:CW * cc + psz]) \
                        .then_inc(sd.sem, 16)
            if o_state is not None:
                s.dma_start(out=o_state, in_=states[:]).then_inc(sd.sem, 16)
            s.wait_ge(sd.sem, d0 + 16 * NDMA)
        sd.n = d0 + 16 * NDMA

        @blk.gpsimd
        def _(g):
            for i, (off, sz) in enumerate(TS):
                if i > 0:
                    g.wait_ge(sv.sem, v_tq(i - 1) + NV)
                for cc in range(16):
                    g.dma_start(out=uq[:, CW * cc:CW * cc + sz],
                                in_=o_u[128 * cc:128 * (cc + 1),
                                        off:off + sz]).then_inc(sgu.sem, 16)
                if not PD:
                    continue
                for n in range(16):
                    par = n % 2
                    g.wait_ge(sv.sem, v_n(i, n, 1))
                    g.wait_ge(ss.sem, s_n(i, n, 1))
                    for dg in range(VD, 16):
                        inst = g.tensor_tensor_scan(
                            out=sn[:, CW * dg:CW * dg + sz],
                            data0=da[par][:, CW * dg:CW * dg + sz],
                            data1=dbu[par][:, CW * dg:CW * dg + sz],
                            initial=states[:, 16 * dg + n:16 * dg + n + 1],
                            op0=ALU.mult, op1=ALU.add)
                    inst.then_inc(sp.sem, 1)
            g.wait_ge(sgu.sem, g0 + 256 * NQ)
        sgu.n = g0 + 256 * NQ
        sp.n = p0 + (16 * NQ if PD else 0)

        @blk.vector
        def _(v):
            vk = [v0]

            def step(inst, k=1):
                inst.then_inc(sv.sem, k)
                vk[0] += k
                v.wait_ge(sv.sem, vk[0])

            v.wait_ge(ss.sem, s0 + 1)
            step(v.tensor_scalar_mul(a_t[:], a_t[:], -1.0))
            if init_ap is None:
                v.memset(states[:], 0.0)
            step(v.memset(onesr[:], 1.0))
            for i, (off, sz) in enumerate(TS):
                v.wait_ge(sd.sem, d0 + 16 * d_del[i])
                v.wait_ge(sgu.sem, g0 + 256 * (i + 1))
                step(v.tensor_mul(duq[:], delq[:], uq[:]))
                if with_y and i > 0:
                    v.wait_ge(sd.sem, d0 + 16 * d_yst[i - 1])
                for n in range(16):
                    par = n % 2
                    v.wait_ge(ss.sem, s_n(i, n, 2))
                    if PD:
                        if n >= 2:
                            v.wait_ge(sp.sem, p_scan(i, n - 2))
                        elif i > 0:
                            v.wait_ge(sp.sem, p_scan(i - 1, n + 14))
                    step(v.tensor_tensor(
                        out=dbu[par][:].rearrange("p (c w) -> p c w", c=16),
                        in0=duq[:].rearrange("p (c w) -> p c w", c=16),
                        in1=bn[:].unsqueeze(1).broadcast_to([128, 16, CW]),
                        op=ALU.mult))
                    v.wait_ge(ss.sem, s_n(i, n, 1))
                    for dg in range(VD):
                        inst = v.tensor_tensor_scan(
                            out=sn[:, CW * dg:CW * dg + sz],
                            data0=da[par][:, CW * dg:CW * dg + sz],
                            data1=dbu[par][:, CW * dg:CW * dg + sz],
                            initial=states[:, 16 * dg + n:16 * dg + n + 1],
                            op0=ALU.mult, op1=ALU.add)
                    step(inst)
                    if PD:
                        v.wait_ge(sp.sem, p_scan(i, n))
                    step(v.tensor_copy(
                        states[:].rearrange("p (c n) -> p c n", c=16)
                        [:, :, n:n + 1],
                        sn[:].rearrange("p (c w) -> p c w", c=16)
                        [:, :, sz - 1:sz]))
                    if with_y:
                        v.wait_ge(ss.sem, s_n(i, n, 3))
                        dst = yq if n == 0 else ym
                        step(v.tensor_tensor(
                            out=dst[:].rearrange("p (c w) -> p c w", c=16),
                            in0=sn[:].rearrange("p (c w) -> p c w", c=16),
                            in1=cn[:].unsqueeze(1).broadcast_to([128, 16, CW]),
                            op=ALU.mult))
                        if n == 0:
                            step(v.tensor_copy(junk[0:1, 0:128],
                                               onesr[0:1, 0:128]))
                        else:
                            step(v.tensor_add(yq[:], yq[:], ym[:]))
                if with_y:
                    for cc in range(16):
                        inst = v.scalar_tensor_tensor(
                            out=yq[:, CW * cc:CW * cc + sz],
                            in0=uq[:, CW * cc:CW * cc + sz],
                            scalar=dp_t[:, cc:cc + 1],
                            in1=yq[:, CW * cc:CW * cc + sz],
                            op0=ALU.mult, op1=ALU.add)
                    step(inst)
        sv.n = v0 + 2 + NV * NQ

        @blk.scalar
        def _(s):
            s.wait_ge(sd.sem, d0 + 16)
            s.activation(out=a_t[:], in_=araw[:], func=AF.Exp) \
                .then_inc(ss.sem, 1)
            for i, (off, sz) in enumerate(TS):
                for n in range(16):
                    par = n % 2
                    if n == 0:
                        s.wait_ge(sd.sem, d0 + 16 * d_del[i])
                        s.wait_ge(sv.sem, v0 + 2)
                    if n >= 2:
                        pi, pn = i, n - 2
                    elif i > 0:
                        pi, pn = i - 1, n + 14
                    else:
                        pi = None
                    if pi is not None:
                        s.wait_ge(sv.sem, v_n(pi, pn, 2))
                        if PD:
                            s.wait_ge(sp.sem, p_scan(pi, pn))
                    for dg in range(16):
                        inst = s.activation(
                            out=da[par][:, CW * dg:CW * dg + sz],
                            in_=delq[:, CW * dg:CW * dg + sz], func=AF.Exp,
                            scale=a_t[:, 16 * dg + n:16 * dg + n + 1])
                    inst.then_inc(ss.sem, 1)
                    s.wait_ge(stt.sem, t_n(i, n, 1))
                    if n > 0:
                        s.wait_ge(sv.sem, v_n(i, n - 1, 1))
                    elif i > 0:
                        s.wait_ge(sv.sem, v_n(i - 1, 15, 1))
                    s.activation(out=bn[:, 0:sz], in_=pb[:, 0:sz],
                                 func=AF.Copy).then_inc(ss.sem, 1)
                    if with_y:
                        s.wait_ge(stt.sem, t_n(i, n, 2))
                        if n > 0:
                            s.wait_ge(sv.sem, v_n(i, n - 1, 4))
                        elif i > 0:
                            s.wait_ge(sv.sem, v_n(i - 1, 15, 4))
                        s.activation(out=cn[:, 0:sz], in_=pc[:, 0:sz],
                                     func=AF.Copy).then_inc(ss.sem, 1)
        ss.n = s0 + 1 + NS * NQ

        @blk.tensor
        def _(t):
            t.wait_ge(sv.sem, v0 + 2)
            for i, (off, sz) in enumerate(TS):
                t.wait_ge(sd.sem, d0 + 16 * d_del[i])
                for n in range(16):
                    if n > 0 or i > 0:
                        ps_ = s_n(i, n - 1, 2) if n > 0 else s_tq(i - 1) + NS
                        t.wait_ge(ss.sem, ps_)
                    t.matmul(pb[:, 0:sz], onesr[:],
                             bcrow[0:1, CW * n:CW * n + sz], start=True,
                             stop=True).then_inc(stt.sem, 1)
                    if with_y:
                        if n > 0 or i > 0:
                            ps_ = s_n(i, n - 1, 3) if n > 0 else \
                                s_tq(i - 1) + NS
                            t.wait_ge(ss.sem, ps_)
                        t.matmul(pc[:, 0:sz], onesr[:],
                                 bcrow[0:1, CW * (16 + n):CW * (16 + n) + sz],
                                 start=True, stop=True).then_inc(stt.sem, 1)
        stt.n = t0 + NT * NQ

    sc.close()



def build_launch1(T):
    Th = T + HALO
    b = B()
    x_ap = b.din("x_dmaj", [1024, Th])
    gw_ap = b.din("gate_w", [1024, 8])
    id_ap = b.din("ident", [128, 128])
    w1_ap = b.din("w1", [8, 1024, 4096])
    w2_ap = b.din("w2", [8, 4096, 1024])
    win_ap = b.din("win", [1024, 4096])
    diag_ap = b.din("conv_diag", [16, 4, 128, 128])
    bconv_ap = b.din("bconv_dm", [128, 16])
    wx_ap = b.din("wx", [2048, 96])
    wdt_ap = b.din("wdt", [64, 2048])
    bdt_ap = b.din("bdt_dm", [128, 16])
    a_ap = b.din("a_dm", [128, 256])
    o_resid1 = b.dout("o_resid1", [1024, T])
    o_u = b.dout("o_u", [2048, T], BF16)
    o_delta = b.dout("o_delta", [2048, T])
    o_zsilu = b.dout("o_zsilu", [2048, T], BF16)
    o_bc = b.dout("o_bc", [32, T])
    o_dsum = b.dout("o_dsum", [128, 16])
    o_state = b.dout("o_state", [128, 256])
    cwT = b.nc.dram_tensor("cwT", [8, Th], F32, kind="Internal").ap()

    with b.es:
        b.init_consts([EPS, 1.0 / D, -0.5, -1.0])
        ident = b.sb("ident_sb", [128, 128])
        phase_load_dmaj(b, id_ap, [ident], 128, tag="ldi")
        with contextlib.ExitStack() as sc1:
            resid = [b.sb(f"res{g}", [128, Th], scope=sc1) for g in range(8)]
            xlb = [b.sb(f"xlb{g}", [128, Th], BF16, scope=sc1)
                   for g in range(8)]
            phase_load_dmaj(b, x_ap, resid, Th, tag="ldx")
            with contextlib.ExitStack() as sc2:
                xlf = [b.sb(f"xlf{g}", [128, Th], scope=sc2) for g in range(8)]
                phase_layernorm(b, resid, Th, out_bf=xlb, out_f32=xlf,
                                tag="ln1")
                phase_gating(b, xlf, Th, gw_ap, ident, cwT, tag="gt")
            phase_moe(b, xlb, resid, Th, w1_ap, w2_ap, cwT, scale_out=0.5,
                      tag="moe")
            phase_layernorm(b, resid, Th, out_bf=xlb, tag="ln2")
            phase_store_dmaj(b, resid, o_resid1, T, tag="str1", col0=HALO)
            phase_mamba_front(b, xlb, T, Th, win_ap, diag_ap, bconv_ap,
                              wx_ap, wdt_ap, bdt_ap, o_u, o_delta, o_zsilu,
                              o_bc, o_dsum, tag="mf")
        phase_scan(b, T, o_delta, o_u, o_bc, a_ap, None, None, None,
                   o_state, tag="sc")
    return b


def host_prep_l0(inp, T):
    w = {}
    w["gate_w"] = np.ascontiguousarray(inp["L0_ffn_gate_w"])
    w["w1"] = np.ascontiguousarray(inp["L0_ffn_w1"])
    w["w2"] = np.ascontiguousarray(inp["L0_ffn_w2"])
    w["ident"] = np.eye(128, dtype=np.float32)
    w["win"] = np.ascontiguousarray(inp["L0_attn_Win"])
    wc = np.asarray(inp["L0_attn_Wconv"])
    diag = np.zeros((16, 4, 128, 128), dtype=np.float32)
    for cc in range(16):
        for k in range(4):
            np.fill_diagonal(diag[cc, k], wc[128 * cc:128 * (cc + 1), 0, k])
    w["conv_diag"] = diag
    w["bconv_dm"] = np.ascontiguousarray(
        np.asarray(inp["L0_attn_bconv"]).reshape(16, 128).T)
    w["wx"] = np.ascontiguousarray(inp["L0_attn_Wx"])
    w["wdt"] = np.ascontiguousarray(inp["L0_attn_Wdt"])
    w["bdt_dm"] = np.ascontiguousarray(
        np.asarray(inp["L0_attn_bdt"]).reshape(16, 128).T)
    w["a_dm"] = np.ascontiguousarray(
        np.asarray(inp["L0_attn_A_log"]).reshape(16, 128, 16)
        .transpose(1, 0, 2).reshape(128, 256))
    w["dp_dm"] = np.ascontiguousarray(
        np.asarray(inp["L0_attn_Dp"]).reshape(16, 128).T)
    w["wout"] = np.ascontiguousarray(inp["L0_attn_Wout"])
    return w



def phase_chain(b, ch_dsum_ap, ch_state_ap, msel_ap, a_dm_ap, sin_hbm,
                tag="chn"):
    nc = b.nc
    sc = contextlib.ExitStack()
    _sb = lambda n, sh, dt=F32: b.sb(tag + n, sh, dt, scope=sc)
    dsums = _sb("dsums", [128, 4 * 16])
    stats = _sb("stats", [128, 4 * 256])
    msel = _sb("msel", [128, 4])
    araw = _sb("araw", [128, 256])
    a_t = _sb("a_t", [128, 256])
    pk = _sb("pk", [128, 256])
    srun = _sb("srun", [128, 256])
    tmp = _sb("tmp", [128, 256])
    sin = _sb("sin", [128, 256])
    sd, sv, ss = b.sem("d"), b.sem("v"), b.sem("s")
    d0, v0, s0 = sd.n, sv.n, ss.n

    with nc.Block() as blk:
        @blk.sync
        def _(s):
            for k in range(4):
                s.dma_start(out=dsums[:, 16 * k:16 * (k + 1)],
                            in_=ch_dsum_ap[k]).then_inc(sd.sem, 16)
                s.dma_start(out=stats[:, 256 * k:256 * (k + 1)],
                            in_=ch_state_ap[k]).then_inc(sd.sem, 16)
            s.dma_start(out=msel[:], in_=msel_ap).then_inc(sd.sem, 16)
            s.dma_start(out=araw[:], in_=a_dm_ap).then_inc(sd.sem, 16)
            s.wait_ge(sd.sem, d0 + 16 * 10)
        sd.n = d0 + 16 * 10

        @blk.scalar
        def _(s):
            s.wait_ge(sd.sem, d0 + 16 * 10)
            s.activation(out=a_t[:], in_=araw[:], func=AF.Exp) \
                .then_inc(ss.sem, 1)
            for k in range(4):
                s.wait_ge(sv.sem, v0 + 2 if k == 0 else v0 + 5 * k)
                s.activation(out=pk[:], in_=pk[:], func=AF.Exp) \
                    .then_inc(ss.sem, 1)
        ss.n = s0 + 5

        @blk.vector
        def _(v):
            vk = [v0]

            def step(inst):
                inst.then_inc(sv.sem, 1)
                vk[0] += 1
                v.wait_ge(sv.sem, vk[0])

            v.wait_ge(ss.sem, s0 + 1)
            step(v.tensor_scalar_mul(a_t[:], a_t[:], -1.0))
            for k in range(4):
                step(v.tensor_tensor(
                    out=pk[:].rearrange("p (c n) -> p c n", c=16),
                    in0=a_t[:].rearrange("p (c n) -> p c n", c=16),
                    in1=dsums[:, 16 * k:16 * (k + 1)].unsqueeze(2)
                    .broadcast_to([128, 16, 16]), op=ALU.mult))
                v.wait_ge(ss.sem, s0 + 2 + k)
                if k == 0:
                    step(v.tensor_copy(srun[:], stats[:, 0:256]))
                else:
                    step(v.tensor_mul(tmp[:], pk[:], srun[:]))
                    step(v.tensor_add(srun[:], tmp[:],
                                      stats[:, 256 * k:256 * (k + 1)]))
                if k == 0:
                    step(v.tensor_scalar_mul(sin[:], srun[:],
                                             msel[:, 0:1]))
                else:
                    step(v.tensor_scalar_mul(tmp[:], srun[:],
                                             msel[:, k:k + 1]))
                    step(v.tensor_add(sin[:], sin[:], tmp[:]))
        sv.n = v0 + 1 + 3 + 5 * 3

        @blk.gpsimd
        def _(g):
            g.wait_ge(sv.sem, v0 + 1 + 3 + 5 * 3)
            g.dma_start(out=sin_hbm, in_=sin[:]).then_inc(sd.sem, 16)
            g.wait_ge(sd.sem, d0 + 16 * 11)
        sd.n = d0 + 16 * 11

    sc.close()



def phase_attn_tail(b, resid, T, y_hbm, zs_ap, wout_ap, tag="at"):
    nc = b.nc
    sc = contextlib.ExitStack()
    _sb = lambda n, sh, dt=F32: b.sb(tag + n, sh, dt, scope=sc)
    _ps = lambda n, sh: b.ps(tag + n, sh, scope=sc)
    CWt = 256
    woutb = [_sb(f"wo{cc}", [128, 1024], BF16) for cc in range(16)]
    ytq = _sb("ytq", [128, 16 * CWt])
    ztq = _sb("ztq", [128, 16 * CWt])
    yz = _sb("yz", [128, 16 * CWt], BF16)
    po = [_ps(f"po{k}", [128, CWt]) for k in range(2)]
    sd, sv, stt = b.sem("d"), b.sem("v"), b.sem("t")
    sgu = b.sem("g0")
    d0, v0, t0 = sd.n, sv.n, stt.n
    g0 = sgu.n
    TQ = fsplit(T, CWt)
    NJ = len(TQ)

    def v_tq(i):
        return v0 + 9 * i

    def t_tq(i):
        return t0 + 8 * i

    with nc.Block() as blk:
        @blk.sync
        def _(s):
            for i, (off, sz) in enumerate(TQ):
                if i > 0:
                    s.wait_ge(sv.sem, v_tq(i - 1) + 1)
                for cc in range(16):
                    s.dma_start(out=ytq[:, CWt * cc:CWt * cc + sz],
                                in_=y_hbm[128 * cc:128 * (cc + 1),
                                          off:off + sz]).then_inc(sd.sem, 16)
            s.wait_ge(sd.sem, d0 + 16 * 16 * NJ)
        sd.n = d0 + 16 * 16 * NJ

        @blk.gpsimd
        def _(g):
            for cc in range(16):
                g.dma_start(out=woutb[cc][:],
                            in_=wout_ap[128 * cc:128 * (cc + 1), :]) \
                    .then_inc(sgu.sem, 16)
            for i, (off, sz) in enumerate(TQ):
                if i > 0:
                    g.wait_ge(sv.sem, v_tq(i - 1) + 1)
                for cc in range(16):
                    g.dma_start(out=ztq[:, CWt * cc:CWt * cc + sz],
                                in_=zs_ap[128 * cc:128 * (cc + 1),
                                          off:off + sz]).then_inc(sgu.sem, 16)
            g.wait_ge(sgu.sem, g0 + 16 * 16 * (NJ + 1))
        sgu.n = g0 + 16 * 16 * (NJ + 1)

        @blk.vector
        def _(v):
            vk = [v0]

            def step(inst):
                inst.then_inc(sv.sem, 1)
                vk[0] += 1
                v.wait_ge(sv.sem, vk[0])

            for i, (off, sz) in enumerate(TQ):
                v.wait_ge(sd.sem, d0 + 16 * 16 * (i + 1))
                v.wait_ge(sgu.sem, g0 + 16 * 16 * (i + 2))
                if i > 0:
                    v.wait_ge(stt.sem, t_tq(i - 1) + 8)
                step(v.tensor_mul(yz[:], ytq[:], ztq[:]))
                for doc in range(8):
                    v.wait_ge(stt.sem, t_tq(i) + doc + 1)
                    step(v.tensor_add(resid[doc][:, off:off + sz],
                                      resid[doc][:, off:off + sz],
                                      po[doc % 2][:, 0:sz]))
        sv.n = v0 + 9 * NJ

        @blk.tensor
        def _(t):
            t.wait_ge(sgu.sem, g0 + 16 * 16)
            for i, (off, sz) in enumerate(TQ):
                t.wait_ge(sv.sem, v_tq(i) + 1)
                for doc in range(8):
                    if i > 0 or doc >= 2:
                        pi, pd = (i, doc - 2) if doc >= 2 else (i - 1, 6 + doc)
                        t.wait_ge(sv.sem, v_tq(pi) + 1 + pd + 1)
                    for cc in range(16):
                        inst = t.matmul(po[doc % 2][:, 0:sz],
                                        woutb[cc][:, 128 * doc:128 * (doc + 1)],
                                        yz[:, CWt * cc:CWt * cc + sz],
                                        start=(cc == 0), stop=(cc == 15))
                    inst.then_inc(stt.sem, 1)
        stt.n = t0 + 8 * NJ

    sc.close()



def build_launch2(T):
    b = B()
    r1_ap = b.din("resid1", [1024, T])
    u_ap = b.din("d_u", [2048, T], BF16)
    dl_ap = b.din("d_delta", [2048, T])
    zs_ap = b.din("d_zsilu", [2048, T], BF16)
    bc_ap = b.din("d_bc", [32, T])
    a_ap = b.din("a_dm", [128, 256])
    dp_ap = b.din("dp_dm", [128, 16])
    chd_ap = b.din("ch_dsum", [4, 128, 16])
    chs_ap = b.din("ch_state", [4, 128, 256])
    msel_ap = b.din("msel", [128, 4])
    wout_ap = b.din("wout", [2048, 1024])
    gw_ap = b.din("gate_w", [1024, 8])
    id_ap = b.din("ident", [128, 128])
    w1_ap = b.din("w1", [8, 1024, 4096])
    w2_ap = b.din("w2", [8, 4096, 1024])
    o_resid3 = b.dout("o_resid3", [1024, T])
    o_part = b.dout("o_part", [128, 8])
    sin_hbm = b.nc.dram_tensor("sin_hbm", [128, 256], F32, kind="Internal").ap()
    y_hbm = b.nc.dram_tensor("y_hbm", [2048, T], F32, kind="Internal").ap()
    cwT = b.nc.dram_tensor("cwT2", [8, T], F32, kind="Internal").ap()

    with b.es:
        b.init_consts([EPS, 1.0 / D, -0.5, -1.0])
        ident = b.sb("ident_sb", [128, 128])
        phase_load_dmaj(b, id_ap, [ident], 128, tag="ldi")
        phase_chain(b, chd_ap, chs_ap, msel_ap, a_ap, sin_hbm, tag="chn")
        phase_scan(b, T, dl_ap, u_ap, bc_ap, a_ap, dp_ap, sin_hbm, y_hbm,
                   None, tag="sc")
        with contextlib.ExitStack() as sc1:
            resid = [b.sb(f"res{g}", [128, T], scope=sc1) for g in range(8)]
            phase_load_dmaj(b, r1_ap, resid, T, tag="ldr")
            phase_attn_tail(b, resid, T, y_hbm, zs_ap, wout_ap, tag="at")
            xlb = [b.sb(f"xlb{g}", [128, T], BF16, scope=sc1)
                   for g in range(8)]
            with contextlib.ExitStack() as sc2:
                xlf = [b.sb(f"xlf{g}", [128, T], scope=sc2) for g in range(8)]
                phase_layernorm(b, resid, T, out_bf=xlb, out_f32=xlf,
                                tag="ln3")
                phase_gating(b, xlf, T, gw_ap, ident, cwT, tag="gt2")
            phase_moe(b, xlb, resid, T, w1_ap, w2_ap, cwT, scale_out=0.5,
                      tag="mo2")
            with contextlib.ExitStack() as sc3:
                xlf2 = [b.sb(f"xl4{g}", [128, T], scope=sc3) for g in range(8)]
                phase_layernorm(b, resid, T, out_f32=xlf2, tag="ln4")
                phase_partials(b, xlf2, T, o_part, tag="pp")
            phase_store_dmaj(b, resid, o_resid3, T, tag="str3")
    return b


def phase_partials(b, xlf, T, o_part, tag="pp"):
    nc = b.nc
    sc = contextlib.ExitStack()
    part = b.sb(tag + "part", [128, 8], scope=sc)
    sv, sd = b.sem("v"), b.sem("d")
    v0, d0 = sv.n, sd.n
    with nc.Block() as blk:
        @blk.vector
        def _(v):
            for g in range(8):
                inst = v.reduce_sum(part[:, g:g + 1], xlf[g][:, 0:T], axis=AXX)
            inst.then_inc(sv.sem, 1)
        sv.n = v0 + 1

        @blk.sync
        def _(s):
            s.wait_ge(sv.sem, v0 + 1)
            s.dma_start(out=o_part, in_=part[:]).then_inc(sd.sem, 16)
            s.wait_ge(sd.sem, d0 + 16)
        sd.n = d0 + 16
    sc.close()



def phase_gf(b, parts_ap, bsel_ap, gcw, ar_ap, c0_t, c1_t, nbtok, tag="gf"):
    nc = b.nc
    sc = contextlib.ExitStack()
    _sb = lambda n, sh, dt=F32: b.sb(tag + n, sh, dt, scope=sc)
    _ps = lambda n, sh: b.ps(tag + n, sh, scope=sc)
    gc_w1_ap, gc_b1_ap, gc_w2_ap, gc_b2_ap, gc_w3_ap, gc_b3_ap = gcw
    parts = _sb("parts", [128, 64])
    bsel = _sb("bsel", [128, 8])
    mean = _sb("mean", [128, 8])
    tmp = _sb("tmp", [128, 8])
    gw1 = _sb("gw1", [128, 8 * 512])
    gw2 = _sb("gw2", [128, 4 * 256])
    gw3 = _sb("gw3", [128, 2])
    gb1 = _sb("gb1", [128, 4])
    gb2 = _sb("gb2", [128, 2])
    b3ar = _sb("b3ar", [1, 8])
    h1 = _sb("h1", [128, 4])
    h2 = _sb("h2", [128, 2])
    sc1 = _sb("sc1", [1, 8])
    ph = _ps("ph", [128, 8])
    p3 = _ps("p3", [1, 8])
    sd, sv, ss, stt = b.sem("d"), b.sem("v"), b.sem("s"), b.sem("t")
    d0, v0, s0, t0 = sd.n, sv.n, ss.n, stt.n

    with nc.Block() as blk:
        @blk.sync
        def _(s):
            for j in range(8):
                s.dma_start(out=parts[:, 8 * j:8 * (j + 1)], in_=parts_ap[j]) \
                    .then_inc(sd.sem, 16)
            s.dma_start(out=bsel[:], in_=bsel_ap).then_inc(sd.sem, 16)
            s.dma_start(out=gb1[:], in_=gc_b1_ap).then_inc(sd.sem, 16)
            s.dma_start(out=gb2[:], in_=gc_b2_ap).then_inc(sd.sem, 16)
            s.dma_start(out=b3ar[0:1, 0:1], in_=gc_b3_ap).then_inc(sd.sem, 16)
            s.dma_start(out=b3ar[0:1, 1:2], in_=ar_ap).then_inc(sd.sem, 16)
            for dc in range(8):
                s.dma_start(out=gw1[:, 512 * dc:512 * (dc + 1)],
                            in_=gc_w1_ap[128 * dc:128 * (dc + 1), :]) \
                    .then_inc(sd.sem, 16)
            for dc in range(4):
                s.dma_start(out=gw2[:, 256 * dc:256 * (dc + 1)],
                            in_=gc_w2_ap[128 * dc:128 * (dc + 1), :]) \
                    .then_inc(sd.sem, 16)
            for dc in range(2):
                s.dma_start(out=gw3[:, dc:dc + 1],
                            in_=gc_w3_ap[128 * dc:128 * (dc + 1), :]) \
                    .then_inc(sd.sem, 16)
            s.wait_ge(sd.sem, d0 + 16 * 27)
        sd.n = d0 + 16 * 27

        @blk.vector
        def _(v):
            vk = [v0]

            def step(inst):
                inst.then_inc(sv.sem, 1)
                vk[0] += 1
                v.wait_ge(sv.sem, vk[0])

            v.wait_ge(sd.sem, d0 + 16 * 27)
            step(v.tensor_scalar_mul(mean[:], parts[:, 0:8], bsel[:, 0:1]))
            for j in range(1, 8):
                step(v.tensor_scalar_mul(tmp[:], parts[:, 8 * j:8 * (j + 1)],
                                         bsel[:, j:j + 1]))
                step(v.tensor_add(mean[:], mean[:], tmp[:]))
            step(v.tensor_scalar_mul(mean[:], mean[:], 1.0 / nbtok))
            step(v.tensor_scalar_mul(c1_t[0:1, 0:1], b3ar[0:1, 1:2], 0.6))
            step(v.tensor_scalar_mul(sc1[0:1, 0:1], b3ar[0:1, 1:2], 2.0))
            v.wait_ge(ss.sem, s0 + 4)
            step(v.tensor_scalar(out=tmp[0:1, 0:1], in0=b3ar[0:1, 2:3],
                                 scalar1=0.7, scalar2=-0.5, op0=ALU.mult,
                                 op1=ALU.add))
            step(v.tensor_mul(tmp[0:1, 1:2], tmp[0:1, 0:1], sc1[0:1, 0:1]))
            step(v.tensor_scalar_add(c0_t[0:1, 0:1], tmp[0:1, 1:2], 0.1))
            step(v.tensor_copy(c0_t[0:1, 1:2], b3ar[0:1, 2:3]))
            step(v.tensor_copy(c0_t[0:1, 2:3], mean[0:1, 0:1]))
            step(v.tensor_copy(c0_t[0:1, 3:4], h1[0:1, 0:1]))
        NV = 24
        sv.n = v0 + NV

        @blk.tensor
        def _(t):
            t.wait_ge(sv.sem, v0 + 16)
            for hc in range(4):
                for dc in range(8):
                    inst = t.matmul(ph[:, 2 * hc:2 * hc + 1],
                                    gw1[:, 512 * dc + 128 * hc:
                                        512 * dc + 128 * (hc + 1)],
                                    mean[:, dc:dc + 1], start=(dc == 0),
                                    stop=(dc == 7))
            inst.then_inc(stt.sem, 1)
            t.wait_ge(ss.sem, s0 + 1)
            for hc in range(2):
                for dc in range(4):
                    inst = t.matmul(ph[:, 2 * hc:2 * hc + 1],
                                    gw2[:, 256 * dc + 128 * hc:
                                        256 * dc + 128 * (hc + 1)],
                                    h1[:, dc:dc + 1], start=(dc == 0),
                                    stop=(dc == 3))
            inst.then_inc(stt.sem, 1)
            t.wait_ge(ss.sem, s0 + 2)
            for dc in range(2):
                inst = t.matmul(p3[0:1, 0:1], gw3[:, dc:dc + 1],
                                h2[:, dc:dc + 1], start=(dc == 0),
                                stop=(dc == 1))
            inst.then_inc(stt.sem, 1)
        stt.n = t0 + 3

        @blk.scalar
        def _(s):
            sk = [s0]

            def sstep(inst):
                inst.then_inc(ss.sem, 1)
                sk[0] += 1
                s.wait_ge(ss.sem, sk[0])

            s.wait_ge(stt.sem, t0 + 1)
            for hc in range(4):
                inst = s.activation(out=h1[:, hc:hc + 1],
                                    in_=ph[:, 2 * hc:2 * hc + 1],
                                    func=AF.Gelu, bias=gb1[:, hc:hc + 1])
            sstep(inst)
            s.wait_ge(stt.sem, t0 + 2)
            for hc in range(2):
                inst = s.activation(out=h2[:, hc:hc + 1],
                                    in_=ph[:, 2 * hc:2 * hc + 1],
                                    func=AF.Gelu, bias=gb2[:, hc:hc + 1])
            sstep(inst)
            s.wait_ge(stt.sem, t0 + 3)
            sstep(s.activation(out=b3ar[0:1, 3:4], in_=p3[0:1, 0:1],
                               func=AF.Copy))
            sstep(s.activation(out=b3ar[0:1, 2:3], in_=b3ar[0:1, 3:4],
                               func=AF.Sigmoid, bias=b3ar[0:1, 0:1]))
        ss.n = s0 + 4

    sc.close()



TW_SRC = [4, 5, 6, 7, 0, 1, 2, 3]
TW_SGN = [1.0, 1.0, -1.0, -1.0, -1.0, -1.0, 1.0, 1.0]


def phase_coupling(b, outA, outbf, resid, Tc, cnw, c0_t, c1_t,
                   tag="cp", CW=512):
    nc = b.nc
    sc = contextlib.ExitStack()
    _sb = lambda n, sh, dt=F32: b.sb(tag + n, sh, dt, scope=sc)
    _ps = lambda n, sh: b.ps(tag + n, sh, scope=sc)
    (w1_ap, b1_ap, w2_ap, b2_ap, w3_ap, b3_ap, w4_ap, b4_ap) = cnw
    cw1 = [_sb(f"w1_{dc}", [128, 1024], BF16) for dc in range(8)]
    cw2 = [_sb(f"w2_{dc}", [128, 512], BF16) for dc in range(8)]
    cw3 = [_sb(f"w3_{dc}", [128, 256], BF16) for dc in range(4)]
    cw4 = [_sb(f"w4_{dc}", [128, 1], BF16) for dc in range(2)]
    cb1 = _sb("cb1", [128, 8])
    cb2 = _sb("cb2", [128, 4])
    cb3 = _sb("cb3", [128, 2])
    cb4 = _sb("cb4", [1, 1])
    h1b = [_sb(f"h1b{k}", [128, CW], BF16) for k in range(8)]
    h2b = [_sb(f"h2b{k}", [128, CW], BF16) for k in range(4)]
    h3b = [_sb(f"h3b{k}", [128, CW], BF16) for k in range(2)]
    tfrow = _sb("tfrow", [1, CW])
    crow = _sb("crow", [1, CW])
    cb_t = _sb("cb_t", [128, CW])
    tmp = _sb("tmp", [128, CW])
    tmp2 = _sb("tmp2", [128, CW])
    onesr = _sb("onesr", [1, 128])
    ph = [_ps(f"ph{k}", [128, CW]) for k in range(2)]
    p4 = _ps("p4", [1, CW])
    pcb = _ps("pcb", [128, CW])

    sd, sv, ss, stt = b.sem("d"), b.sem("v"), b.sem("s"), b.sem("t")
    sg_ = b.sem("g0")
    d0, v0, s0, t0, g0 = sd.n, sv.n, ss.n, stt.n, sg_.n
    TQ = fsplit(Tc, CW)
    NJ = len(TQ)

    NTq, NSq, NVq = 16, 16, 25

    def t_q(c, j):
        return t0 + NTq * (NJ * c + j)

    def s_q(c, j):
        return s0 + NSq * (NJ * c + j)

    def v_q(c, j):
        return v0 + 9 + NVq * (NJ * c + j)

    with nc.Block() as blk:
        @blk.gpsimd
        def _(g):
            for dc in range(8):
                g.dma_start(out=cw1[dc][:],
                            in_=w1_ap[128 * dc:128 * (dc + 1), :]) \
                    .then_inc(sg_.sem, 16)
            for dc in range(8):
                g.dma_start(out=cw2[dc][:],
                            in_=w2_ap[128 * dc:128 * (dc + 1), :]) \
                    .then_inc(sg_.sem, 16)
            for dc in range(4):
                g.dma_start(out=cw3[dc][:],
                            in_=w3_ap[128 * dc:128 * (dc + 1), :]) \
                    .then_inc(sg_.sem, 16)
            for dc in range(2):
                g.dma_start(out=cw4[dc][:],
                            in_=w4_ap[128 * dc:128 * (dc + 1), :]) \
                    .then_inc(sg_.sem, 16)
            g.wait_ge(sg_.sem, g0 + 16 * 22)
        sg_.n = g0 + 16 * 22

        @blk.sync
        def _(s):
            s.dma_start(out=cb1[:], in_=b1_ap).then_inc(sd.sem, 16)
            s.dma_start(out=cb2[:], in_=b2_ap).then_inc(sd.sem, 16)
            s.dma_start(out=cb3[:], in_=b3_ap).then_inc(sd.sem, 16)
            s.dma_start(out=cb4[:], in_=b4_ap).then_inc(sd.sem, 16)
            s.wait_ge(sd.sem, d0 + 16 * 4)
        sd.n = d0 + 16 * 4

        @blk.vector
        def _(v):
            vk = [v0]

            def step(inst):
                inst.then_inc(sv.sem, 1)
                vk[0] += 1
                v.wait_ge(sv.sem, vk[0])

            v.memset(onesr[:], 1.0).then_inc(sv.sem, 1)
            vk[0] += 1
            for g_ in range(8):
                inst = v.tensor_copy(outbf[g_][:, 0:Tc], outA[g_][:, 0:Tc])
                inst.then_inc(sv.sem, 1)
                vk[0] += 1
            v.wait_ge(sv.sem, vk[0])
            for c in range(3):
                for j, (off, sz) in enumerate(TQ):
                    v.wait_ge(ss.sem, s_q(c, j) + 15)
                    step(v.tensor_scalar(out=crow[0:1, 0:sz],
                                         in0=tfrow[0:1, 0:sz],
                                         scalar1=c1_t[0:1, 0:1],
                                         scalar2=c0_t[0:1, 0:1],
                                         op0=ALU.mult, op1=ALU.add))
                    v.wait_ge(ss.sem, s_q(c, j) + 16)
                    for pg in range(4):
                        step(v.tensor_mul(tmp[:, 0:sz], cb_t[:, 0:sz],
                                          outA[pg + 4][:, off:off + sz]))
                        step(v.tensor_mul(tmp2[:, 0:sz], cb_t[:, 0:sz],
                                          outA[pg][:, off:off + sz]))
                        step(v.scalar_tensor_tensor(
                            out=outA[pg][:, off:off + sz], in0=tmp[:, 0:sz],
                            scalar=TW_SGN[pg], in1=outA[pg][:, off:off + sz],
                            op0=ALU.mult, op1=ALU.add))
                        step(v.tensor_copy(outbf[pg][:, off:off + sz],
                                           outA[pg][:, off:off + sz]))
                        step(v.scalar_tensor_tensor(
                            out=outA[pg + 4][:, off:off + sz],
                            in0=tmp2[:, 0:sz], scalar=TW_SGN[pg + 4],
                            in1=outA[pg + 4][:, off:off + sz],
                            op0=ALU.mult, op1=ALU.add))
                        step(v.tensor_copy(outbf[pg + 4][:, off:off + sz],
                                           outA[pg + 4][:, off:off + sz]))
            for g_ in range(8):
                step(v.scalar_tensor_tensor(
                    out=resid[g_][:, 0:Tc], in0=outA[g_][:, 0:Tc], scalar=0.5,
                    in1=resid[g_][:, 0:Tc], op0=ALU.mult, op1=ALU.add))
        sv.n = v0 + 9 + NVq * 3 * NJ + 8

        @blk.tensor
        def _(t):
            t.wait_ge(sg_.sem, g0 + 16 * 22)
            t.wait_ge(sv.sem, v0 + 9)
            for c in range(3):
                for j, (off, sz) in enumerate(TQ):
                    tb = t_q(c, j)
                    if c > 0 or j > 0:
                        pass
                    for hc in range(8):
                        if c > 0 or j > 0:
                            pc_, pj = (c, j - 1) if j > 0 else (c - 1, NJ - 1)
                            t.wait_ge(ss.sem, s_q(pc_, pj) + hc % 2 + 15)
                        if c > 0:
                            t.wait_ge(sv.sem, v_q(c - 1, j) + NVq)
                        for dc in range(8):
                            inst = t.matmul(ph[hc % 2][:, 0:sz],
                                            cw1[dc][:, 128 * hc:128 * (hc + 1)],
                                            outbf[dc][:, off:off + sz],
                                            start=(dc == 0), stop=(dc == 7))
                        inst.then_inc(stt.sem, 1)
                        t.wait_ge(ss.sem, s_q(c, j) + hc + 1)
                    for hc in range(4):
                        for dc in range(8):
                            inst = t.matmul(ph[hc % 2][:, 0:sz],
                                            cw2[dc][:, 128 * hc:128 * (hc + 1)],
                                            h1b[dc][:, 0:sz],
                                            start=(dc == 0), stop=(dc == 7))
                        inst.then_inc(stt.sem, 1)
                        t.wait_ge(ss.sem, s_q(c, j) + 8 + hc + 1)
                    for hc in range(2):
                        for dc in range(4):
                            inst = t.matmul(ph[hc % 2][:, 0:sz],
                                            cw3[dc][:, 128 * hc:128 * (hc + 1)],
                                            h2b[dc][:, 0:sz],
                                            start=(dc == 0), stop=(dc == 3))
                        inst.then_inc(stt.sem, 1)
                        t.wait_ge(ss.sem, s_q(c, j) + 12 + hc + 1)
                    for dc in range(2):
                        inst = t.matmul(p4[0:1, 0:sz], cw4[dc][:],
                                        h3b[dc][:, 0:sz], start=(dc == 0),
                                        stop=(dc == 1))
                    inst.then_inc(stt.sem, 1)
                    t.wait_ge(ss.sem, s_q(c, j) + 15)
                    t.wait_ge(sv.sem, v_q(c, j) + 1)
                    t.matmul(pcb[:, 0:sz], onesr[:], crow[0:1, 0:sz],
                             start=True, stop=True).then_inc(stt.sem, 1)
                    t.wait_ge(ss.sem, s_q(c, j) + 16)
        stt.n = t0 + NTq * 3 * NJ

        @blk.scalar
        def _(s):
            s.wait_ge(sd.sem, d0 + 16 * 4)
            for c in range(3):
                for j, (off, sz) in enumerate(TQ):
                    for hc in range(8):
                        s.wait_ge(stt.sem, t_q(c, j) + hc + 1)
                        s.activation(out=h1b[hc][:, 0:sz],
                                     in_=ph[hc % 2][:, 0:sz], func=AF.Gelu,
                                     bias=cb1[:, hc:hc + 1]).then_inc(ss.sem, 1)
                    for hc in range(4):
                        s.wait_ge(stt.sem, t_q(c, j) + 8 + hc + 1)
                        s.activation(out=h2b[hc][:, 0:sz],
                                     in_=ph[hc % 2][:, 0:sz], func=AF.Gelu,
                                     bias=cb2[:, hc:hc + 1]).then_inc(ss.sem, 1)
                    for hc in range(2):
                        s.wait_ge(stt.sem, t_q(c, j) + 12 + hc + 1)
                        s.activation(out=h3b[hc][:, 0:sz],
                                     in_=ph[hc % 2][:, 0:sz], func=AF.Gelu,
                                     bias=cb3[:, hc:hc + 1]).then_inc(ss.sem, 1)
                    s.wait_ge(stt.sem, t_q(c, j) + 15)
                    s.activation(out=tfrow[0:1, 0:sz], in_=p4[0:1, 0:sz],
                                 func=AF.Sigmoid, bias=cb4[0:1, 0:1]) \
                        .then_inc(ss.sem, 1)
                    s.wait_ge(stt.sem, t_q(c, j) + 16)
                    s.activation(out=cb_t[:, 0:sz], in_=pcb[:, 0:sz],
                                 func=AF.Copy).then_inc(ss.sem, 1)
        ss.n = s0 + NSq * 3 * NJ

    sc.close()



def build_launch4(T, nbtok=8192.0):
    b = B()
    r5_ap = b.din("resid5", [1024, T])
    parts_ap = b.din("parts", [8, 128, 8])
    bsel_ap = b.din("bsel", [128, 8])
    gw1_ap = b.din("gc_w1", [1024, 512])
    gb1_ap = b.din("gc_b1_dm", [128, 4])
    gw2_ap = b.din("gc_w2", [512, 256])
    gb2_ap = b.din("gc_b2_dm", [128, 2])
    gw3_ap = b.din("gc_w3", [256, 1])
    gb3_ap = b.din("gc_b3", [1, 1])
    ar_ap = b.din("ar", [1, 1])
    cw1_ap = b.din("cn_w1", [1024, 1024])
    cb1_ap = b.din("cn_b1_dm", [128, 8])
    cw2_ap = b.din("cn_w2", [1024, 512])
    cb2_ap = b.din("cn_b2_dm", [128, 4])
    cw3_ap = b.din("cn_w3", [512, 256])
    cb3_ap = b.din("cn_b3_dm", [128, 2])
    cw4_ap = b.din("cn_w4", [256, 1])
    cb4_ap = b.din("cn_b4", [1, 1])
    o_final = b.dout("o_final", [1024, T])
    o_dbg4 = b.dout("o_dbg4", [1, 8])

    with b.es:
        b.init_consts([EPS, 1.0 / D, -0.5, -1.0])
        c0_t = b.sb("c0t", [1, 4])
        c1_t = b.sb("c1t", [1, 4])
        phase_gf(b, parts_ap, bsel_ap,
                 (gw1_ap, gb1_ap, gw2_ap, gb2_ap, gw3_ap, gb3_ap), ar_ap,
                 c0_t, c1_t, nbtok, tag="gf")
        resid = [b.sb(f"res{g}", [128, T]) for g in range(8)]
        outA = [b.sb(f"oA{g}", [128, T]) for g in range(8)]
        outbf = [b.sb(f"obf{g}", [128, T], BF16) for g in range(8)]
        phase_load_dmaj(b, r5_ap, resid, T, tag="ldr")
        phase_layernorm(b, resid, T, out_f32=outA, tag="ln6")
        phase_coupling(b, outA, outbf, resid, T,
                       (cw1_ap, cb1_ap, cw2_ap, cb2_ap, cw3_ap, cb3_ap,
                        cw4_ap, cb4_ap), c0_t, c1_t, tag="cp", CW=256)
        phase_store_dmaj(b, resid, o_final, T, tag="stf")
        sdbg = b.sem("d")
        with b.nc.Block() as blk:
            @blk.vector
            def _(v):
                v.tensor_copy(c0_t[0:1, 0:4], c0_t[0:1, 0:4]) if False else None
            @blk.sync
            def _(s):
                s.dma_start(out=o_dbg4[0:1, 0:4], in_=c0_t[0:1, 0:4]) \
                    .then_inc(sdbg.sem, 16)
                s.dma_start(out=o_dbg4[0:1, 4:8], in_=c1_t[0:1, 0:4]) \
                    .then_inc(sdbg.sem, 16)
                s.wait_ge(sdbg.sem, sdbg.n + 32)
        sdbg.n += 32
    return b


def host_prep_l1(inp):
    w = {}
    f32 = np.float32
    w["gc_w1"] = np.ascontiguousarray(inp["L1_ffn_gc_w1"], dtype=f32)
    w["gc_b1_dm"] = np.ascontiguousarray(
        np.asarray(inp["L1_ffn_gc_b1"], dtype=f32).reshape(4, 128).T)
    w["gc_w2"] = np.ascontiguousarray(inp["L1_ffn_gc_w2"], dtype=f32)
    w["gc_b2_dm"] = np.ascontiguousarray(
        np.asarray(inp["L1_ffn_gc_b2"], dtype=f32).reshape(2, 128).T)
    w["gc_w3"] = np.ascontiguousarray(inp["L1_ffn_gc_w3"], dtype=f32)
    w["gc_b3"] = np.asarray(inp["L1_ffn_gc_b3"], dtype=f32).reshape(1, 1)
    w["ar"] = np.asarray(inp["L1_ffn_adaptive_range"], dtype=f32).reshape(1, 1)
    w["cn_w1"] = np.ascontiguousarray(inp["L1_ffn_cn_w1"], dtype=f32)
    w["cn_b1_dm"] = np.ascontiguousarray(
        np.asarray(inp["L1_ffn_cn_b1"], dtype=f32).reshape(8, 128).T)
    w["cn_w2"] = np.ascontiguousarray(inp["L1_ffn_cn_w2"], dtype=f32)
    w["cn_b2_dm"] = np.ascontiguousarray(
        np.asarray(inp["L1_ffn_cn_b2"], dtype=f32).reshape(4, 128).T)
    w["cn_w3"] = np.ascontiguousarray(inp["L1_ffn_cn_w3"], dtype=f32)
    w["cn_b3_dm"] = np.ascontiguousarray(
        np.asarray(inp["L1_ffn_cn_b3"], dtype=f32).reshape(2, 128).T)
    w["cn_w4"] = np.ascontiguousarray(inp["L1_ffn_cn_w4"], dtype=f32)
    w["cn_b4"] = np.asarray(inp["L1_ffn_cn_b4"], dtype=f32).reshape(1, 1)
    w["xp_mu_dm"] = np.ascontiguousarray(
        np.asarray(inp["L1_attn_xp_mu"], dtype=f32).reshape(8, 128).T)
    w["xp_w1T"] = np.ascontiguousarray(np.asarray(inp["L1_attn_xp_W1"]).T,
                                       dtype=f32)
    w["xp_w2r"] = np.ascontiguousarray(
        np.asarray(inp["L1_attn_xp_W2"], dtype=f32).reshape(1024, 5, 32)
        .transpose(1, 2, 0))
    w["x_bias_dm"] = np.ascontiguousarray(
        np.asarray(inp["L1_attn_x_bias"], dtype=f32).reshape(5, 8, 128)
        .transpose(0, 2, 1))
    w["rwT"] = np.ascontiguousarray(np.asarray(inp["L1_attn_r_W"]).T, dtype=f32)
    w["kwT"] = np.ascontiguousarray(np.asarray(inp["L1_attn_k_W"]).T, dtype=f32)
    w["vwT"] = np.ascontiguousarray(np.asarray(inp["L1_attn_v_W"]).T, dtype=f32)
    w["waT"] = np.ascontiguousarray(np.asarray(inp["L1_attn_w_A"]).T, dtype=f32)
    w["wbT"] = np.ascontiguousarray(np.asarray(inp["L1_attn_w_B"]).T, dtype=f32)
    w["w_b_dm"] = np.ascontiguousarray(
        np.asarray(inp["L1_attn_w_b"], dtype=f32).reshape(4, 128).T)
    w["gaT"] = np.ascontiguousarray(np.asarray(inp["L1_attn_g_A"]).T, dtype=f32)
    w["gbT"] = np.ascontiguousarray(np.asarray(inp["L1_attn_g_B"]).T, dtype=f32)
    w["g_b_row"] = np.asarray(inp["L1_attn_g_b"], dtype=f32).reshape(1, 1024)
    w["bonus_dm"] = np.ascontiguousarray(np.asarray(inp["L1_attn_bonus"]).T,
                                         dtype=f32)
    w["gn_w_row"] = np.asarray(inp["L1_attn_gn_w"], dtype=f32).reshape(1, 1024)
    w["gn_b_row"] = np.asarray(inp["L1_attn_gn_b"], dtype=f32).reshape(1, 1024)
    w["owT"] = np.ascontiguousarray(np.asarray(inp["L1_attn_o_W"]).T, dtype=f32)
    m = np.zeros((128, 128), dtype=f32)
    for sc_ in range(2):
        for t in range(64):
            for tau in range(64):
                if tau < t:
                    m[64 * sc_ + t, 64 * sc_ + tau] = 1.0
    w["m_low"] = np.ascontiguousarray(m.T)
    w["m_low_t"] = np.ascontiguousarray(m)
    return w



class Prog:
    def __init__(self, b, tag):
        self.b = b
        self.items = []
        self.tick = b.sem(tag + "_tick")

    def op(self, eng, fn, inc=1):
        self.items.append((eng, fn, inc))

    def v(self, fn):
        self.op("vector", fn)

    def s(self, fn):
        self.op("scalar", fn)

    def t(self, fn):
        self.op("tensor", fn)

    def d(self, fn):
        self.op("sync", fn, inc=16)

    def g(self, fn):
        self.op("gpsimd", fn, inc=16)

    def emit(self):
        nc = self.b.nc
        base = self.tick.n
        targets = []
        c = base
        for (_, _, inc) in self.items:
            targets.append(c)
            c += inc
        total = c
        items = self.items
        tick = self.tick

        with nc.Block() as blk:
            def body(eng_name):
                def run(eng):
                    for k, (en, fn, inc) in enumerate(items):
                        if en != eng_name:
                            continue
                        if targets[k] > base:
                            eng.wait_ge(tick.sem, targets[k])
                        inst = fn(eng)
                        inst.then_inc(tick.sem, inc)
                    if eng_name == "sync":
                        eng.wait_ge(tick.sem, total)
                return run

            blk.vector(body("vector"))
            blk.scalar(body("scalar"))
            blk.tensor(body("tensor"))
            blk.sync(body("sync"))
            blk.gpsimd(body("gpsimd"))
        self.tick.n = total



def phase_rwkv_front(b, x4, T, Tc, w_aps, hm_ap, out_hbm, tag="rk"):
    nc = b.nc
    sc = contextlib.ExitStack()
    _sb = lambda n, sh, dt=F32: b.sb(tag + n, sh, dt, scope=sc)
    _ps = lambda n, sh: b.ps(tag + n, sh, scope=sc)
    P = Prog(b, tag)
    TQ = fsplit(T, 512)
    NTJ = T // 128
    (r_hbm, k_hbm, w_hbm, v_hbm, g_hbm) = out_hbm

    mixbf = [_sb(f"mx{g}", [128, T], BF16) for g in range(8)]
    mu_t = _sb("mu", [128, 8])
    xb_t = _sb("xb", [128, 40])
    hm = _sb("hm", [128, 1])
    xh = _sb("xh", [128, 1])
    xw1b = [_sb(f"xw1{dc}", [128, 160], BF16) for dc in range(8)]
    xw2b = _sb("xw2", [64, 1024], BF16)
    xlA = _sb("xlA", [128, T], BF16)
    xlB = _sb("xlB", [32, T], BF16)
    xlC = _sb("xlC", [64, T], BF16)
    x5t = _sb("x5t", [128, 512])
    tmp5a = _sb("tm5a", [128, 512])
    tmp5 = _sb("tmp5", [128, 512])
    pwb = [_sb(f"pw{dc}", [128, 1024], BF16) for dc in range(8)]
    wbb = _sb("wbb", [64, 512], BF16)
    w_bt = _sb("w_bt", [128, 4])
    rkstg = [_sb(f"rk{i}", [128, T], BF16) for i in range(2)]
    wstg = [_sb(f"ws{i}", [128, T]) for i in range(2)]
    vstg = [_sb(f"vs{i}", [128, 1024], BF16) for i in range(2)]
    wgstg = _sb("wg", [64, T], BF16)
    pp1 = _ps("pp1", [128, 512])
    pp2 = _ps("pp2", [64, 512])

    (xw1_ap, xw2_ap, xbias_ap, mu_ap, rw_ap, kw_ap, vw_ap, wa_ap, wb_ap,
     wbias_ap, ga_ap, gb_ap, gbrow_ap, bonus_ap, gnw_ap, gnb_ap, ow_ap,
     mlow_ap, ident) = w_aps

    for dc in range(8):
        P.g(lambda g, dc=dc: g.dma_start(out=xw1b[dc][:],
                                         in_=xw1_ap[128 * dc:128 * (dc + 1), :]))
    for i in range(5):
        P.d(lambda s, i=i: s.dma_start(out=xb_t[:, 8 * i:8 * (i + 1)],
                                       in_=xbias_ap[i]))
    P.d(lambda s: s.dma_start(out=mu_t[:], in_=mu_ap))
    P.d(lambda s: s.dma_start(out=w_bt[:], in_=wbias_ap))
    P.d(lambda s: s.dma_start(out=hm[:], in_=hm_ap))

    def delta_chunk(g_, off, sz):
        P.v(lambda v, g_=g_, off=off, sz=sz: v.tensor_sub(
            tmp5a[:, 0:sz], x4[g_][:, off:off + sz],
            x4[g_][:, 1 + off:1 + off + sz]))
        if off == 0:
            P.v(lambda v, g_=g_: v.tensor_scalar(
                out=xh[:], in0=x4[g_][:, 0:1], scalar1=hm[:, 0:1],
                scalar2=None, op0=ALU.mult))
            P.v(lambda v, g_=g_: v.tensor_sub(tmp5a[:, 0:1], xh[:],
                                              x4[g_][:, 1:2]))

    for g_ in range(8):
        for (off, sz) in TQ:
            delta_chunk(g_, off, sz)
            P.v(lambda v, g_=g_, off=off, sz=sz: v.scalar_tensor_tensor(
                out=mixbf[g_][:, off:off + sz], in0=tmp5a[:, 0:sz],
                scalar=mu_t[:, g_:g_ + 1], in1=x4[g_][:, 1 + off:1 + off + sz],
                op0=ALU.mult, op1=ALU.add))

    for (off, sz) in TQ:
        def mmA(t, off=off, sz=sz):
            for dc in range(8):
                inst = t.matmul(pp1[:, 0:sz], xw1b[dc][:, 0:128],
                                mixbf[dc][:, off:off + sz], start=(dc == 0),
                                stop=(dc == 7))
            return inst
        P.t(mmA)
        P.s(lambda s, off=off, sz=sz: s.activation(
            out=xlA[:, off:off + sz], in_=pp1[:, 0:sz], func=AF.Tanh))
        def mmB(t, off=off, sz=sz):
            for dc in range(8):
                inst = t.matmul(pp2[0:32, 0:sz], xw1b[dc][:, 128:160],
                                mixbf[dc][:, off:off + sz], start=(dc == 0),
                                stop=(dc == 7))
            return inst
        P.t(mmB)
        P.s(lambda s, off=off, sz=sz: s.activation(
            out=xlB[:, off:off + sz], in_=pp2[0:32, 0:sz], func=AF.Tanh))
    P.v(lambda v: v.tensor_copy(xlC[:, 0:T], xlA[64:128, 0:T]))

    def xl_slice(i):
        return [xlA[0:32, :], xlA[32:64, :], xlC[0:32, :], xlC[32:64, :],
                xlB[0:32, :]][i]

    def build_mix(i):
        P.g(lambda g, i=i: g.dma_start(out=xw2b[0:32, :], in_=xw2_ap[i]))
        P.g(lambda g, i=i: g.dma_start(out=xw2b[32:64, :], in_=xw2_ap[i]))
        wrow = slice(32, 64) if i in (1, 3) else slice(0, 32)
        for hc in range(8):
            for (off, sz) in TQ:
                P.t(lambda t, i=i, hc=hc, off=off, sz=sz, wrow=wrow: t.matmul(
                    pp1[:, 0:sz], xw2b[wrow, 128 * hc:128 * (hc + 1)],
                    xl_slice(i)[:, off:off + sz], start=True, stop=True))
                P.v(lambda v, i=i, hc=hc, off=off, sz=sz: v.tensor_scalar(
                    out=x5t[:, 0:sz], in0=pp1[:, 0:sz],
                    scalar1=xb_t[:, 8 * i + hc:8 * i + hc + 1],
                    scalar2=None, op0=ALU.add))
                delta_chunk(hc, off, sz)
                P.v(lambda v, off=off, sz=sz: v.tensor_mul(
                    tmp5[:, 0:sz], tmp5a[:, 0:sz], x5t[:, 0:sz]))
                P.v(lambda v, hc=hc, off=off, sz=sz: v.tensor_add(
                    mixbf[hc][:, off:off + sz], tmp5[:, 0:sz],
                    x4[hc][:, 1 + off:1 + off + sz]))

    def load_proj(w_ap, width):
        for dc in range(8):
            P.g(lambda g, dc=dc, w_ap=w_ap, width=width: g.dma_start(
                out=pwb[dc][:, 0:width],
                in_=w_ap[128 * dc:128 * (dc + 1), 0:width]))

    def proj_kmaj(dst_hbm, width):
        for kc in range(4):
            stg = rkstg[kc % 2]
            for (off, sz) in TQ:
                def mm(t, kc=kc, off=off, sz=sz):
                    for dc in range(8):
                        inst = t.matmul(pp1[:, 0:sz],
                                        pwb[dc][:, 128 * kc:128 * (kc + 1)],
                                        mixbf[dc][:, off:off + sz],
                                        start=(dc == 0), stop=(dc == 7))
                    return inst
                P.t(mm)
                P.s(lambda s, stg=stg, off=off, sz=sz: s.activation(
                    out=stg[:, off:off + sz], in_=pp1[:, 0:sz], func=AF.Copy))
            P.d(lambda s, kc=kc, stg=stg, dst_hbm=dst_hbm: s.dma_start(
                out=dst_hbm[128 * kc:128 * (kc + 1), :], in_=stg[:]))

    build_mix(0)
    load_proj(rw_ap, 512)
    proj_kmaj(r_hbm, 512)
    build_mix(1)
    load_proj(wa_ap, 64)
    for (off, sz) in TQ:
        def mmw(t, off=off, sz=sz):
            for dc in range(8):
                inst = t.matmul(pp2[:, 0:sz], pwb[dc][:, 0:64],
                                mixbf[dc][:, off:off + sz], start=(dc == 0),
                                stop=(dc == 7))
            return inst
        P.t(mmw)
        P.s(lambda s, off=off, sz=sz: s.activation(
            out=wgstg[:, off:off + sz], in_=pp2[:, 0:sz], func=AF.Tanh))
    P.g(lambda g: g.dma_start(out=wbb[:], in_=wb_ap))
    for kc in range(4):
        stg = wstg[kc % 2]
        for (off, sz) in TQ:
            P.t(lambda t, kc=kc, off=off, sz=sz: t.matmul(
                pp1[:, 0:sz], wbb[:, 128 * kc:128 * (kc + 1)],
                wgstg[:, off:off + sz], start=True, stop=True))
            P.v(lambda v, kc=kc, stg=stg, off=off, sz=sz: v.tensor_scalar(
                out=stg[:, off:off + sz], in0=pp1[:, 0:sz],
                scalar1=w_bt[:, kc:kc + 1], scalar2=None, op0=ALU.add))
        P.d(lambda s, kc=kc, stg=stg: s.dma_start(
            out=w_hbm[128 * kc:128 * (kc + 1), :], in_=stg[:]))
    build_mix(2)
    load_proj(kw_ap, 512)
    proj_kmaj(k_hbm, 512)
    build_mix(3)
    load_proj(vw_ap, 1024)
    for j in range(NTJ):
        stg = vstg[j % 2]
        for hf in range(2):
            def mmv(t, j=j, hf=hf):
                for dc in range(8):
                    inst = t.matmul(pp1[:, 0:512],
                                    mixbf[dc][:, 128 * j:128 * (j + 1)],
                                    pwb[dc][:, 512 * hf:512 * (hf + 1)],
                                    start=(dc == 0), stop=(dc == 7))
                return inst
            P.t(mmv)
            P.s(lambda s, stg=stg, hf=hf: s.activation(
                out=stg[:, 512 * hf:512 * (hf + 1)], in_=pp1[:, 0:512],
                func=AF.Copy))
        P.d(lambda s, j=j, stg=stg: s.dma_start(
            out=v_hbm[128 * j:128 * (j + 1), :], in_=stg[:]))
    build_mix(4)
    load_proj(ga_ap, 64)
    for (off, sz) in TQ:
        def mmg(t, off=off, sz=sz):
            for dc in range(8):
                inst = t.matmul(pp2[:, 0:sz], pwb[dc][:, 0:64],
                                mixbf[dc][:, off:off + sz], start=(dc == 0),
                                stop=(dc == 7))
            return inst
        P.t(mmg)
        P.s(lambda s, off=off, sz=sz: s.activation(
            out=wgstg[:, off:off + sz], in_=pp2[:, 0:sz], func=AF.Tanh))
    P.d(lambda s: s.dma_start(out=g_hbm, in_=wgstg[:]))
    P.emit()
    sc.close()


def phase_rwkv_core(b, resid, T, hbm, w_aps, ident, tag="rc", dbg_ap=None):
    nc = b.nc
    sc = contextlib.ExitStack()
    _sb = lambda n, sh, dt=F32: b.sb(tag + n, sh, dt, scope=sc)
    _ps = lambda n, sh: b.ps(tag + n, sh, scope=sc)
    P = Prog(b, tag)
    NTJ = T // 128
    (r_hbm, k_hbm, w_hbm, v_hbm, g_hbm) = hbm
    (xw1_ap, xw2_ap, xbias_ap, mu_ap, rw_ap, kw_ap, vw_ap, wa_ap, wb_ap,
     wbias_ap, ga_ap, gb_ap, gbrow_ap, bonus_ap, gnw_ap, gnb_ap, ow_ap,
     mlow_ap, _id) = w_aps
    rj = [_sb(f"rj{h}", [128, 128], BF16) for h in range(4)]
    kj = [_sb(f"kj{h}", [128, 128], BF16) for h in range(4)]
    wj = [_sb(f"wj{h}", [128, 128]) for h in range(4)]
    vj = _sb("vj", [128, 1024], BF16)
    gj = _sb("gj", [64, 128], BF16)

    gbb = _sb("gbb", [64, 1024], BF16)
    owb = [_sb(f"ow{vc}", [128, 1024], BF16) for vc in range(8)]
    mlow = _sb("mlow", [128, 128])
    idbf = _sb("idbf", [128, 128], BF16)
    bon_t = _sb("bon", [128, 4])
    gbrow = _sb("gbrow", [1, 1024])
    gnwrow = _sb("gnwrow", [1, 1024])
    gnbrow = _sb("gnbrow", [1, 1024])
    gwt = _sb("gwt", [128, 1024])
    gbt = _sb("gbt", [128, 1024])
    gvt = _sb("gvt", [128, 1024])
    onesr = _sb("onesr", [1, 128])
    S_t = [_sb(f"S{h}", [128, 256]) for h in range(4)]
    pex = _sb("pex", [128, 128])
    cumA = _sb("cumA", [128, 128])
    cumB = _sb("cumB", [128, 128])
    wex = _sb("wex", [128, 128])
    lqb = _sb("lqb", [128, 128], BF16)
    qt = _sb("qt", [128, 128], BF16)
    negw = _sb("negw", [128, 128])
    kpb = _sb("kpb", [128, 128], BF16)
    ktl = _sb("ktl", [128, 128], BF16)
    argd = _sb("argd", [128, 128])
    ldb = _sb("ldb", [128, 128], BF16)
    kdec = _sb("kdec", [128, 128], BF16)
    lLc = _sb("lLc", [128, 2])
    epst = _sb("epst", [128, 1])
    mhalf = _sb("mhalf", [128, 1])
    mone = _sb("mone", [128, 1])
    rub = _sb("rub", [128, 128], BF16)
    pmb = _sb("pmb", [128, 128], BF16)
    tmpb = _sb("tmpb", [128, 128], BF16)
    ptb = _sb("ptb", [128, 128], BF16)
    kdTf = _sb("kdTf", [128, 128], BF16)
    smid = _sb("smid", [128, 256])
    sbf = _sb("sbf", [128, 256], BF16)
    smbf = _sb("smbf", [128, 256], BF16)
    gsil = _sb("gsil", [128, 1024])
    og = _sb("og", [128, 1024])
    sqt = _sb("sqt", [128, 1024])
    gnr = _sb("gnr", [128, 8])
    gatb = _sb("gatb", [128, 1024], BF16)
    gatT = _sb("gatT", [128, 8 * 128], BF16)
    pP = _ps("pP", [128, 128])
    pT = _ps("pT", [128, 128])
    po = _ps("po", [128, 1024])
    pA = _ps("pA", [128, 512])
    pG = _ps("pG", [128, 512])

    P.g(lambda g: g.dma_start(out=gbb[:], in_=gb_ap))
    for vc in range(8):
        P.g(lambda g, vc=vc: g.dma_start(
            out=owb[vc][:], in_=ow_ap[128 * vc:128 * (vc + 1), :]))
    P.d(lambda s: s.dma_start(out=mlow[:], in_=mlow_ap))
    P.d(lambda s: s.dma_start(out=bon_t[:], in_=bonus_ap))
    P.d(lambda s: s.dma_start(out=gbrow[:], in_=gbrow_ap))
    P.d(lambda s: s.dma_start(out=gnwrow[:], in_=gnw_ap))
    P.d(lambda s: s.dma_start(out=gnbrow[:], in_=gnb_ap))
    P.v(lambda v: v.tensor_copy(idbf[:], ident[:]))
    P.v(lambda v: v.memset(onesr[:], 1.0))
    P.v(lambda v: v.memset(epst[:], EPS))
    P.v(lambda v: v.memset(mhalf[:], -0.5))
    P.v(lambda v: v.memset(mone[:], -1.0))
    for h in range(4):
        P.v(lambda v, h=h: v.memset(S_t[h][:], 0.0))
    for (row, dstt) in ((gbrow, gbt), (gnwrow, gwt), (gnbrow, gvt)):
        for c4 in range(2):
            P.t(lambda t, row=row, c4=c4: t.matmul(
                pG[:, 0:512], onesr[:], row[0:1, 512 * c4:512 * (c4 + 1)],
                start=True, stop=True))
            P.s(lambda s, dstt=dstt, c4=c4: s.activation(
                out=dstt[:, 512 * c4:512 * (c4 + 1)], in_=pG[:, 0:512],
                func=AF.Copy))

    for j in range(NTJ):
        jc = slice(128 * j, 128 * (j + 1))
        for h in range(4):
            P.d(lambda s, h=h, jc=jc: s.dma_start(
                out=rj[h][:], in_=r_hbm[128 * h:128 * (h + 1), jc]))
            P.d(lambda s, h=h, jc=jc: s.dma_start(
                out=kj[h][:], in_=k_hbm[128 * h:128 * (h + 1), jc]))
            P.d(lambda s, h=h, jc=jc: s.dma_start(
                out=wj[h][:], in_=w_hbm[128 * h:128 * (h + 1), jc]))
        P.d(lambda s, j=j: s.dma_start(
            out=vj[:], in_=v_hbm[128 * j:128 * (j + 1), :]))
        P.d(lambda s, jc=jc: s.dma_start(out=gj[:], in_=g_hbm[:, jc]))
        for hf in range(2):
            P.t(lambda t, hf=hf: t.matmul(
                pG[:, 0:512], gj[:],
                gbb[:, 512 * hf:512 * (hf + 1)], start=True, stop=True))
            P.v(lambda v, hf=hf: v.tensor_add(
                gsil[:, 512 * hf:512 * (hf + 1)], pG[:, 0:512],
                gbt[:, 512 * hf:512 * (hf + 1)]))
            P.s(lambda s, hf=hf: s.activation(
                out=gsil[:, 512 * hf:512 * (hf + 1)],
                in_=gsil[:, 512 * hf:512 * (hf + 1)], func=AF.Silu))
        for h in range(4):
            P.s(lambda s, h=h: s.activation(
                out=pex[:], in_=wj[h][:], func=AF.Exp))
            P.v(lambda v: v.tensor_copy(cumA[:], pex[:]))
            for st_, (bsrc, bdst) in zip(
                    [1, 2, 4, 8, 16, 32],
                    [(cumA, cumB), (cumB, cumA), (cumA, cumB), (cumB, cumA),
                     (cumA, cumB), (cumB, cumA)]):
                P.v(lambda v, s_=st_, bs=bsrc, bd=bdst: v.tensor_copy(
                    bd[:].rearrange("p (c w) -> p c w", c=2)[:, :, 0:s_],
                    bs[:].rearrange("p (c w) -> p c w", c=2)[:, :, 0:s_]))
                P.v(lambda v, s_=st_, bs=bsrc, bd=bdst: v.tensor_add(
                    bd[:].rearrange("p (c w) -> p c w", c=2)[:, :, s_:],
                    bs[:].rearrange("p (c w) -> p c w", c=2)[:, :, s_:],
                    bs[:].rearrange("p (c w) -> p c w", c=2)[:, :, 0:64 - s_]))
            cum = cumA
            P.v(lambda v, cum=cum: v.tensor_sub(wex[:], pex[:], cum[:]))
            P.s(lambda s: s.activation(out=wex[:], in_=wex[:], func=AF.Exp))
            P.v(lambda v: v.tensor_copy(lqb[:], wex[:]))
            P.v(lambda v, h=h: v.tensor_mul(qt[:], rj[h][:], lqb[:]))
            P.v(lambda v, cum=cum: v.tensor_scalar(
                out=negw[:], in0=cum[:], scalar1=80.0, scalar2=None,
                op0=ALU.min))
            P.s(lambda s: s.activation(out=negw[:], in_=negw[:], func=AF.Exp))
            P.v(lambda v: v.tensor_copy(kpb[:], negw[:]))
            P.v(lambda v, h=h: v.tensor_mul(ktl[:], kj[h][:], kpb[:]))
            for s_ in range(2):
                P.v(lambda v, s_=s_, cum=cum: v.tensor_scalar(
                    out=argd[:, 64 * s_:64 * (s_ + 1)],
                    in0=cum[:, 64 * s_:64 * (s_ + 1)],
                    scalar1=cum[:, 64 * s_ + 63:64 * (s_ + 1)],
                    scalar2=None, op0=ALU.subtract))
                P.s(lambda s, s_=s_, cum=cum: s.activation(
                    out=lLc[:, s_:s_ + 1],
                    in_=cum[:, 64 * s_ + 63:64 * (s_ + 1)], func=AF.Exp,
                    scale=mone[:, 0:1]))
            P.s(lambda s: s.activation(out=argd[:], in_=argd[:], func=AF.Exp))
            P.v(lambda v: v.tensor_copy(ldb[:], argd[:]))
            P.v(lambda v, h=h: v.tensor_mul(kdec[:], kj[h][:], ldb[:]))
            P.t(lambda t: t.matmul(pP[:], qt[:], ktl[:], start=True,
                                   stop=True))
            P.v(lambda v: v.tensor_mul(pmb[:], pP[:], mlow[:]))
            P.v(lambda v, h=h: v.tensor_scalar_mul(
                rub[:], rj[h][:], bon_t[:, h:h + 1]))
            P.t(lambda t, h=h: t.matmul(pP[:], rub[:], kj[h][:],
                                        start=True, stop=True))
            P.v(lambda v: v.tensor_mul(tmpb[:], pP[:], ident[:]))
            P.v(lambda v: v.tensor_add(pmb[:], pmb[:], tmpb[:]))
            P.t(lambda t: t.matmul(pT[:], pmb[:], idbf[:], start=True,
                                   stop=True))
            P.v(lambda v: v.tensor_copy(ptb[:], pT[:]))
            for s_ in range(2):
                rows = slice(64 * s_, 64 * (s_ + 1))
                P.t(lambda t, rows=rows: t.matmul(
                    pT[rows, :], kdec[:, rows], idbf[:], start=True,
                    stop=True))
                P.v(lambda v, rows=rows: v.tensor_copy(kdTf[rows, :],
                                                       pT[rows, :]))
                P.t(lambda t, s_=s_, h=h, rows=rows: t.matmul(
                    pA[:, 256 * s_:256 * (s_ + 1)], kdTf[rows, :],
                    vj[rows, 256 * h:256 * (h + 1)], start=True,
                    stop=True))
            P.v(lambda v, h=h: v.tensor_scalar_mul(smid[:], S_t[h][:],
                                                   lLc[:, 0:1]))
            P.v(lambda v: v.tensor_add(smid[:], smid[:], pA[:, 0:256]))
            P.v(lambda v, h=h: v.tensor_copy(sbf[:], S_t[h][:]))
            P.v(lambda v: v.tensor_copy(smbf[:], smid[:]))
            P.t(lambda t, h=h: t.matmul(
                po[:, 256 * h:256 * (h + 1)], ptb[0:64, :],
                vj[0:64, 256 * h:256 * (h + 1)], start=True, stop=False))
            P.t(lambda t, h=h: t.matmul(
                po[0:64, 256 * h:256 * (h + 1)], qt[:, 0:64], sbf[:],
                start=False, stop=False))
            P.t(lambda t, h=h: t.matmul(
                po[64:128, 256 * h:256 * (h + 1)], qt[:, 64:128], smbf[:],
                start=False, stop=False))
            P.t(lambda t, h=h: t.matmul(
                po[:, 256 * h:256 * (h + 1)], ptb[64:128, :],
                vj[64:128, 256 * h:256 * (h + 1)], start=False,
                stop=True))
            P.v(lambda v, h=h: v.tensor_scalar_mul(S_t[h][:], smid[:],
                                                   lLc[:, 1:2]))
            P.v(lambda v, h=h: v.tensor_add(S_t[h][:], S_t[h][:],
                                            pA[:, 256:512]))
        P.s(lambda s: s.activation(out=og[:], in_=po[:], func=AF.Copy))
        if dbg_ap is not None:
            P.d(lambda s, j=j: s.dma_start(
                out=dbg_ap[0][:, 1024 * j:1024 * (j + 1)], in_=og[:]))
        P.v(lambda v: v.reduce_sum(gnr[:, 0:1], og[:], axis=AXX))
        P.v(lambda v: v.tensor_scalar_mul(gnr[:, 1:2], gnr[:, 0:1],
                                          1.0 / 1024.0))
        P.s(lambda s: s.activation(out=sqt[:], in_=og[:], func=AF.Square))
        P.v(lambda v: v.reduce_sum(gnr[:, 2:3], sqt[:], axis=AXX))
        P.v(lambda v: v.tensor_scalar_mul(gnr[:, 3:4], gnr[:, 2:3],
                                          1.0 / 1024.0))
        P.s(lambda s: s.activation(out=gnr[:, 4:5], in_=gnr[:, 1:2],
                                   func=AF.Square))
        P.v(lambda v: v.tensor_sub(gnr[:, 5:6], gnr[:, 3:4], gnr[:, 4:5]))
        P.s(lambda s: s.activation(out=gnr[:, 6:7], in_=gnr[:, 5:6],
                                   func=AF.Ln, bias=epst[:, 0:1]))
        P.s(lambda s: s.activation(out=gnr[:, 6:7], in_=gnr[:, 6:7],
                                   func=AF.Exp, scale=mhalf[:, 0:1]))
        P.v(lambda v: v.tensor_scalar(out=og[:], in0=og[:],
                                      scalar1=gnr[:, 1:2],
                                      scalar2=gnr[:, 6:7], op0=ALU.subtract,
                                      op1=ALU.mult))
        P.v(lambda v: v.tensor_mul(og[:], og[:], gwt[:]))
        P.v(lambda v: v.tensor_add(og[:], og[:], gvt[:]))
        P.v(lambda v: v.tensor_mul(gatb[:], og[:], gsil[:]))
        if dbg_ap is not None:
            P.d(lambda s, j=j: s.dma_start(
                out=dbg_ap[1][:, 1024 * j:1024 * (j + 1)], in_=gatb[:]))
            P.d(lambda s, j=j: s.dma_start(
                out=dbg_ap[2][:, 1024 * j:1024 * (j + 1)], in_=gsil[:]))
        for gc in range(8):
            P.t(lambda t, gc=gc: t.matmul(
                pT[:], gatb[:, 128 * gc:128 * (gc + 1)], idbf[:], start=True,
                stop=True))
            P.v(lambda v, gc=gc: v.tensor_copy(
                gatT[:, 128 * gc:128 * (gc + 1)], pT[:]))
        for doc in range(8):
            def mmo(t, j=j, doc=doc):
                for vc in range(8):
                    inst = t.matmul(pT[:], owb[vc][:, 128 * doc:
                                                   128 * (doc + 1)],
                                    gatT[:, 128 * vc:128 * (vc + 1)],
                                    start=(vc == 0), stop=(vc == 7))
                return inst
            P.t(mmo)
            P.v(lambda v, j=j, doc=doc: v.tensor_add(
                resid[doc][:, 1 + 128 * j:1 + 128 * (j + 1)],
                resid[doc][:, 1 + 128 * j:1 + 128 * (j + 1)], pT[:]))
    P.emit()
    sc.close()



def build_launch3(T, nbtok=8192.0, dbg=False):
    b = B()
    Tc = T + 1
    r3_ap = b.din("resid3h", [1024, Tc])
    parts_ap = b.din("parts", [8, 128, 8])
    bsel_ap = b.din("bsel", [128, 8])
    gw1_ap = b.din("gc_w1", [1024, 512])
    gb1_ap = b.din("gc_b1_dm", [128, 4])
    gw2_ap = b.din("gc_w2", [512, 256])
    gb2_ap = b.din("gc_b2_dm", [128, 2])
    gw3_ap = b.din("gc_w3", [256, 1])
    gb3_ap = b.din("gc_b3", [1, 1])
    ar_ap = b.din("ar", [1, 1])
    cw1_ap = b.din("cn_w1", [1024, 1024])
    cb1_ap = b.din("cn_b1_dm", [128, 8])
    cw2_ap = b.din("cn_w2", [1024, 512])
    cb2_ap = b.din("cn_b2_dm", [128, 4])
    cw3_ap = b.din("cn_w3", [512, 256])
    cb3_ap = b.din("cn_b3_dm", [128, 2])
    cw4_ap = b.din("cn_w4", [256, 1])
    cb4_ap = b.din("cn_b4", [1, 1])
    hm_ap = b.din("hm", [128, 1])
    xw1_ap = b.din("xp_w1T", [1024, 160])
    xw2_ap = b.din("xp_w2r", [5, 32, 1024])
    xbias_ap = b.din("x_bias_dm", [5, 128, 8])
    mu_ap = b.din("xp_mu_dm", [128, 8])
    rw_ap = b.din("rwT", [1024, 512])
    kw_ap = b.din("kwT", [1024, 512])
    vw_ap = b.din("vwT", [1024, 1024])
    wa_ap = b.din("waT", [1024, 64])
    wb_ap = b.din("wbT", [64, 512])
    wbias_ap = b.din("w_b_dm", [128, 4])
    ga_ap = b.din("gaT", [1024, 64])
    gb_ap = b.din("gbT", [64, 1024])
    gbrow_ap = b.din("g_b_row", [1, 1024])
    bonus_ap = b.din("bonus_dm", [128, 4])
    gnw_ap = b.din("gn_w_row", [1, 1024])
    gnb_ap = b.din("gn_b_row", [1, 1024])
    ow_ap = b.din("owT", [1024, 1024])
    mlow_ap = b.din("m_low_t", [128, 128])
    id_ap = b.din("ident", [128, 128])
    o_resid5 = b.dout("o_resid5", [1024, T])
    o_part = b.dout("o_part", [128, 8])
    r4_hbm = b.nc.dram_tensor("r4_hbm", [1024, Tc], F32, kind="Internal").ap()
    x4_hbm = b.nc.dram_tensor("x4_hbm", [1024, Tc], F32, kind="Internal").ap()
    kind5 = "ExternalOutput" if dbg else "Internal"
    rH = b.nc.dram_tensor("rH", [512, T], BF16, kind=kind5).ap()
    kH = b.nc.dram_tensor("kH", [512, T], BF16, kind=kind5).ap()
    wH = b.nc.dram_tensor("wH", [512, T], F32, kind=kind5).ap()
    vH = b.nc.dram_tensor("vH", [T, 1024], BF16, kind=kind5).ap()
    gH = b.nc.dram_tensor("gH", [64, T], BF16, kind=kind5).ap()
    x4H = b.nc.dram_tensor("x4H", [1024, T + 1], F32,
                           kind=kind5) if dbg else None
    hbm5 = (rH, kH, wH, vH, gH)

    with b.es:
        b.init_consts([EPS, 1.0 / D, -0.5, -1.0])
        ident = b.sb("ident_sb", [128, 128])
        phase_load_dmaj(b, id_ap, [ident], 128, tag="ldi")
        c0_t = b.sb("c0t", [1, 4])
        c1_t = b.sb("c1t", [1, 4])
        phase_gf(b, parts_ap, bsel_ap,
                 (gw1_ap, gb1_ap, gw2_ap, gb2_ap, gw3_ap, gb3_ap), ar_ap,
                 c0_t, c1_t, nbtok, tag="gf")
        w_aps = (xw1_ap, xw2_ap, xbias_ap, mu_ap, rw_ap, kw_ap, vw_ap,
                 wa_ap, wb_ap, wbias_ap, ga_ap, gb_ap, gbrow_ap, bonus_ap,
                 gnw_ap, gnb_ap, ow_ap, mlow_ap, ident)
        with contextlib.ExitStack() as scA:
            resid = [b.sb(f"res{g}", [128, Tc], scope=scA) for g in range(8)]
            phase_load_dmaj(b, r3_ap, resid, Tc, tag="ldr")
            with contextlib.ExitStack() as scB:
                outA = [b.sb(f"oA{g}", [128, Tc], scope=scB)
                        for g in range(8)]
                outbf = [b.sb(f"obf{g}", [128, Tc], BF16, scope=scB)
                         for g in range(8)]
                phase_layernorm(b, resid, Tc, out_f32=outA, tag="ln5")
                phase_coupling(b, outA, outbf, resid, Tc,
                               (cw1_ap, cb1_ap, cw2_ap, cb2_ap, cw3_ap,
                                cb3_ap, cw4_ap, cb4_ap), c0_t, c1_t,
                               tag="cp", CW=256)
            with contextlib.ExitStack() as scC:
                x4t = [b.sb(f"x4{g}", [128, Tc], scope=scC) for g in range(8)]
                phase_layernorm(b, resid, Tc, out_f32=x4t, tag="lnx4")
                phase_store_dmaj(b, x4t, x4_hbm, Tc, tag="stx4")
                if dbg:
                    phase_store_dmaj(b, x4t, x4H.ap(), Tc, tag="stx4d")
            phase_store_dmaj(b, resid, r4_hbm, Tc, tag="str4")
        with contextlib.ExitStack() as scF:
            x4 = [b.sb(f"xf{g}", [128, Tc], scope=scF) for g in range(8)]
            phase_load_dmaj(b, x4_hbm, x4, Tc, tag="ldx4")
            phase_rwkv_front(b, x4, T, Tc, w_aps, hm_ap, hbm5, tag="rk")
        with contextlib.ExitStack() as scD:
            resid2 = [b.sb(f"rs{g}", [128, Tc], scope=scD) for g in range(8)]
            phase_load_dmaj(b, r4_hbm, resid2, Tc, tag="ldr4")
            dbgc = None
            if dbg:
                dbgc = (b.dout("o_dbgc", [128, 1024 * (T // 128)]),
                        b.dout("o_dbg2", [128, 1024 * (T // 128)], BF16),
                        b.dout("o_dbg3", [128, 1024 * (T // 128)]))
            phase_rwkv_core(b, resid2, T, hbm5, w_aps, ident, tag="rc",
                            dbg_ap=dbgc)
            with contextlib.ExitStack() as scE:
                xlf = [b.sb(f"xl{g}", [128, T], scope=scE) for g in range(8)]
                phase_layernorm(b, [resid2[g][0:128, 1:Tc] for g in range(8)],
                                T, out_f32=xlf, tag="ln6")
                phase_partials(b, xlf, T, o_part, tag="pp")
            phase_store_dmaj(b, resid2, o_resid5, T, tag="str5", col0=1)
    return b



L0_KEYS = ("gate_w", "w1", "w2", "ident", "win", "conv_diag", "bconv_dm",
           "wx", "wdt", "bdt_dm", "a_dm")
L1_GF_KEYS = ("gc_w1", "gc_b1_dm", "gc_w2", "gc_b2_dm", "gc_w3", "gc_b3",
              "ar")
L1_CN_KEYS = ("cn_w1", "cn_b1_dm", "cn_w2", "cn_b2_dm", "cn_w3", "cn_b3_dm",
              "cn_w4", "cn_b4")
L1_RK_KEYS = ("xp_w1T", "xp_w2r", "x_bias_dm", "xp_mu_dm", "rwT", "kwT",
              "vwT", "waT", "wbT", "w_b_dm", "gaT", "gbT", "g_b_row",
              "bonus_dm", "gn_w_row", "gn_b_row", "owT", "m_low_t")


def kernel(**inputs):
    from concourse.bass_utils import run_bass_kernel_spmd
    T = 2048
    cores = list(range(8))
    x = np.asarray(inputs["L0_attn_x"], np.float32)
    w0 = host_prep_l0(inputs, T)
    w1w = host_prep_l1(inputs)

    b1 = build_launch1(T)
    feeds1 = []
    for c in cores:
        bi, qi = c // 4, c % 4
        xT = x[bi].T
        xh = np.zeros((1024, T + HALO), np.float32)
        xh[:, HALO:] = xT[:, T * qi:T * (qi + 1)]
        if qi > 0:
            xh[:, 0:HALO] = xT[:, T * qi - HALO:T * qi]
        feeds1.append(dict(x_dmaj=np.ascontiguousarray(xh),
                           cconsts=b1.host_consts(),
                           **{k: w0[k] for k in L0_KEYS}))
    r1 = run_bass_kernel_spmd(b1.nc, feeds1, core_ids=cores).results

    b2 = build_launch2(T)
    feeds2 = []
    for c in cores:
        bi, qi = c // 4, c % 4
        ch_dsum = np.stack([np.asarray(r1[4 * bi + k]["o_dsum"], np.float32)
                            for k in range(4)])
        ch_state = np.stack([np.asarray(r1[4 * bi + k]["o_state"], np.float32)
                             for k in range(4)])
        msel = np.zeros((128, 4), np.float32)
        if qi > 0:
            msel[:, qi - 1] = 1.0
        feeds2.append(dict(
            resid1=r1[c]["o_resid1"], d_u=r1[c]["o_u"],
            d_delta=r1[c]["o_delta"], d_zsilu=r1[c]["o_zsilu"],
            d_bc=r1[c]["o_bc"], a_dm=w0["a_dm"], dp_dm=w0["dp_dm"],
            ch_dsum=ch_dsum, ch_state=ch_state, msel=msel, wout=w0["wout"],
            gate_w=w0["gate_w"], ident=w0["ident"], w1=w0["w1"],
            w2=w0["w2"], cconsts=b2.host_consts()))
    r2 = run_bass_kernel_spmd(b2.nc, feeds2, core_ids=cores).results

    b3 = build_launch3(T)
    parts3 = np.stack([np.asarray(r2[j]["o_part"], np.float32)
                       for j in range(8)])
    feeds3 = []
    for c in cores:
        bi, qi = c // 4, c % 4
        bsel = np.zeros((128, 8), np.float32)
        bsel[:, 4 * bi:4 * bi + 4] = 1.0
        r3h = np.zeros((1024, T + 1), np.float32)
        r3h[:, 1:] = np.asarray(r2[c]["o_resid3"], np.float32)
        hmv = 0.0
        if qi > 0:
            r3h[:, 0] = np.asarray(r2[c - 1]["o_resid3"],
                                   np.float32)[:, T - 1]
            hmv = 1.0
        feeds3.append(dict(
            resid3h=np.ascontiguousarray(r3h), parts=parts3, bsel=bsel,
            hm=np.full((128, 1), hmv, np.float32), ident=w0["ident"],
            cconsts=b3.host_consts(),
            **{k: w1w[k] for k in L1_GF_KEYS + L1_CN_KEYS + L1_RK_KEYS}))
    r3 = run_bass_kernel_spmd(b3.nc, feeds3, core_ids=cores).results

    b4 = build_launch4(T)
    parts4 = np.stack([np.asarray(r3[j]["o_part"], np.float32)
                       for j in range(8)])
    feeds4 = []
    for c in cores:
        bi = c // 4
        bsel = np.zeros((128, 8), np.float32)
        bsel[:, 4 * bi:4 * bi + 4] = 1.0
        feeds4.append(dict(
            resid5=r3[c]["o_resid5"], parts=parts4, bsel=bsel,
            cconsts=b4.host_consts(),
            **{k: w1w[k] for k in L1_GF_KEYS + L1_CN_KEYS}))
    r4 = run_bass_kernel_spmd(b4.nc, feeds4, core_ids=cores).results

    out = np.empty((2, 8192, 1024), np.float32)
    for c in cores:
        bi, qi = c // 4, c % 4
        out[bi, T * qi:T * (qi + 1), :] = \
            np.asarray(r4[c]["o_final"], np.float32).T
    return out
